# version 08
# baseline (speedup 1.0000x reference)
import numpy as np
import ml_dtypes
import concourse.bass as bass
import concourse.mybir as mybir
from concourse.bass_utils import run_bass_kernel_spmd
from contextlib import ExitStack

F32 = mybir.dt.float32
BF16 = mybir.dt.bfloat16
U8 = mybir.dt.uint8
AF = mybir.ActivationFunctionType
AX = mybir.AxisListType
ALU = mybir.AluOpType

NCORE = 8
D = 2048
TOK = 1024
NTH = 9
NT = 8
CH = 16
F_ = 8192
NFC = 64
SIG = float(1.0 / np.sqrt(128.0))
LN_EPS = 1e-5
RMS2_EPS = 1e-6
QK_EPS = 1e-8
SWA_FILL = -10000.0
MIN_FILL = -1.0e38

ENGS = ("pe", "dve", "act", "pool", "sync")


class Env:

    def __init__(self, dry, sems=None, milestones=None):
        self.dry = dry
        self.ctr = {}
        self.sems = sems or {}
        self.ms = {} if milestones is None else milestones

    def E(self, f, sem=None, k=1):
        if sem is not None:
            self.ctr[sem] = self.ctr.get(sem, 0) + k
        if self.dry:
            return None
        inst = f()
        if sem is not None:
            inst.then_inc(self.sems[sem], k)
        return inst

    def MS(self, name, sem):
        if self.dry:
            assert name not in self.ms, f"dup milestone {name}"
            self.ms[name] = (sem, self.ctr.get(sem, 0))

    def W(self, eng, name):
        if not self.dry:
            sem, v = self.ms[name]
            if v > 0:
                eng.wait_ge(self.sems[sem], v)

    def bar(self, eng, which, tag):
        if which in ("sync", "pe"):
            self.E(lambda: eng.nop(), f"bs_{which}", 1)
        else:
            self.E(lambda: eng.drain(), f"bs_{which}", 1)
        self.MS(f"bar_{tag}_{which}", f"bs_{which}")
        for o in ENGS:
            if o != which:
                self.W(eng, f"bar_{tag}_{o}")


class FakeBlock:
    def _r(self, f):
        f(None)
    def sync(self, f): self._r(f)
    def vector(self, f): self._r(f)
    def scalar(self, f): self._r(f)
    def tensor(self, f): self._r(f)
    def gpsimd(self, f): self._r(f)


F32_TOT = 19688
BF_TOT = 61500
U8_TOT = 4608

FO = dict(
    ident=0, ln1w=128, ln2w=144, scale_bc=160, onesr=161,
    stats=292, fillswa=356, fillmin=612,
    qqn=1124, kkn=1636, b_bk=2148, b_bv=2660,
    b_bq=3172, b_bo=5220,
    w0=7268, w1=9316, w2=11364, w3=13412, w4=15460,
    cs0=17508, cs1=18020,
)
UO = dict(mask_swa=0, mask_p1=512, mask_p2=2560)
FILL2 = 18532
NEGONES = 19556
BO = dict(identb=0, a_bf=128, woap=384)

SEM_NAMES = [
    "d_c", "d_x0", "d_x1", "d_cs0", "d_cs1",
    "d_tz0", "d_tz1", "d_tx0", "d_tx1", "d_tq0", "d_tq1", "d_tk0", "d_tk1",
    "d_xs0", "d_xs1", "d_yp0", "d_yp1",
    "d_ag", "d_w0", "d_sw", "d_wo",
    "d_z0", "d_z1", "d_zs0", "d_zs1", "d_zt4",
    "d_mp", "d_bb", "d_xr0", "d_xr1",
    "d_wi0", "d_wi1", "d_wu0", "d_wu1", "d_wd0", "d_wd1",
    "d_wi2", "d_wi3", "d_wu2", "d_wu3", "d_wd2", "d_wd3",
    "d_wv0", "d_wv1", "d_wv2", "d_wv3", "d_wx0", "d_wx1", "d_wx2", "d_wx3",
    "d_we0", "d_we1", "d_we2", "d_we3",
    "d_yb0", "d_yb1", "d_yw", "d_out",
    "cc1", "cc2",
    "pe", "dve", "act", "pool",
    "bs_pe", "bs_dve", "bs_act", "bs_pool", "bs_sync",
]

B_ZT = 448
B_ZBF = (16832, 18880)
B_XNBF = (20928, 22976)
B_ZTHS = 25024
B_XNT = 448
B_KFM = 18880
B_VSB = 23488
B_WKS = 28096
B_WVS = 36288
B_XN2 = (44480, 46528)
B_KB = (48576, 49088)
B_WQS = 28096
B_QFM = 44480
B_QB = (36288, 36800)
B_PN = 28096
B_PT = 28352
B_HEADS = 448
B_WOL = 16832
B_WOH = 33216
B_KF3 = 448
B_ZTD = (16832, 25024)
B_WKF = 33216
B_WVF = 37312
B_VA = 41408
B_VT = 43584
B_WQF = 33216
B_QS = (37312, 38336)
B_PTB = (43584, 44608)
B_ZTS = 16832
B_WOFB = 448
B_MLPIN = 448
B_ZTHB = 33216
B_WIN = (41408, 43456)
B_WUP = (45504, 47552)
B_BBF = 49600
B_WDN = (41408, 45504)
B_HAMT = 57792
B_MPB = 58816


def build_nc(upto="full"):
    nc = bass.Bass("TRN2", target_bir_lowering=False)
    d_in = {}

    def IN(name, shape, dt=F32):
        d_in[name] = nc.dram_tensor(name, list(shape), dt, kind="ExternalInput").ap()
        return d_in[name]

    x_sl = IN("x_sl", (NTH * 128, D))
    cos_t = IN("cos_t", (NTH * 128, 256))
    sin_t = IN("sin_t", (NTH * 128, 256))
    m_swa = IN("m_swa", (2 * 128, 256), U8)
    m_p1 = IN("m_p1", (4 * 128, 512), U8)
    m_p2 = IN("m_p2", (4 * 128, 512), U8)
    identf_i = IN("identf", (128, 128))
    identb_i = IN("identb", (128, 128), BF16)
    onesr_i = IN("onesr", (1, 128))
    ln1w_i = IN("ln1w", (128, 16))
    ln2w_i = IN("ln2w", (128, 16))
    scale_i = IN("scale_bc", (128, 1))
    b_bq_i = IN("b_bq", (128, D))
    b_bk_i = IN("b_bk", (128, 512))
    b_bv_i = IN("b_bv", (128, 512))
    b_bo_i = IN("b_bo", (128, D))
    qqn_i = IN("qqn", (128, 512))
    kkn_i = IN("kkn", (128, 512))
    wq_sw = IN("wq_sw", (4, 128, CH, 512))
    wk_sw = IN("wk_sw", (128, CH, 512))
    wv_sw = IN("wv_sw", (128, CH, 512))
    wo_sw = IN("wo_sw", (16, 128, 4, 512))
    wq_ff = IN("wq_ff", (128, CH, 256))
    wk_ff = IN("wk_ff", (128, CH, 256))
    wv_ff = IN("wv_ff", (128, CH, 256))
    wo_ff = IN("wo_ff", (128, CH, 256))
    a_t = IN("a_t", (128, CH, 16))
    b_t = IN("b_t", (16, F_))
    w_in_t = IN("w_in_t", (NFC, 128, CH, 128))
    w_up_t = IN("w_up_t", (NFC, 128, CH, 128))
    w_dn_t = IN("w_dn_t", (4, 128, NFC, 512))

    y_out = nc.dram_tensor("y", [TOK, D], F32, kind="ExternalOutput").ap()

    DBG_P2 = False
    DBG_P4 = False
    if DBG_P4:
        dbg_mlpin = nc.dram_tensor("dbg_mlpin", [128, 32768], BF16, kind="ExternalOutput")
        dbg_hamt = nc.dram_tensor("dbg_hamt", [16, 1024], BF16, kind="ExternalOutput")
        dbg_ha = nc.dram_tensor("dbg_ha", [128, 128], F32, kind="ExternalOutput")
    dbgkind = "ExternalOutput" if upto in ("p1",) else "Internal"
    if upto == "p2" and DBG_P2:
        dbg_heads = nc.dram_tensor("dbg_heads", [128, 16384], BF16, kind="ExternalOutput")
        dbg_qfm = nc.dram_tensor("dbg_qfm", [128, 16384], BF16, kind="ExternalOutput")
        dbg_kfm = nc.dram_tensor("dbg_kfm", [128, 4608], BF16, kind="ExternalOutput")
        dbg_vsb = nc.dram_tensor("dbg_vsb", [128, 4608], BF16, kind="ExternalOutput")
        dbg_kfm_a = nc.dram_tensor("dbg_kfm_a", [128, 4608], BF16, kind="ExternalOutput")
        dbg_kfm_b = nc.dram_tensor("dbg_kfm_b", [128, 4608], BF16, kind="ExternalOutput")
    ag_in = nc.dram_tensor("ag_in", [NT * 128, D], BF16, kind=dbgkind)
    ag_out_t = nc.dram_tensor("ag_out", [NCORE * NT * 128, D], BF16, addr_space="Shared")
    ar_in = nc.dram_tensor("ar_in", [64, 2048], F32,
                           kind="ExternalOutput" if upto == "p3" else "Internal")
    ar_out = nc.dram_tensor("ar_out", [8, 2048], F32)
    y_part = nc.dram_tensor("y_part", [TOK, D], F32,
                            kind="ExternalOutput" if upto in ("p1", "p2") else "Internal")
    xn_dram = nc.dram_tensor("xn_dram", [NTH * 128, D], BF16, kind=dbgkind)
    zth_dram = nc.dram_tensor("zth_dram", [2, 128, CH * 512], BF16, kind=dbgkind)
    ag_out = ag_out_t.ap()

    ctx = ExitStack()
    AR_F = ctx.enter_context(nc.sbuf_tensor("arena_f", [128, F32_TOT], F32))
    AR_B = ctx.enter_context(nc.sbuf_tensor("arena_b", [128, BF_TOT], BF16))
    AR_U = ctx.enter_context(nc.sbuf_tensor("arena_u", [128, U8_TOT], U8))
    PBW01 = ctx.enter_context(nc.psum_tensor("pbw01", [128, 1024], F32))
    PBW56 = ctx.enter_context(nc.psum_tensor("pbw56", [128, 1024], F32))
    PB3T = ctx.enter_context(nc.psum_tensor("pb3", [128, 512], F32))
    PB4T = ctx.enter_context(nc.psum_tensor("pb4", [128, 512], F32))
    PBF = ctx.enter_context(nc.psum_tensor("pbf", [128, 1024], BF16))
    PBF2 = ctx.enter_context(nc.psum_tensor("pbf2", [128, 1024], BF16))
    PBFS = (PBF, PBF2)

    class _Half:
        def __init__(self, t, off):
            self.t, self.off = t, off
        def __getitem__(self, idx):
            ps, cs = idx
            a = cs.start or 0
            b_ = 512 if cs.stop is None else cs.stop
            return self.t[ps, self.off + a:self.off + b_]

    PB = {0: _Half(PBW01, 0), 1: _Half(PBW01, 512),
          3: _Half(PB3T, 0), 4: _Half(PB4T, 0),
          5: _Half(PBW56, 0), 6: _Half(PBW56, 512)}
    sems = {}
    for s_ in SEM_NAMES:
        sems[s_] = ctx.enter_context(nc.semaphore(s_))

    def FA(off, n, p0=0, p=128):
        return AR_F[p0:p0 + p, off:off + n]

    def BA(off, n, p0=0, p=128):
        return AR_B[p0:p0 + p, off:off + n]

    def UA(off, n):
        return AR_U[:, off:off + n]

    V_ident = FA(FO["ident"], 128)
    V_identb = BA(BO["identb"], 128)
    V_ln2w = FA(FO["ln2w"], 16)
    V_ln1w = FA(FO["ln1w"], 16)
    V_scale = FA(FO["scale_bc"], 1)
    V_onesr = FA(FO["onesr"], 128, p0=0, p=1)
    V_fillswa = FA(FO["fillswa"], 256)
    V_fillmin = FA(FO["fillmin"], 512)
    V_qqn = FA(FO["qqn"], 512)
    V_kkn = FA(FO["kkn"], 512)
    V_bbk = FA(FO["b_bk"], 512)
    V_bbv = FA(FO["b_bv"], 512)
    V_bbq = FA(FO["b_bq"], D)
    V_bbo = FA(FO["b_bo"], D)
    V_abf = BA(BO["a_bf"], 256)
    V_woap = BA(BO["woap"], 32)
    W0, W1, W2, W3, W4 = (FO["w0"], FO["w1"], FO["w2"], FO["w3"], FO["w4"])
    CS = (FO["cs0"], FO["cs1"])

    def stat(i, n=1, p=128, p0=0):
        return FA(FO["stats"] + i, n, p0=p0, p=p)

    def emit(env, block):
        E, W, MS = env.E, env.W, env.MS

        def emit_zev(sc, j):
            tt = j - 1
            for c in range(CH):
                W(sc, f"ztrp_{j}_{c}")
                E(lambda c=c, tt=tt: sc.copy(
                    out=BA(B_ZT + 2048 * tt + 128 * c, 128),
                    in_=PBFS[c % 2][:, 0:128]), "act", 1)
                MS(f"zev_{j}_{c}", "act")
            MS(f"ztr_{j}", "act")

        def sync_p1_tail(q, j):
            par = j % 2
            W(q, f"xnb_{j}")
            E(lambda j=j, par=par: q.dma_start(
                out=xn_dram.ap()[128 * j:128 * (j + 1), :],
                in_=BA(B_XNBF[par], D)), f"d_xs{par}", 16)
            MS(f"xnsp_{j}", f"d_xs{par}")
            if j >= 1:
                tt = j - 1
                W(q, f"y_{j}")
                E(lambda tt=tt: q.dma_start(
                    out=y_part.ap()[128 * tt:128 * (tt + 1), :], in_=FA(W4, D)),
                  f"d_yp{j%2}", 16)
                MS(f"yst_{j}", f"d_yp{j%2}")
                if upto == "p1":
                    E(lambda tt=tt: q.dma_start(
                        out=y_out[128 * tt:128 * (tt + 1), :], in_=FA(W4, D)), "d_out", 16)


        @block.sync
        def _(q):
            for f in [
                lambda: q.dma_start(out=V_ident, in_=identf_i),
                lambda: q.dma_start(out=V_identb, in_=identb_i),
                lambda: q.dma_start(out=V_ln1w, in_=ln1w_i),
                lambda: q.dma_start(out=V_ln2w, in_=ln2w_i),
                lambda: q.dma_start(out=V_scale, in_=scale_i),
                lambda: q.dma_start(out=V_onesr, in_=onesr_i),
                lambda: q.dma_start(out=V_qqn, in_=qqn_i),
                lambda: q.dma_start(out=V_kkn, in_=kkn_i),
                lambda: q.dma_start(out=V_bbk, in_=b_bk_i),
                lambda: q.dma_start(out=V_bbv, in_=b_bv_i),
                lambda: q.dma_start(out=V_bbq, in_=b_bq_i),
                lambda: q.dma_start(out=V_bbo, in_=b_bo_i),
                lambda: q.dma_start(out=UA(UO["mask_swa"], 512).rearrange("p (m n) -> p m n", m=2),
                                    in_=m_swa.rearrange("(m p) n -> p m n", p=128)),
                lambda: q.dma_start(out=UA(UO["mask_p1"], 2048).rearrange("p (m n) -> p m n", m=4),
                                    in_=m_p1.rearrange("(m p) n -> p m n", p=128)),
                lambda: q.dma_start(out=UA(UO["mask_p2"], 2048).rearrange("p (m n) -> p m n", m=4),
                                    in_=m_p2.rearrange("(m p) n -> p m n", p=128)),
            ]:
                E(f, "d_c", 16)
            MS("consts", "d_c")
            for i in range(NTH):
                par = i % 2
                if i >= 2:
                    W(q, f"p1x_free_{i-2}")
                E(lambda i=i, par=par: q.dma_start(
                    out=FA((W0, W1)[par], D), in_=x_sl[128 * i:128 * (i + 1), :]),
                  f"d_x{par}", 16)
                MS(f"x_{i}", f"d_x{par}")
                if i >= 1:
                    sync_p1_tail(q, i - 1)
            sync_p1_tail(q, NTH - 1)
            W(q, "ztr_8")
            E(lambda: q.dma_start(
                out=ag_in.ap().rearrange("(t p) n -> p t n", p=128),
                in_=BA(B_ZT, NT * D).rearrange("p (t n) -> p t n", t=NT)), "d_ag", 16)
            MS("ag_in_done", "d_ag")
            for c in range(CH):
                W(q, f"zths_{c}")
                E(lambda c=c: q.dma_start(
                    out=zth_dram.ap()[0, :, 512 * c:512 * (c + 1)],
                    in_=BA(B_ZTHS, 512)), "d_mp", 16)
                E(lambda c=c: q.dma_start(
                    out=zth_dram.ap()[1, :, 512 * c:512 * (c + 1)],
                    in_=BA(B_ZTHS + 512, 512)), "d_mp", 16)
                MS(f"zthsp_{c}", "d_mp")

        @block.vector
        def _(v):
            E(lambda: v.memset(V_fillswa, SWA_FILL))
            E(lambda: v.memset(stat(8), 0.0))
            E(lambda: v.memset(stat(9), LN_EPS))
            E(lambda: v.memset(stat(10), RMS2_EPS))
            E(lambda: v.memset(stat(11), QK_EPS))
            E(lambda: v.memset(FA(FILL2, 1024), MIN_FILL))
            E(lambda: v.memset(FA(NEGONES, 128, p=1), -1.0))
            E(lambda: v.memset(V_fillmin, MIN_FILL), "dve", 1)
            MS("fills", "dve")
            W(v, "consts")
            for i in range(NTH):
                par = i % 2
                own = i >= 1
                W(v, f"x_{i}")
                xb = FA((W0, W1)[par], D)
                E(lambda xb=xb: v.reduce_sum(out=stat(0), in_=xb, axis=AX.X))
                E(lambda: v.drain())
                E(lambda: v.tensor_scalar_mul(out=stat(1), in0=stat(0),
                                              scalar1=-1.0 / D))
                E(lambda: v.drain(), "dve", 1)
                MS(f"nm_{i}", "dve")
                W(v, f"sd_{i}")
                if i >= 1:
                    W(v, f"sd2_{i-1}" if i - 1 >= 1 else "xnb_0")
                E(lambda: v.reciprocal(out=stat(3), in_=stat(2)))
                E(lambda: v.drain())
                E(lambda xb=xb: v.tensor_scalar_add(out=FA(W3, D), in0=xb, scalar1=stat(1)))
                E(lambda: v.tensor_scalar_mul(out=FA(W2, D), in0=FA(W3, D), scalar1=stat(3)))
                if own:
                    if i >= 2:
                        W(v, f"yst_{i-1}")
                    E(lambda xb=xb: v.tensor_add(out=FA(W4, D), in0=xb, in1=FA(W2, D)))
                    E(lambda: v.tensor_add(out=FA(W4, D), in0=FA(W4, D), in1=V_bbo))
                E(lambda: v.drain(), "dve", 1)
                MS(f"xn_{i}", "dve")
                MS(f"y_{i}", "dve")
                MS(f"p1x_free_{i}", "dve")
                if own:
                    W(v, f"sd2_{i}")
                    if i >= 3:
                        W(v, f"ztr_{i-2}")
                    E(lambda: v.reciprocal(out=stat(5), in_=stat(4)))
                    E(lambda: v.drain())
                    E(lambda par=par: v.tensor_scalar_mul(
                        out=BA(B_ZBF[par], D), in0=FA(W2, D), scalar1=stat(5)), "dve", 1)
                    MS(f"z_{i}", "dve")
            W(v, "ztr_8")
            zt3 = BA(B_ZT, NT * D).rearrange("p (t n) -> p t n", t=NT)
            for c in range(CH):
                if c >= 1:
                    W(v, f"zthsp_{c-1}")
                E(lambda c=c: v.tensor_scalar_mul(
                    out=BA(B_ZTHS, 1024).rearrange("p (b n) -> p b n", b=8),
                    in0=zt3[:, :, 128 * c:128 * (c + 1)],
                    scalar1=V_ln2w[:, c:c + 1]), "dve", 1)
                MS(f"zths_{c}", "dve")

        @block.scalar
        def _(sc):
            for i in range(NTH):
                par = i % 2
                own = i >= 1
                xb = FA((W0, W1)[par], D)
                W(sc, f"nm_{i}")
                E(lambda xb=xb: sc.activation(out=FA(W3, D), in_=xb, func=AF.Square,
                                              bias=stat(1), scale=1.0, accum_out=stat(6)))
                E(lambda: sc.activation(out=stat(2), in_=stat(6), func=AF.Sqrt,
                                        bias=stat(9), scale=1.0 / D))
                E(lambda: sc.drain(), "act", 1)
                MS(f"sd_{i}", "act")
                W(sc, f"xn_{i}")
                if i >= 2:
                    W(sc, f"xnsp_{i-2}")
                E(lambda par=par: sc.copy(out=BA(B_XNBF[par], D), in_=FA(W2, D)))
                if own:
                    E(lambda: sc.activation(out=FA(W3, D), in_=FA(W2, D), func=AF.Square,
                                            bias=stat(8), scale=1.0, accum_out=stat(7)))
                    E(lambda: sc.activation(out=stat(4), in_=stat(7), func=AF.Sqrt,
                                            bias=stat(10), scale=1.0 / D))
                E(lambda: sc.drain(), "act", 1)
                MS(f"xnb_{i}", "act")
                if own:
                    MS(f"sd2_{i}", "act")
                if i >= 2:
                    emit_zev(sc, i - 1)
            emit_zev(sc, NTH - 1)

        @block.tensor
        def _(pe):
            for j in range(1, NTH):
                par = j % 2
                W(pe, f"z_{j}")
                for c in range(CH):
                    if (j, c) != (1, 0) and not (j == 1 and c == 1):
                        pc_, cc_ = (j, c - 2) if c >= 2 else (j - 1, CH - 2 + c)
                        W(pe, f"zev_{pc_}_{cc_}")
                    E(lambda c=c, par=par: pe.transpose(
                        PBFS[c % 2][:, 0:128], BA(B_ZBF[par] + 128 * c, 128),
                        V_identb), "pe", 1)
                    MS(f"ztrp_{j}_{c}", "pe")

        for name in ENGS:
            dec = dict(pe=block.tensor, dve=block.vector, act=block.scalar,
                       pool=block.gpsimd, sync=block.sync)[name]
            def mk(name):
                return lambda e: env.bar(e, name, "p1")
            dec(mk(name))

        if upto == "p1":
            @block.gpsimd
            def _(g):
                n = env.ctr.get("d_out", 0)
                if not env.dry and n:
                    g.wait_ge(sems["d_out"], n)
            return

        emit_p2(env, block)
        if upto == "p2":
            @block.gpsimd
            def _(g):
                n = env.ctr.get("d_yw", 0)
                if not env.dry and n:
                    g.wait_ge(sems["d_yw"], n)
            return
        emit_p3(env, block)
        if upto in ("p3", "p3r"):
            @block.gpsimd
            def _(g):
                if upto == "p3r":
                    W(g, "rs_trig")
                W(g, "ar_in_done")
                n = env.ctr.get("d_yw", 0)
                if not env.dry and n:
                    g.wait_ge(sems["d_yw"], n)
            return
        emit_p4(env, block, stop=(upto if upto in ("p4a", "p4z", "p4y") else None))
        @block.gpsimd
        def _(g):
            if upto == "p4a":
                W(g, "ham_done")
            elif upto == "p4z":
                W(g, "mod_in")
            elif upto == "p4y":
                W(g, "hAev_7")
            else:
                W(g, "y_done")

    ST_MC = 33
    ST_CS3 = 35
    ST_NEGB = 37
    ST_BIAS = 39
    ST_RZ3 = 41

    def emit_p3(env, block):
        E, W, MS = env.E, env.W, env.MS

        def zt_view(buf_off):
            return BA(buf_off, 8192).rearrange("p (b c l) -> p b c l", b=4, c=CH)

        def nkt(qt):
            return 4 * qt + 4

        def p1_iters(qt):
            out = []
            for sub in range(4):
                nf = (128 * (4 * qt + sub)) // 512
                for kc in range(nf + 1):
                    out.append((sub, kc, kc == nf))
            return out

        @block.gpsimd
        def _(g):
            E(lambda: g.dma_start(out=BA(448, 4096).rearrange("p (c j) -> p c j", c=CH),
                                  in_=wo_ff), "d_w0", 16)
            E(lambda: g.dma_start(out=V_abf.rearrange("p (c r) -> p c r", c=CH),
                                  in_=a_t), "d_w0", 16)
            E(lambda: g.dma_start(out=BA(B_WKF, 4096).rearrange("p (c o) -> p c o", c=CH),
                                  in_=wk_ff), "d_w0", 16)
            E(lambda: g.dma_start(out=BA(B_WVF, 4096).rearrange("p (c o) -> p c o", c=CH),
                                  in_=wv_ff), "d_w0", 16)
            MS("p3w", "d_w0")
            E(lambda: g.memset(BA(B_VA, 2176), 1.0))
            E(lambda: g.drain(), "pool", 1)
            MS("va_ones", "pool")
            W(g, "p3kv_pe_done")
            E(lambda: g.dma_start(out=BA(B_WQF, 4096).rearrange("p (c o) -> p c o", c=CH),
                                  in_=wq_ff), "d_w0", 16)
            MS("p3wq", "d_w0")
            if upto not in ("p3",):
                W(g, "ar_in_done")
                E(lambda: g.collective_compute(
                    "ReduceScatter", ALU.add, replica_groups=[list(range(NCORE))],
                    ins=[ar_in.ap().opt()], outs=[ar_out.ap().opt()]), "cc2", 1)
                MS("rs_trig", "cc2")

        @block.tensor
        def _(pe):
            W(pe, "p3w")
            for hh in range(2):
                if hh == 1:
                    W(pe, "woa_ev_0")
                for c in range(CH):
                    E(lambda hh=hh, c=c: pe.matmul(
                        PB[0][:, 0:16], BA(448 + 256 * c + 128 * hh, 128),
                        V_abf[:, 16 * c:16 * (c + 1)],
                        start=(c == 0), stop=(c == CH - 1)),
                      "pe" if c == CH - 1 else None, 1)
                MS(f"woa_{hh}", "pe")
            MS("woa_pe_done", "pe")

        @block.scalar
        def _(sc):
            for hh in range(2):
                W(sc, f"woa_{hh}")
                E(lambda hh=hh: sc.copy(out=V_woap[:, 16 * hh:16 * (hh + 1)],
                                        in_=PB[0][:, 0:16]))
                E(lambda: sc.drain(), "act", 1)
                MS(f"woa_ev_{hh}", "act")

        @block.vector
        def _(v):
            W(v, "p3w")
            for c in range(CH):
                E(lambda c=c: v.tensor_scalar_mul(
                    out=BA(B_WKF + 256 * c, 256), in0=BA(B_WKF + 256 * c, 256),
                    scalar1=V_ln1w[:, c:c + 1]))
                E(lambda c=c: v.tensor_scalar_mul(
                    out=BA(B_WVF + 256 * c, 256), in0=BA(B_WVF + 256 * c, 256),
                    scalar1=V_ln1w[:, c:c + 1]))
            E(lambda: v.drain(), "dve", 1)
            MS("kvf_folded", "dve")

        @block.sync
        def _(q):
            for cc in range(16):
                par = cc % 2
                if cc >= 2:
                    W(q, f"p3kv_used_{cc-2}")
                if cc == 0:
                    W(q, "ag_trig")
                E(lambda cc=cc, par=par: q.dma_start(
                    out=zt_view(B_ZTD[par]),
                    in_=ag_out[512 * cc:512 * (cc + 1), :]
                    .rearrange("(b p) n -> p b n", p=128)), f"d_z{par}", 16)
                MS(f"zt_{cc}", f"d_z{par}")

        @block.tensor
        def _(pe):
            W(pe, "kvf_folded")
            W(pe, "woa_ev_1")
            for cc in range(16):
                par = cc % 2
                W(pe, f"zt_{cc}")
                for h in range(2):
                    if not (cc == 0 and h == 0):
                        pcc, ph = (cc, 0) if h == 1 else (cc - 1, 1)
                        W(pe, f"kev_{pcc}_{ph}")
                        W(pe, f"vev_{pcc}_{ph}")
                        W(pe, f"vaev_{pcc}_{ph}")
                    for c in range(CH):
                        E(lambda h=h, c=c, par=par: pe.matmul(
                            PB[0][:, :], BA(B_WKF + 256 * c + 128 * h, 128),
                            zt_view(B_ZTD[par])[:, :, c, :],
                            start=(c == 0), stop=(c == CH - 1)),
                          "pe" if c == CH - 1 else None, 1)
                    MS(f"kpe3_{cc}_{h}", "pe")
                    for c in range(CH):
                        E(lambda h=h, c=c, par=par: pe.matmul(
                            PB[1][:, :], BA(B_WVF + 256 * c + 128 * h, 128),
                            zt_view(B_ZTD[par])[:, :, c, :],
                            start=(c == 0), stop=(c == CH - 1)),
                          "pe" if c == CH - 1 else None, 1)
                    MS(f"vpe3_{cc}_{h}", "pe")
                    W(pe, f"vev_{cc}_{h}")
                    for j in range(4):
                        E(lambda h=h, j=j: pe.matmul(
                            PB[3][:, 16 * j:16 * (j + 1)], BA(B_VT + 128 * j, 128),
                            V_woap[:, 16 * h:16 * (h + 1)], start=True, stop=True),
                          "pe" if j == 3 else None, 1)
                    MS(f"vape_{cc}_{h}", "pe")
                MS(f"p3kv_used_{cc}", "pe")
            MS("p3kv_pe_done", "pe")

        @block.scalar
        def _(sc):
            W(sc, "va_ones")
            for cc in range(16):
                for h in range(2):
                    W(sc, f"kpe3_{cc}_{h}")
                    E(lambda cc=cc, h=h: sc.copy(
                        out=BA(B_KF3 + 8192 * h + 512 * cc, 512), in_=PB[0][:, :]), "act", 1)
                    MS(f"kev_{cc}_{h}", "act")
                    W(sc, f"vpe3_{cc}_{h}")
                    E(lambda h=h: sc.copy(out=BA(B_VT, 512), in_=PB[1][:, :]), "act", 1)
                    MS(f"vev_{cc}_{h}", "act")
                    W(sc, f"vape_{cc}_{h}")
                    for j in range(4):
                        E(lambda cc=cc, h=h, j=j: sc.copy(
                            out=BA(B_VA + 1088 * h + 17 * (4 * cc + j), 16),
                            in_=PB[3][:, 16 * j:16 * (j + 1)]),
                          "act" if j == 3 else None, 1)
                    MS(f"vaev_{cc}_{h}", "act")

        PW = (PBW01, PBW56)
        s_last = {0: "vev_15_1", 1: None}
        C0 = FO["cs0"]
        KCH = C0
        KPF = C0 + 32
        QM2 = C0 + 96
        QK2 = C0 + 100
        LTS = C0 + 104
        CVS = C0 + 108
        K2B = 45632
        Q2B = 47680

        @block.sync
        def _(q):
            for qt in range(16):
                if qt >= 1:
                    W(q, f"qproj_done_{qt-1}")
                else:
                    W(q, "p3kv_pe_done")
                E(lambda qt=qt: q.dma_start(
                    out=zt_view(B_ZTS),
                    in_=ag_out[512 * qt:512 * (qt + 1), :]
                    .rearrange("(b p) n -> p b n", p=128)), f"d_zs{qt%2}", 16)
                MS(f"zts_{qt}", f"d_zs{qt%2}")
            W(q, "modpreT_ready")
            E(lambda: q.dma_start(
                out=ar_in.ap().rearrange("s (r p) -> r s p", r=16),
                in_=FA(W0, 8192, p=16).rearrange("r (s p) -> r s p", s=64)),
              "d_mp", 16)
            MS("ar_in_done", "d_mp")

        @block.tensor
        def _(pe):
            W(pe, "qf_folded")
            for i in range(32):
                if i == 0:
                    W(pe, "vaev_15_1")
                if i >= 2:
                    W(pe, f"km_{i-2}")
                W(pe, f"k2k_{i}")
                E(lambda i=i: pe.matmul(
                    PB[(3, 4)[i % 2]][0:1, 0:512], BA(B_VA + 16, 1),
                    BA(K2B + 512 * (i % 4), 512),
                    start=True, stop=True), "pe", 1)
                MS(f"csk_{i}", "pe")
            W(pe, "km_30")
            W(pe, "km_31")
            for qt in range(16):
                W(pe, f"zts_{qt}")
                for h in range(2):
                    if qt >= 1 or h >= 1:
                        pqt, ph = (qt, 0) if h == 1 else (qt - 1, 1)
                        W(pe, f"qev_{pqt}_{ph}")
                    for c in range(CH):
                        E(lambda h=h, c=c: pe.matmul(
                            PB[3][:, :], BA(B_WQF + 256 * c + 128 * h, 128),
                            zt_view(B_ZTS)[:, :, c, :],
                            start=(c == 0), stop=(c == CH - 1)),
                          "pe" if c == CH - 1 else None, 1)
                    MS(f"qpe3_{qt}_{h}", "pe")
                for h in range(2):
                    W(pe, f"q2_{qt}_{h}")
                    E(lambda h=h: pe.matmul(
                        PB[(4, 3)[h]][0:1, 0:512], BA(B_VA + 16, 1),
                        BA(Q2B + 512 * h, 512),
                        start=True, stop=True), "pe", 1)
                    MS(f"csq_{qt}_{h}", "pe")
                for h in range(2):
                    np2 = nkt(qt) // 2

                    def emit_av(pj, qt=qt, h=h):
                        W(pe, f"exppt_{qt}_{h}_{pj}")
                        for half in range(2):
                            kt = 2 * pj + half
                            E(lambda kt=kt, pj=pj, half=half, qt=qt, h=h: pe.matmul(
                                PB[4][0:17, :], BA(B_VA + 1088 * h + 17 * kt, 17),
                                BA(B_PTB[pj % 2] + 512 * half, 512),
                                start=(kt == 0), stop=(kt == nkt(qt) - 1)),
                              "pe", 1)
                        MS(f"av_{qt}_{h}_{pj}", "pe")

                    for pi in range(np2):
                        wb = pi % 2
                        if s_last[wb]:
                            W(pe, s_last[wb])
                        for half in range(2):
                            kt = 2 * pi + half
                            E(lambda qt=qt, h=h, kt=kt, wb=wb, half=half: pe.matmul(
                                PW[wb][:, 512 * half:512 * (half + 1)],
                                BA(B_KF3 + 8192 * h + 128 * kt, 128),
                                BA(B_QS[qt % 2] + 512 * h, 512),
                                start=True, stop=True), "pe", 1)
                        MS(f"s2_{qt}_{h}_{pi}", "pe")
                        s_last[wb] = f"exp_{qt}_{h}_{pi}"
                        if pi == 1:
                            if h == 0:
                                W(pe, f"qev_{qt}_1")
                                W(pe, f"qk2_{qt}_1")
                            W(pe, f"cq_{qt}_{h}")
                            E(lambda h=h: pe.matmul(
                                PB[3][:, 256 + 32 * h:257 + 32 * h],
                                FA(NEGONES, 128, p=1),
                                FA(CVS + h, 1, p=1), start=True, stop=True),
                              "pe", 1)
                            MS(f"biasmm_{qt}_{h}", "pe")
                        if pi >= 1:
                            emit_av(pi - 1)
                    emit_av(np2 - 1)
                    MS(f"avdone_{qt}_{h}", "pe")
                    W(pe, f"oev_{qt}_{h}")
                    for jj in range(4):
                        E(lambda jj=jj: pe.transpose(
                            PB[3][:, 32 * jj:32 * jj + 17],
                            FA(W2 + 128 * jj, 128, p=17), V_ident[0:17, 0:17]),
                          "pe" if jj == 3 else None, 1)
                    MS(f"otr_{qt}_{h}", "pe")
                    W(pe, f"modp_{qt}_{h}")
            if s_last[0]:
                W(pe, s_last[0])
            for s in range(64):
                if s >= 2:
                    W(pe, f"mptev_{s-2}")
                E(lambda s=s: pe.transpose(PB[s % 2][0:16, 0:128],
                                           FA(W4 + 16 * s, 16), V_ident),
                  "pe", 1)
                MS(f"mptr_{s}", "pe")

        @block.vector
        def _(v):
            W(v, "p3wq")
            for c in range(CH):
                E(lambda c=c: v.tensor_scalar_mul(
                    out=BA(B_WQF + 256 * c, 256), in0=BA(B_WQF + 256 * c, 256),
                    scalar1=V_ln1w[:, c:c + 1]))
            E(lambda: v.drain(), "dve", 1)
            MS("qf_folded", "dve")

            def emit_km(i):
                W(v, f"csk_{i}")
                E(lambda i=i: v.reduce_max(
                    out=FA(KCH + 16 * (i % 2) + i // 2, 1, p=1),
                    in_=PB[(3, 4)[i % 2]][0:1, 0:512], axis=AX.X))
                E(lambda: v.drain(), "dve", 1)
                MS(f"km_{i}", "dve")

            for i in range(32):
                cc, h = i // 2, i % 2
                W(v, f"kev_{cc}_{h}")
                if i >= 4:
                    W(v, f"csk_{i-4}")
                E(lambda cc=cc, h=h, i=i: v.tensor_mul(
                    out=BA(K2B + 512 * (i % 4), 512),
                    in0=BA(B_KF3 + 8192 * h + 512 * cc, 512),
                    in1=BA(B_KF3 + 8192 * h + 512 * cc, 512)), "dve", 1)
                MS(f"k2k_{i}", "dve")
                if i >= 2:
                    emit_km(i - 2)
            for i in (30, 31):
                emit_km(i)
            E(lambda: v.drain())
            for h in range(2):
                for qt in range(16):
                    E(lambda h=h, qt=qt: v.reduce_max(
                        out=FA(KPF + 16 * h + qt, 1, p=1),
                        in_=FA(KCH + 16 * h, qt + 1, p=1), axis=AX.X))
                E(lambda: v.drain())
                E(lambda h=h: v.tensor_scalar_mul(
                    out=FA(KPF + 16 * h, 16, p=1),
                    in0=FA(KPF + 16 * h, 16, p=1), scalar1=SIG * SIG))
            E(lambda: v.drain(), "dve", 1)
            MS("kpf_done", "dve")
            for qt in range(16):
                for h in range(2):
                    W(v, f"qev_{qt}_{h}")
                    if qt >= 1:
                        W(v, f"csq_{qt-1}_{h}")
                    E(lambda h=h, qt=qt: v.tensor_mul(
                        out=BA(Q2B + 512 * h, 512),
                        in0=BA(B_QS[qt % 2] + 512 * h, 512),
                        in1=BA(B_QS[qt % 2] + 512 * h, 512)), "dve", 1)
                    MS(f"q2_{qt}_{h}", "dve")
                for h in range(2):
                    W(v, f"csq_{qt}_{h}")
                    E(lambda h=h: v.reduce_max(
                        out=FA(QM2 + h, 1, p=1),
                        in_=PB[(4, 3)[h]][0:1, 0:512], axis=AX.X))
                    E(lambda: v.drain())
                    E(lambda h=h, qt=qt: v.tensor_mul(
                        out=FA(QK2 + h, 1, p=1), in0=FA(QM2 + h, 1, p=1),
                        in1=FA(KPF + 16 * h + qt, 1, p=1)))
                    E(lambda: v.drain(), "dve", 1)
                    MS(f"qk2_{qt}_{h}", "dve")
                for h in range(2):
                    np2 = nkt(qt) // 2
                    for pi in range(2 * qt, np2):
                        wb = pi % 2
                        W(v, f"s2_{qt}_{h}_{pi}")
                        if pi - 1 >= 2 * qt:
                            W(v, f"exp_{qt}_{h}_{pi-1}")
                        E(lambda wb=wb: v.tensor_copy(out=FA(W1, 1024),
                                                      in_=PW[wb][:, 0:1024]))
                        E(lambda qt=qt, pi=pi: v.copy_predicated(
                            out=FA(W1, 1024),
                            mask=UA(UO["mask_p2"] + 1024 * (pi - 2 * qt), 1024),
                            data=FA(FILL2, 1024)), "dve", 1)
                        MS(f"sel2_{qt}_{h}_{pi}", "dve")
                    W(v, f"otr_{qt}_{h}")
                    for jj in range(4):
                        s = 4 * qt + jj
                        E(lambda jj=jj: v.reciprocal(
                            out=stat(ST_RZ3),
                            in_=PB[3][:, 32 * jj + 16:32 * jj + 17]))
                        E(lambda: v.drain())
                        if h == 0:
                            E(lambda jj=jj, s=s: v.tensor_scalar_mul(
                                out=FA(W4 + 16 * s, 16),
                                in0=PB[3][:, 32 * jj:32 * jj + 16],
                                scalar1=stat(ST_RZ3)))
                        else:
                            E(lambda jj=jj, s=s: v.scalar_tensor_tensor(
                                out=FA(W4 + 16 * s, 16),
                                in0=PB[3][:, 32 * jj:32 * jj + 16],
                                scalar=stat(ST_RZ3), in1=FA(W4 + 16 * s, 16),
                                op0=ALU.mult, op1=ALU.add))
                    E(lambda: v.drain(), "dve", 1)
                    MS(f"modp_{qt}_{h}", "dve")
            for s in range(64):
                W(v, f"mptr_{s}")
                E(lambda s=s: v.tensor_copy(out=FA(W0 + 128 * s, 128, p=16),
                                            in_=PB[s % 2][0:16, 0:128]), "dve", 1)
                MS(f"mptev_{s}", "dve")
            MS("modpreT_ready", "dve")

        @block.scalar
        def _(sc):
            for qt in range(16):
                for h in range(2):
                    W(sc, f"qpe3_{qt}_{h}")
                    if qt >= 2 and h == 0:
                        W(sc, f"avdone_{qt-2}_1")
                    E(lambda qt=qt, h=h: sc.copy(
                        out=BA(B_QS[qt % 2] + 512 * h, 512), in_=PB[3][:, :]), "act", 1)
                    MS(f"qev_{qt}_{h}", "act")
                MS(f"qproj_done_{qt}", "act")
                for h in range(2):
                    W(sc, f"qk2_{qt}_{h}")
                    E(lambda h=h: sc.activation(
                        out=FA(LTS + h, 1, p=1), in_=FA(QK2 + h, 1, p=1),
                        func=AF.Ln, bias=stat(8, 1, p=1), scale=1.0))
                    E(lambda: sc.drain())
                    E(lambda h=h: sc.activation(
                        out=FA(CVS + h, 1, p=1), in_=FA(LTS + h, 1, p=1),
                        func=AF.Exp, bias=stat(8, 1, p=1), scale=0.5))
                    E(lambda: sc.drain(), "act", 1)
                    MS(f"cq_{qt}_{h}", "act")
                for h in range(2):
                    W(sc, f"biasmm_{qt}_{h}")
                    E(lambda h=h: sc.copy(out=stat(ST_BIAS + h),
                                          in_=PB[3][:, 256 + 32 * h:257 + 32 * h]))
                    E(lambda: sc.drain(), "act", 1)
                    MS(f"bias_{qt}_{h}", "act")
                    np2 = nkt(qt) // 2
                    for pi in range(np2):
                        diag = pi >= 2 * qt
                        if pi >= 2:
                            W(sc, f"av_{qt}_{h}_{pi-2}")
                        if diag:
                            W(sc, f"sel2_{qt}_{h}_{pi}")
                            E(lambda pi=pi, h=h: sc.activation(
                                out=BA(B_PTB[pi % 2], 1024), in_=FA(W1, 1024),
                                func=AF.Exp, bias=stat(ST_BIAS + h), scale=SIG), "act", 1)
                        else:
                            W(sc, f"s2_{qt}_{h}_{pi}")
                            E(lambda pi=pi, h=h, wb=pi % 2: sc.activation(
                                out=BA(B_PTB[pi % 2], 1024),
                                in_=PW[wb][:, 0:1024],
                                func=AF.Exp, bias=stat(ST_BIAS + h), scale=SIG), "act", 1)
                        MS(f"exp_{qt}_{h}_{pi}", "act")
                        MS(f"exppt_{qt}_{h}_{pi}", "act")
                    W(sc, f"avdone_{qt}_{h}")
                    E(lambda: sc.copy(out=FA(W2, 512, p=17), in_=PB[4][0:17, :]), "act", 1)
                    MS(f"oev_{qt}_{h}", "act")

        for name in ENGS:
            dec = dict(pe=block.tensor, dve=block.vector, act=block.scalar,
                       pool=block.gpsimd, sync=block.sync)[name]
            def mk(name):
                return lambda e: env.bar(e, name, "p3")
            dec(mk(name))

    HA_F = FO["w4"] + 1536
    MPRE_F = HA_F + 128
    GS_F = FO["w4"]
    GL_F = FO["w4"] + 512
    MS_F = FO["w4"] + 1024
    OWNX_F = FO["b_bq"]

    def emit_p4(env, block, stop=None):
        E, W, MS = env.E, env.W, env.MS

        @block.sync
        def _(q):
            for b in range(2):
                if stop == "p4z":
                    break
                if b == 1:
                    W(q, "hA_pe_0")
                E(lambda b=b: q.dma_start(out=BA(B_ZTHB, 8192),
                                          in_=zth_dram.ap()[b]), "d_zt4", 16)
                MS(f"zthA_{b}", "d_zt4")
            W(q, "rs_trig")
            E(lambda: q.dma_start(out=FA(OWNX_F, 2048, p=8), in_=ar_out.ap()),
              "d_mp", 16)
            MS("mod_in", "d_mp")
            if stop in ("p4a", "p4z", "p4y"):
                return
            for b in range(2):
                if b == 1 and DBG_P4:
                    W(q, "mlpin_0_63")
                    E(lambda: q.dma_start(out=dbg_mlpin.ap(), in_=BA(B_MLPIN, 32768)), "d_bb", 16)
                    E(lambda: q.dma_start(out=dbg_hamt.ap(), in_=BA(B_HAMT, 1024, p=16)), "d_bb", 16)
                    E(lambda: q.dma_start(out=dbg_ha.ap(), in_=FA(HA_F, 128)), "d_bb", 16)
                    MS("dbg_p4_done", "d_bb")
                W(q, "hA_pe_1" if b == 0 else f"gate_pe_{b-1}")
                E(lambda b=b: q.dma_start(out=BA(B_ZTHB, 8192),
                                          in_=zth_dram.ap()[b]), "d_zt4", 16)
                MS(f"zthB_{b}", "d_zt4")
                for ts in range(4):
                    W(q, "ham_done" if b == 0 else f"yfin_st_{b-1}_{ts}")
                    E(lambda b=b, ts=ts: q.dma_start(
                        out=FA((W0, W1, W2, W3)[ts], D),
                        in_=y_part.ap()[512 * b + 128 * ts:512 * b + 128 * (ts + 1), :]),
                      f"d_yb{ts%2}", 16)
                    MS(f"ybk4_{b}_{ts}", f"d_yb{ts%2}")
                for ts in range(4):
                    for dc in range(4):
                        W(q, f"yadd4_{b}_{dc}_{ts}")
                    E(lambda b=b, ts=ts: q.dma_start(
                        out=y_out[512 * b + 128 * ts:512 * b + 128 * (ts + 1), :],
                        in_=FA((W0, W1, W2, W3)[ts], D)), "d_out", 16)
                    MS(f"yfin_st_{b}_{ts}", "d_out")
            MS("y_done", "d_out")

        @block.gpsimd
        def _(g):
            E(lambda: g.dma_start(out=BA(B_BBF, F_, p=16), in_=b_t), "d_bb", 16)
            MS("bbf", "d_bb")
            if stop in ("p4a", "p4z", "p4y"):
                return

            for b in range(2):
                for fc in range(NFC):
                    par = fc % 2
                    if (b, fc) != (0, 0) and not (b == 1 and fc == 0):
                        if fc >= 2:
                            W(g, f"mlpin_{b}_{fc-2}")
                    if b == 1 and fc < 2:
                        W(g, f"mlpin_{0}_{NFC-2+fc}")
                    if b == 1 and fc == 0:
                        W(g, "wdd_0_3")
                    q4 = fc % 4
                    E(lambda fc=fc, par=par, q4=q4: g.dma_start(
                        out=BA(B_WIN[par], 2048).rearrange("p (c f) -> p c f",
                                                           c=CH)[:, :, 0:64],
                        in_=w_in_t[fc, :, :, 0:64]), f"d_wi{q4}", 16)
                    E(lambda fc=fc, par=par, q4=q4: g.dma_start(
                        out=BA(B_WIN[par], 2048).rearrange("p (c f) -> p c f",
                                                           c=CH)[:, :, 64:128],
                        in_=w_in_t[fc, :, :, 64:128]), f"d_wu{q4}", 16)
                    E(lambda fc=fc, par=par, q4=q4: g.dma_start(
                        out=BA(B_WUP[par], 2048).rearrange("p (c f) -> p c f",
                                                           c=CH)[:, :, 0:64],
                        in_=w_up_t[fc, :, :, 0:64]), f"d_wv{q4}", 16)
                    E(lambda fc=fc, par=par, q4=q4: g.dma_start(
                        out=BA(B_WUP[par], 2048).rearrange("p (c f) -> p c f",
                                                           c=CH)[:, :, 64:128],
                        in_=w_up_t[fc, :, :, 64:128]), f"d_wx{q4}", 16)
                    MS(f"wio_{b}_{fc}", f"d_wu{q4}")
                    MS(f"wiob_{b}_{fc}", f"d_wi{q4}")
                    MS(f"wup_{b}_{fc}", f"d_wx{q4}")
                    MS(f"wupb_{b}_{fc}", f"d_wv{q4}")
                for dc in range(4):
                    for pc in range(8):
                        par = pc % 2
                        k = dc * 8 + pc
                        if k == 0:
                            W(g, f"gate_pe_{b}")
                        if k >= 2:
                            W(g, f"wdn_used_{b}_{k-2}")
                        q4 = k % 4
                        E(lambda dc=dc, pc=pc, par=par, q4=q4: g.dma_start(
                            out=BA(B_WDN[par], 4096).rearrange(
                                "p (f n) -> p f n", f=8)[:, 0:4, :],
                            in_=w_dn_t[dc, :, 8 * pc:8 * pc + 4, :]),
                          f"d_wd{q4}", 16)
                        E(lambda dc=dc, pc=pc, par=par, q4=q4: g.dma_start(
                            out=BA(B_WDN[par], 4096).rearrange(
                                "p (f n) -> p f n", f=8)[:, 4:8, :],
                            in_=w_dn_t[dc, :, 8 * pc + 4:8 * (pc + 1), :]),
                          f"d_we{q4}", 16)
                        MS(f"wdn_{b}_{k}", f"d_wd{q4}")
                        MS(f"wdnb_{b}_{k}", f"d_we{q4}")

        @block.tensor
        def _(pe):
            if stop == "p4z":
                return
            for b in range(2):
                W(pe, f"zthA_{b}")
                for tsl in range(4):
                    tt = 4 * b + tsl
                    if tt >= 1:
                        W(pe, f"hAev_{tt-1}")
                    for c in range(CH):
                        E(lambda c=c, tsl=tsl: pe.matmul(
                            PB[6][:, 0:16], BA(B_ZTHB + 512 * c + 128 * tsl, 128),
                            V_abf[:, 16 * c:16 * (c + 1)],
                            start=(c == 0), stop=(c == CH - 1)),
                          "pe" if c == CH - 1 else None, 1)
                    MS(f"hA_{tt}", "pe")
                MS(f"hA_pe_{b}", "pe")
            if stop == "p4y":
                return
            W(pe, "mod_in")
            for r in range(16):
                if r >= 2:
                    W(pe, f"mprev_{r-2}")
                E(lambda r=r: pe.transpose(PB[r % 2][:, 0:8],
                                           FA(OWNX_F + 128 * r, 128, p=8),
                                           V_ident[0:8, 0:8]), "pe", 1)
                MS(f"mptr4_{r}", "pe")
            W(pe, "hambf")
            for tt in range(8):
                if tt >= 2:
                    W(pe, f"hamtev_{tt-2}")
                E(lambda tt=tt: pe.transpose(
                    PBFS[tt % 2][0:16, 0:128],
                    BA(B_MPB + 16 * tt, 16), V_identb), "pe", 1)
                MS(f"hamtr_{tt}", "pe")
            if stop == "p4a":
                return
            for b in range(2):
                W(pe, f"zthB_{b}")
                W(pe, "ham_done")
                if b == 1:
                    for ts in range(4):
                        W(pe, f"yadd4_0_3_{ts}")
                    if DBG_P4:
                        W(pe, "dbg_p4_done")
                for fc in range(NFC):
                    par = fc % 2
                    W(pe, f"wio_{b}_{fc}")
                    W(pe, f"wiob_{b}_{fc}")
                    W(pe, f"wup_{b}_{fc}")
                    W(pe, f"wupb_{b}_{fc}")
                    if fc >= 2:
                        W(pe, f"gadd_{b}_{fc-2}")
                        W(pe, f"mlpin_{b}_{fc-2}")
                        W(pe, f"mev_{b}_{fc-2}")
                    for c in range(CH):
                        E(lambda c=c, par=par: pe.matmul(
                            PB[0 + par][:, :], BA(B_WIN[par] + 128 * c, 128),
                            BA(B_ZTHB + 512 * c, 512),
                            start=(c == 0), stop=(c == CH - 1)),
                          "pe" if c == CH - 1 else None, 1)
                    E(lambda b=b, fc=fc, par=par: pe.matmul(
                        PB[5 + par][:, :], BA(B_BBF + 128 * fc, 128, p=16),
                        BA(B_HAMT + 512 * b, 512, p=16), start=True, stop=True),
                      "pe", 1)
                    for c in range(CH):
                        E(lambda c=c, par=par: pe.matmul(
                            PB[3 + par][:, :], BA(B_WUP[par] + 128 * c, 128),
                            BA(B_ZTHB + 512 * c, 512),
                            start=(c == 0), stop=(c == CH - 1)),
                          "pe" if c == CH - 1 else None, 1)
                    MS(f"gup_{b}_{fc}", "pe")
                MS(f"gate_pe_{b}", "pe")
                for dc in range(4):
                    for pc in range(8):
                        k = dc * 8 + pc
                        par = pc % 2
                        W(pe, f"wdn_{b}_{k}")
                        W(pe, f"wdnb_{b}_{k}")
                        if k == 0:
                            W(pe, f"mlpin_{b}_{NFC-1}")
                            W(pe, f"gadd_{b}_{NFC-1}")
                            W(pe, f"gadd_{b}_{NFC-2}")
                            W(pe, f"mev_{b}_{NFC-1}")
                            W(pe, f"mev_{b}_{NFC-2}")
                        if pc == 0 and dc >= 1:
                            for ts in range(4):
                                W(pe, f"yadd4_{b}_{dc-1}_{ts}")
                        for ts in range(4):
                            for fcl in range(8):
                                fc = 8 * pc + fcl
                                E(lambda ts=ts, fcl=fcl, fc=fc, par=par: pe.matmul(
                                    PB[(0, 1, 3, 4)[ts]][:, :],
                                    BA(B_MLPIN + 512 * fc + 128 * ts, 128),
                                    BA(B_WDN[par] + 512 * fcl, 512),
                                    start=(fc == 0), stop=(fc == NFC - 1)),
                                  "pe" if fcl == 7 else None, 1)
                        MS(f"wdnp_{b}_{k}", "pe")
                        MS(f"wdn_used_{b}_{k}", "pe")
                    MS(f"wdd_{b}_{dc}", "pe")

        @block.vector
        def _(v):
            if stop == "p4z":
                return
            for b in range(2):
                for tsl in range(4):
                    tt = 4 * b + tsl
                    W(v, f"hA_{tt}")
                    E(lambda tt=tt: v.tensor_copy(out=FA(HA_F + 16 * tt, 16),
                                                  in_=PB[6][:, 0:16]), "dve", 1)
                    MS(f"hAev_{tt}", "dve")
            if stop == "p4y":
                return
            for r in range(16):
                W(v, f"mptr4_{r}")
                E(lambda r=r: v.tensor_copy(
                    out=AR_F[:, MPRE_F + r:MPRE_F + 128:16],
                    in_=PB[r % 2][:, 0:8]), "dve", 1)
                MS(f"mprev_{r}", "dve")
            W(v, "tanh_done")
            E(lambda: v.scalar_tensor_tensor(out=FA(MPRE_F, 128), in0=FA(FO["w3"], 128),
                                             scalar=0.1, in1=FA(HA_F, 128),
                                             op0=ALU.mult, op1=ALU.mult))
            E(lambda: v.tensor_copy(out=BA(B_MPB, 128), in_=FA(MPRE_F, 128)))
            E(lambda: v.drain(), "dve", 1)
            MS("hambf", "dve")
            for tt in range(8):
                W(v, f"hamtr_{tt}")
                E(lambda tt=tt: v.tensor_copy(
                    out=BA(B_HAMT + 128 * tt, 128, p=16),
                    in_=PBFS[tt % 2][0:16, 0:128]), "dve", 1)
                MS(f"hamtev_{tt}", "dve")
            MS("ham_done", "dve")
            if stop == "p4a":
                return
            for b in range(2):
                for fc in range(NFC):
                    par = fc % 2
                    W(v, f"gup_{b}_{fc}")
                    W(v, f"mev_{b}_{fc}")
                    if fc >= 1:
                        W(v, f"silu_{b}_{fc-1}")
                    elif b == 1:
                        W(v, f"silu_0_{NFC-1}")
                    E(lambda par=par: v.tensor_add(out=FA(GS_F, 512),
                                                   in0=PB[0 + par][:, :],
                                                   in1=FA(MS_F, 512)), "dve", 1)
                    MS(f"gadd_{b}_{fc}", "dve")
                    W(v, f"silu_{b}_{fc}")
                    E(lambda fc=fc, par=par: v.tensor_mul(
                        out=BA(B_MLPIN + 512 * fc, 512), in0=PB[3 + par][:, :],
                        in1=FA(GL_F, 512)), "dve", 1)
                    MS(f"mlpin_{b}_{fc}", "dve")
                for dc in range(4):
                    for ts in range(4):
                        W(v, f"wdd_{b}_{dc}") if ts == 0 else None
                        W(v, f"ybk4_{b}_{ts}")
                        E(lambda b=b, dc=dc, ts=ts: v.tensor_add(
                            out=FA((W0, W1, W2, W3)[ts] + 512 * dc, 512),
                            in0=PB[(0, 1, 3, 4)[ts]][:, :],
                            in1=FA((W0, W1, W2, W3)[ts] + 512 * dc, 512)), "dve", 1)
                        MS(f"yadd4_{b}_{dc}_{ts}", "dve")

        @block.scalar
        def _(sc):
            if stop in ("p4z", "p4y"):
                return
            W(sc, "mprev_15")
            E(lambda: sc.activation(out=FA(FO["w3"], 128), in_=FA(MPRE_F, 128),
                                    func=AF.Tanh, bias=stat(8), scale=1.0))
            E(lambda: sc.drain(), "act", 1)
            MS("tanh_done", "act")
            if stop == "p4a":
                return
            for b in range(2):
                for fc in range(NFC):
                    par = fc % 2
                    W(sc, f"gup_{b}_{fc}")
                    E(lambda par=par: sc.copy(out=FA(MS_F, 512), in_=PB[5 + par][:, :]), "act", 1)
                    MS(f"mev_{b}_{fc}", "act")
                    W(sc, f"gadd_{b}_{fc}")
                    E(lambda: sc.activation(out=FA(GL_F, 512), in_=FA(GS_F, 512),
                                            func=AF.Silu, bias=stat(8), scale=1.0), "act", 1)
                    MS(f"silu_{b}_{fc}", "act")

    def emit_p2(env, block):
        E, W, MS = env.E, env.W, env.MS
        ST_KS = 16
        ST_SD4 = 20
        ST_R4 = 24
        ST_NEGM = 28
        ST_Z = 29
        ST_RZ = 30

        def rope_dve(v, src_off, cs_off, dst_off, n4):
            se = AR_F[:, src_off:src_off + 512:2]
            so = AR_F[:, src_off + 1:src_off + 512:2]
            de = AR_F[:, dst_off:dst_off + 512:2]
            do = AR_F[:, dst_off + 1:dst_off + 512:2]
            cc_ = FA(cs_off, 256)
            ss_ = FA(cs_off + 256, 256)
            t1 = FA(W3, 256)
            t2 = FA(W3 + 256, 256)
            E(lambda: v.tensor_mul(out=t1, in0=se, in1=cc_))
            E(lambda: v.tensor_mul(out=t2, in0=so, in1=ss_))
            E(lambda: v.tensor_sub(out=de, in0=t1, in1=t2))
            E(lambda: v.tensor_mul(out=t1, in0=se, in1=ss_))
            E(lambda: v.tensor_mul(out=t2, in0=so, in1=cc_))
            E(lambda: v.tensor_add(out=do, in0=t1, in1=t2))

        @block.gpsimd
        def _(g):
            W(g, "ag_in_done")
            E(lambda: g.collective_compute(
                "AllGather", ALU.bypass, replica_groups=[list(range(NCORE))],
                ins=[ag_in.ap().opt()], outs=[ag_out.opt()]), "cc1", 1)
            MS("ag_trig", "cc1")
            E(lambda: g.dma_start(out=BA(B_WKS, 8192).rearrange("p (c o) -> p c o", c=CH),
                                  in_=wk_sw), "d_sw", 16)
            E(lambda: g.dma_start(out=BA(B_WVS, 8192).rearrange("p (c o) -> p c o", c=CH),
                                  in_=wv_sw), "d_sw", 16)
            MS("kvw", "d_sw")
            for hg in range(4):
                if hg >= 1:
                    W(g, f"qpe_all_{hg-1}")
                else:
                    W(g, "p2kv_pe_done")
                E(lambda hg=hg: g.dma_start(
                    out=BA(B_WQS, 8192).rearrange("p (c o) -> p c o", c=CH),
                    in_=wq_sw[hg]), "d_sw", 16)
                MS(f"wq_{hg}", "d_sw")
            W(g, "attn_all")
            if upto == "p2" and DBG_P2:
                E(lambda: g.dma_start(out=dbg_heads.ap(), in_=BA(B_HEADS, 16384)), "d_out", 16)
                E(lambda: g.dma_start(out=dbg_qfm.ap(), in_=BA(B_QFM, 16384)), "d_out", 16)
                E(lambda: g.dma_start(out=dbg_kfm.ap(), in_=BA(B_KFM, 4608)), "d_out", 16)
                E(lambda: g.dma_start(out=dbg_vsb.ap(), in_=BA(B_VSB, 4608)), "d_out", 16)
                MS("dbgp2_done", "d_out")
                W(g, "dbgp2_done")
            E(lambda: g.dma_start(
                out=BA(B_WOL, 16384).rearrange("p (h n) -> p h n", h=8),
                in_=wo_sw[0:8].rearrange("h p a o -> p h (a o)")), "d_wo", 16)
            E(lambda: g.dma_start(
                out=BA(B_WOH, 16384).rearrange("p (h n) -> p h n", h=8),
                in_=wo_sw[8:16].rearrange("h p a o -> p h (a o)")), "d_wo", 16)
            MS("wo_ready", "d_wo")

        @block.sync
        def _(q):
            for i in range(NTH):
                par = i % 2
                if i >= 2:
                    W(q, f"xnT_{i-2}")
                    W(q, f"krope_{i-2}")
                E(lambda i=i, par=par: q.dma_start(
                    out=BA(B_XN2[par], D), in_=xn_dram.ap()[128 * i:128 * (i + 1), :]),
                  f"d_xr{par}", 16)
                MS(f"xr_{i}", f"d_xr{par}")
                E(lambda i=i, par=par: q.dma_start(
                    out=FA(CS[par], 256), in_=cos_t[128 * i:128 * (i + 1), :]),
                  f"d_cs{par}", 16)
                E(lambda i=i, par=par: q.dma_start(
                    out=FA(CS[par] + 256, 256), in_=sin_t[128 * i:128 * (i + 1), :]),
                  f"d_cs{par}", 16)
                MS(f"cs_{i}", f"d_cs{par}")


        pbfu = {0: None, 1: None}

        def pbf_tr(pe, idx, msname, f):
            if pbfu[idx]:
                W(pe, pbfu[idx])
            E(f, "pe", 1)
            MS(msname + "_p", "pe")

        @block.tensor
        def _(pe):
            for i in range(NTH):
                par = i % 2
                W(pe, f"xr_{i}")
                for c in range(CH):
                    pbf_tr(pe, c % 2, f"xtr_{i}_{c}",
                           lambda c=c, par=par: pe.transpose(
                               PBFS[c % 2][:, 0:128],
                               BA(B_XN2[par] + 128 * c, 128), V_identb))
                    pbfu[c % 2] = f"xtr_{i}_{c}_e"
                W(pe, "kvw")
                W(pe, f"xnT_{i}")
                if i >= 1:
                    W(pe, f"kc_{i-1}")
                    W(pe, f"vc_{i-1}")
                for c in range(CH):
                    E(lambda i=i, c=c: pe.matmul(
                        PB[0][:, :], BA(B_XNT + 1152 * c + 128 * i, 128),
                        BA(B_WKS + 512 * c, 512), start=(c == 0), stop=(c == CH - 1)),
                      "pe" if c == CH - 1 else None, 1)
                MS(f"kpe_{i}", "pe")
                for c in range(CH):
                    E(lambda i=i, c=c: pe.matmul(
                        PB[1][:, :], BA(B_XNT + 1152 * c + 128 * i, 128),
                        BA(B_WVS + 512 * c, 512), start=(c == 0), stop=(c == CH - 1)),
                      "pe" if c == CH - 1 else None, 1)
                MS(f"vpe_{i}", "pe")
                if i >= 1:
                    W(pe, f"krope_{i-1}")
                    for hv in range(4):
                        pbf_tr(pe, hv % 2, f"ktr_{i-1}_{hv}",
                               lambda hv=hv, p2=(i - 1) % 2: pe.transpose(
                                   PBFS[hv % 2][:, 0:128],
                                   BA(B_KB[p2] + 128 * hv, 128), V_identb))
                        pbfu[hv % 2] = f"ktr_{i-1}_{hv}_e"
            W(pe, f"krope_{NTH-1}")
            for hv in range(4):
                pbf_tr(pe, hv % 2, f"ktr_{NTH-1}_{hv}",
                       lambda hv=hv, p2=(NTH - 1) % 2: pe.transpose(
                           PBFS[hv % 2][:, 0:128],
                           BA(B_KB[p2] + 128 * hv, 128), V_identb))
                pbfu[hv % 2] = f"ktr_{NTH-1}_{hv}_e"
            MS("p2kv_pe_done", "pe")

        @block.vector
        def _(v):
            for i in range(NTH):
                par = i % 2
                W(v, f"kpe_{i}")
                E(lambda: v.tensor_add(out=FA(W1, 512), in0=PB[0][:, :], in1=V_bbk), "dve", 1)
                MS(f"kc_{i}", "dve")
                W(v, f"ksd_{i}")
                if i >= 2:
                    W(v, f"ktr_{i-2}")
                if i >= 1:
                    W(v, f"vsb_{i-1}")
                E(lambda: v.reciprocal(out=stat(ST_R4, 4), in_=stat(ST_SD4, 4)))
                E(lambda: v.drain())
                for hv in range(4):
                    E(lambda hv=hv: v.tensor_scalar_mul(
                        out=FA(W2 + 128 * hv, 128), in0=FA(W1 + 128 * hv, 128),
                        scalar1=stat(ST_R4 + hv)))
                E(lambda: v.tensor_mul(out=FA(W2, 512), in0=FA(W2, 512), in1=V_kkn))
                W(v, f"cs_{i}")
                rope_dve(v, W2, CS[par], W1, 4)
                E(lambda par=par: v.tensor_copy(out=BA(B_KB[par], 512), in_=FA(W1, 512)), "dve", 1)
                MS(f"krope_{i}", "dve")
                W(v, f"vpe_{i}")
                E(lambda: v.tensor_add(out=FA(W2, 512), in0=PB[1][:, :], in1=V_bbv), "dve", 1)
                MS(f"vc_{i}", "dve")

        @block.scalar
        def _(sc):
            for i in range(NTH):
                for c in range(CH):
                    W(sc, f"xtr_{i}_{c}_p")
                    E(lambda i=i, c=c: sc.copy(
                        out=BA(B_XNT + 1152 * c + 128 * i, 128),
                        in_=PBFS[c % 2][:, 0:128]), "act", 1)
                    MS(f"xtr_{i}_{c}_e", "act")
                MS(f"xnT_{i}", "act")
                W(sc, f"kc_{i}")
                for hv in range(4):
                    E(lambda hv=hv: sc.activation(
                        out=FA(W3, 128), in_=FA(W1 + 128 * hv, 128), func=AF.Square,
                        bias=stat(8), scale=1.0, accum_out=stat(ST_KS + hv)))
                E(lambda: sc.activation(out=stat(ST_SD4, 4), in_=stat(ST_KS, 4),
                                        func=AF.Sqrt, bias=stat(11), scale=1.0 / 128))
                E(lambda: sc.drain(), "act", 1)
                MS(f"ksd_{i}", "act")
                W(sc, f"vc_{i}")
                E(lambda i=i: sc.copy(out=BA(B_VSB + 512 * i, 512), in_=FA(W2, 512)), "act", 1)
                MS(f"vsb_{i}", "act")
                if i >= 1:
                    for hv in range(4):
                        W(sc, f"ktr_{i-1}_{hv}_p")
                        E(lambda i=i, hv=hv: sc.copy(
                            out=BA(B_KFM + 1152 * hv + 128 * (i - 1), 128),
                            in_=PBFS[hv % 2][:, 0:128]), "act", 1)
                        MS(f"ktr_{i-1}_{hv}_e", "act")
                    MS(f"ktr_{i-1}", "act")
            for hv in range(4):
                W(sc, f"ktr_{NTH-1}_{hv}_p")
                E(lambda hv=hv: sc.copy(
                    out=BA(B_KFM + 1152 * hv + 128 * (NTH - 1), 128),
                    in_=PBFS[hv % 2][:, 0:128]), "act", 1)
                MS(f"ktr_{NTH-1}_{hv}_e", "act")
            MS(f"ktr_{NTH-1}", "act")

        @block.sync
        def _(q):
            for hg in range(4):
                for ti, i in enumerate(range(1, NTH)):
                    k = hg * 8 + ti
                    par = k % 2
                    if k >= 2:
                        W(q, f"qrope_{k-2}")
                    else:
                        W(q, f"krope_{NTH-1}")
                    E(lambda i=i, par=par: q.dma_start(
                        out=FA(CS[par], 256), in_=cos_t[128 * i:128 * (i + 1), :]),
                      f"d_cs{par}", 16)
                    E(lambda i=i, par=par: q.dma_start(
                        out=FA(CS[par] + 256, 256), in_=sin_t[128 * i:128 * (i + 1), :]),
                      f"d_cs{par}", 16)
                    MS(f"qcs_{k}", f"d_cs{par}")

        @block.tensor
        def _(pe):
            for hg in range(4):
                W(pe, f"wq_{hg}")
                for ti, i in enumerate(range(1, NTH)):
                    k = hg * 8 + ti
                    if k >= 1:
                        W(pe, f"qc_{k-1}")
                    for c in range(CH):
                        E(lambda i=i, c=c: pe.matmul(
                            PB[3][:, :], BA(B_XNT + 1152 * c + 128 * i, 128),
                            BA(B_WQS + 512 * c, 512), start=(c == 0), stop=(c == CH - 1)),
                          "pe" if c == CH - 1 else None, 1)
                    MS(f"qpe_{k}", "pe")
                    if k >= 1:
                        W(pe, f"qrope_{k-1}")
                        for hh in range(4):
                            pbf_tr(pe, hh % 2, f"qtr_{k-1}_{hh}",
                                   lambda hh=hh, p2=(k - 1) % 2: pe.transpose(
                                       PBFS[hh % 2][:, 0:128],
                                       BA(B_QB[p2] + 128 * hh, 128), V_identb))
                            pbfu[hh % 2] = f"qtr_{k-1}_{hh}_e"
                MS(f"qpe_all_{hg}", "pe")
            W(pe, "qrope_31")
            for hh in range(4):
                pbf_tr(pe, hh % 2, f"qtr_31_{hh}",
                       lambda hh=hh: pe.transpose(
                           PBFS[hh % 2][:, 0:128],
                           BA(B_QB[31 % 2] + 128 * hh, 128), V_identb))
                pbfu[hh % 2] = f"qtr_31_{hh}_e"

        @block.vector
        def _(v):
            for hg in range(4):
                for ti, i in enumerate(range(1, NTH)):
                    k = hg * 8 + ti
                    par = k % 2
                    W(v, f"qpe_{k}")
                    E(lambda hg=hg: v.tensor_add(out=FA(W0, 512), in0=PB[3][:, :],
                                                 in1=V_bbq[:, 512 * hg:512 * (hg + 1)]), "dve", 1)
                    MS(f"qc_{k}", "dve")
                    W(v, f"qsd_{k}")
                    if k >= 2:
                        W(v, f"qtr_{k-2}")
                    E(lambda: v.reciprocal(out=stat(ST_R4, 4), in_=stat(ST_SD4, 4)))
                    E(lambda: v.drain())
                    for hh in range(4):
                        E(lambda hh=hh: v.tensor_scalar_mul(
                            out=FA(W2 + 128 * hh, 128), in0=FA(W0 + 128 * hh, 128),
                            scalar1=stat(ST_R4 + hh)))
                    E(lambda: v.tensor_mul(out=FA(W2, 512), in0=FA(W2, 512), in1=V_qqn))
                    W(v, f"qcs_{k}")
                    rope_dve(v, W2, CS[par], W0, 4)
                    E(lambda: v.tensor_scalar_mul(out=FA(W0, 512), in0=FA(W0, 512),
                                                  scalar1=V_scale))
                    E(lambda par=par: v.tensor_copy(out=BA(B_QB[par], 512), in_=FA(W0, 512)), "dve", 1)
                    MS(f"qrope_{k}", "dve")

        @block.scalar
        def _(sc):
            for hg in range(4):
                for ti in range(8):
                    k = hg * 8 + ti
                    W(sc, f"qc_{k}")
                    for hh in range(4):
                        E(lambda hh=hh: sc.activation(
                            out=FA(W3, 128), in_=FA(W0 + 128 * hh, 128), func=AF.Square,
                            bias=stat(8), scale=1.0, accum_out=stat(ST_KS + hh)))
                    E(lambda: sc.activation(out=stat(ST_SD4, 4), in_=stat(ST_KS, 4),
                                            func=AF.Sqrt, bias=stat(11), scale=1.0 / 128))
                    E(lambda: sc.drain(), "act", 1)
                    MS(f"qsd_{k}", "act")
                    if k >= 1:
                        kk = k - 1
                        hgp, tip = kk // 8, kk % 8
                        for hh in range(4):
                            W(sc, f"qtr_{kk}_{hh}_p")
                            E(lambda hgp=hgp, hh=hh, tip=tip: sc.copy(
                                out=BA(B_QFM + 1024 * (4 * hgp + hh) + 128 * tip, 128),
                                in_=PBFS[hh % 2][:, 0:128]), "act", 1)
                            MS(f"qtr_{kk}_{hh}_e", "act")
                        MS(f"qtr_{kk}", "act")
            kk = 31
            for hh in range(4):
                W(sc, f"qtr_{kk}_{hh}_p")
                E(lambda hh=hh: sc.copy(
                    out=BA(B_QFM + 1024 * (4 * 3 + hh) + 128 * 7, 128),
                    in_=PBFS[hh % 2][:, 0:128]), "act", 1)
                MS(f"qtr_{kk}_{hh}_e", "act")
            MS(f"qtr_{kk}", "act")

        @block.tensor
        def _(pe):
            first = True
            for h in range(16):
                hv = h // 4
                for qt in range(8):
                    k = h * 8 + qt
                    if first:
                        W(pe, "qtr_31")
                        W(pe, "qtr_30")
                        W(pe, "ktr_8")
                        W(pe, "ktr_7")
                        W(pe, "vsb_8")
                        first = False
                    par = k % 2
                    if k >= 2:
                        W(pe, f"sel_{k-2}")
                    E(lambda h=h, qt=qt, hv=hv, par=par: pe.matmul(
                        PB[3 + par][:, 0:256], BA(B_QFM + 1024 * h + 128 * qt, 128),
                        BA(B_KFM + 1152 * hv + 128 * qt, 256),
                        start=True, stop=True), "pe", 1)
                    MS(f"s_{k}", "pe")
                    W(pe, f"pn_{k}")
                    if k >= 2:
                        W(pe, f"pt_{k-2}")
                    E(lambda par=par: pe.transpose(PBFS[par][:, 0:128], BA(B_PN, 128), V_identb))
                    E(lambda par=par: pe.transpose(PBFS[par][:, 128:256], BA(B_PN + 128, 128), V_identb),
                      "pe", 1)
                    MS(f"ptr_{k}", "pe")
                    W(pe, f"pt_{k}")
                    if k >= 2:
                        W(pe, f"hv_{k-2}")
                    E(lambda qt=qt, hv=hv: pe.matmul(
                        PB[5 + par][:, 0:128], BA(B_VSB + 512 * qt + 128 * hv, 128),
                        BA(B_PT, 128), start=True, stop=False))
                    E(lambda qt=qt, hv=hv: pe.matmul(
                        PB[5 + par][:, 0:128], BA(B_VSB + 512 * (qt + 1) + 128 * hv, 128),
                        BA(B_PT + 128, 128), start=False, stop=True), "pe", 1)
                    MS(f"av_{k}", "pe")

        @block.vector
        def _(v):
            for h in range(16):
                for qt in range(8):
                    k = h * 8 + qt
                    par = k % 2
                    W(v, f"s_{k}")
                    E(lambda par=par: v.tensor_copy(out=FA(W2, 256), in_=PB[3 + par][:, 0:256]))
                    moff = 0 if qt == 0 else 256
                    E(lambda moff=moff: v.copy_predicated(
                        out=FA(W2, 256), mask=UA(UO["mask_swa"] + moff, 256),
                        data=V_fillswa))
                    E(lambda: v.reduce_max(out=stat(ST_NEGM), in_=FA(W2, 256),
                                           axis=AX.X, negate=True))
                    E(lambda: v.drain(), "dve", 1)
                    MS(f"sel_{k}", "dve")
                    W(v, f"pf_{k}")
                    E(lambda: v.reciprocal(out=stat(ST_RZ), in_=stat(ST_Z)))
                    E(lambda: v.drain())
                    E(lambda: v.tensor_scalar_mul(out=BA(B_PN, 256), in0=FA(W3, 256),
                                                  scalar1=stat(ST_RZ)), "dve", 1)
                    MS(f"pn_{k}", "dve")
                    W(v, f"ptr_{k}")
                    if k >= 1:
                        W(v, f"av_{k-1}")
                    E(lambda par=par: v.tensor_copy(out=BA(B_PT, 256), in_=PBFS[par][:, 0:256]), "dve", 1)
                    MS(f"pt_{k}", "dve")

        @block.scalar
        def _(sc):
            for h in range(16):
                for qt in range(8):
                    k = h * 8 + qt
                    par = k % 2
                    W(sc, f"sel_{k}")
                    E(lambda: sc.activation(out=FA(W3, 256), in_=FA(W2, 256), func=AF.Exp,
                                            bias=stat(ST_NEGM), scale=1.0,
                                            accum_out=stat(ST_Z)))
                    E(lambda: sc.drain(), "act", 1)
                    MS(f"pf_{k}", "act")
                    W(sc, f"av_{k}")
                    E(lambda h=h, qt=qt, par=par: sc.copy(
                        out=BA(B_HEADS + 1024 * h + 128 * qt, 128),
                        in_=PB[5 + par][:, 0:128]), "act", 1)
                    MS(f"hv_{k}", "act")
            MS("attn_all", "act")

        @block.sync
        def _(q):
            for tt in range(8):
                par = tt % 2
                if tt >= 2:
                    W(q, f"yadd_{tt-2}_3")
                E(lambda tt=tt: q.dma_start(
                    out=FA(W4, D), in_=y_part.ap()[128 * tt:128 * (tt + 1), :]),
                  f"d_yb{par}", 16)
                MS(f"ywb_{tt}", f"d_yb{par}")
                for oc in range(4):
                    W(q, f"yadd_{tt}_{oc}")
                E(lambda tt=tt: q.dma_start(
                    out=y_part.ap()[128 * tt:128 * (tt + 1), :], in_=FA(W4, D)), "d_yw", 16)
                if upto == "p2":
                    E(lambda tt=tt: q.dma_start(
                        out=y_out[128 * tt:128 * (tt + 1), :], in_=FA(W4, D)), "d_yw", 16)
                MS(f"yso_{tt}", "d_yw")

        @block.tensor
        def _(pe):
            W(pe, "wo_ready")
            W(pe, "attn_all")
            for tt in range(8):
                for oc in range(4):
                    kk = tt * 4 + oc
                    if kk >= 2:
                        W(pe, f"yadd_{(kk-2)//4}_{(kk-2)%4}")
                    for h in range(16):
                        base = B_WOL if h < 8 else B_WOH
                        E(lambda tt=tt, oc=oc, h=h, base=base, kk=kk: pe.matmul(
                            PB[0 + kk % 2][:, :],
                            BA(B_HEADS + 1024 * h + 128 * tt, 128),
                            BA(base + 2048 * (h % 8) + 512 * oc, 512),
                            start=(h == 0), stop=(h == 15)),
                          "pe" if h == 15 else None, 1)
                    MS(f"wop_{tt}_{oc}", "pe")

        @block.vector
        def _(v):
            for tt in range(8):
                W(v, f"ywb_{tt}")
                for oc in range(4):
                    kk = tt * 4 + oc
                    W(v, f"wop_{tt}_{oc}")
                    E(lambda oc=oc, kk=kk: v.tensor_add(
                        out=FA(W4 + 512 * oc, 512), in0=PB[0 + kk % 2][:, :],
                        in1=FA(W4 + 512 * oc, 512)))
                    E(lambda: v.drain(), "dve", 1)
                    MS(f"yadd_{tt}_{oc}", "dve")

        if upto == "p2" and DBG_P2:
            @block.sync
            def _(q):
                W(q, f"ktr_{NTH-1}")
                E(lambda: q.dma_start(out=dbg_kfm_a.ap(), in_=BA(B_KFM, 4608)), "d_c", 16)
                W(q, "qtr_31")
                E(lambda: q.dma_start(out=dbg_kfm_b.ap(), in_=BA(B_KFM, 4608)), "d_c", 16)

        for name in ENGS:
            dec = dict(pe=block.tensor, dve=block.vector, act=block.scalar,
                       pool=block.gpsimd, sync=block.sync)[name]
            def mk(name):
                return lambda e: env.bar(e, name, "p2")
            dec(mk(name))

    env_dry = Env(True)
    emit(env_dry, FakeBlock())
    env = Env(False, sems=sems, milestones=env_dry.ms)
    with nc.Block() as block:
        emit(env, block)
    assert env.ctr == env_dry.ctr
    ctx.close()
    return nc, d_in


def host_inputs(inp):
    f32 = np.float32
    x = np.asarray(inp["L0_attn_x"]).reshape(8192, D)
    cores = []
    t_glob = np.arange(-128, 8192 + 0, dtype=np.int64)
    inv = (1.0 / 10000.0 ** (np.arange(0, 128, 2, dtype=f32) / 128.0)).astype(f32)

    identf = np.eye(128, dtype=f32)
    identb = np.eye(128, dtype=ml_dtypes.bfloat16)
    onesr = np.ones((1, 128), dtype=f32)
    ln1w = np.ascontiguousarray(np.asarray(inp["L0_ffn_ln1_w"], f32).reshape(CH, 128).T)
    ln2w = np.ascontiguousarray(np.asarray(inp["L0_ffn_ln2_w"], f32).reshape(CH, 128).T)
    scale_bc = np.broadcast_to(np.asarray(inp["L0_attn_scale"], f32).reshape(1, 1), (128, 1)).copy()
    b_bq = np.broadcast_to(np.asarray(inp["L0_attn_bq"], f32)[None, :], (128, D)).copy()
    b_bk = np.broadcast_to(np.asarray(inp["L0_attn_bk"], f32)[None, :], (128, 512)).copy()
    b_bv = np.broadcast_to(np.asarray(inp["L0_attn_bv"], f32)[None, :], (128, 512)).copy()
    b_bo = np.broadcast_to(np.asarray(inp["L0_attn_bo"], f32)[None, :], (128, D)).copy()
    qqn = np.broadcast_to(np.tile(np.asarray(inp["L0_attn_qn_w"], f32), 4)[None, :], (128, 512)).copy()
    kkn = np.broadcast_to(np.tile(np.asarray(inp["L0_attn_kn_w"], f32), 4)[None, :], (128, 512)).copy()

    ii = np.arange(128)[:, None]
    jj = np.arange(256)[None, :]
    m_std = ~((jj >= ii + 64) & (jj <= ii + 128))
    m_first0 = ~((jj >= np.maximum(ii + 64, 128)) & (jj <= ii + 128))
    m_p1 = np.zeros((4, 128, 512), np.uint8)
    m_p2 = np.zeros((4, 128, 512), np.uint8)
    qq = np.arange(128)[:, None]
    kk = np.arange(512)[None, :]
    for pat in range(4):
        m_p1[pat] = (kk > qq + 128 * pat).astype(np.uint8)
    kk2 = np.arange(128)[:, None]
    qq2 = np.arange(512)[None, :]
    for pat in range(4):
        m_p2[pat] = (128 * pat + kk2 > qq2).astype(np.uint8)
    m_p1 = m_p1.reshape(4 * 128, 512)
    m_p2 = m_p2.reshape(4 * 128, 512)

    Wq_a = np.asarray(inp["L0_attn_Wq"], f32)
    Wk_a = np.asarray(inp["L0_attn_Wk"], f32)
    Wv_a = np.asarray(inp["L0_attn_Wv"], f32)
    Wo_a = np.asarray(inp["L0_attn_Wo"], f32)
    wq_sw = np.ascontiguousarray(
        Wq_a.reshape(4, 512, CH, 128).transpose(0, 3, 2, 1))
    wk_sw = np.ascontiguousarray(Wk_a.reshape(512, CH, 128).transpose(2, 1, 0))
    wv_sw = np.ascontiguousarray(Wv_a.reshape(512, CH, 128).transpose(2, 1, 0))
    wo_sw = np.ascontiguousarray(
        Wo_a.reshape(4, 512, 16, 128).transpose(2, 3, 0, 1))

    Wq_f = np.asarray(inp["L0_ffn_Wq"], f32)
    Wk_f = np.asarray(inp["L0_ffn_Wk"], f32)
    Wv_f = np.asarray(inp["L0_ffn_Wv"], f32)
    Wo_f = np.asarray(inp["L0_ffn_Wo"], f32)
    A_m = np.asarray(inp["L0_ffn_A"], f32)
    B_m = np.asarray(inp["L0_ffn_B"], f32)
    W_in = np.asarray(inp["L0_ffn_W_in_base"], f32)
    W_up = np.asarray(inp["L0_ffn_W_up_base"], f32)
    W_dn = np.asarray(inp["L0_ffn_W_down"], f32)

    a_t = np.ascontiguousarray(A_m.reshape(CH, 128, 16).transpose(1, 0, 2))
    b_t = np.ascontiguousarray(B_m)
    w_in_t = np.ascontiguousarray(
        W_in.reshape(NFC, 128, CH, 128).transpose(0, 3, 2, 1))
    w_up_t = np.ascontiguousarray(
        W_up.reshape(NFC, 128, CH, 128).transpose(0, 3, 2, 1))
    w_dn_t = np.ascontiguousarray(
        W_dn.reshape(4, 512, NFC, 128).transpose(0, 3, 2, 1))
    shared = dict(
        m_p1=m_p1, m_p2=m_p2, identf=identf, identb=identb, onesr=onesr,
        ln1w=ln1w, ln2w=ln2w, scale_bc=scale_bc, b_bq=b_bq, b_bk=b_bk,
        b_bv=b_bv, b_bo=b_bo, qqn=qqn, kkn=kkn,
        wq_sw=wq_sw, wk_sw=wk_sw, wv_sw=wv_sw, wo_sw=wo_sw,
        a_t=a_t, b_t=b_t, w_in_t=w_in_t, w_up_t=w_up_t, w_dn_t=w_dn_t,
    )

    for c in range(NCORE):
        m = dict(shared)
        lo = 1024 * c - 128
        if c == 0:
            xs = np.concatenate([np.zeros((128, D), f32), x[0:1024]], axis=0)
        else:
            xs = x[lo:lo + NTH * 128]
        m["x_sl"] = np.ascontiguousarray(xs)
        pos = np.arange(lo, lo + NTH * 128, dtype=f32)
        fr = pos[:, None] * inv[None, :]
        m["cos_t"] = np.ascontiguousarray(np.tile(np.cos(fr), (1, 4)).astype(f32))
        m["sin_t"] = np.ascontiguousarray(np.tile(np.sin(fr), (1, 4)).astype(f32))
        msk0 = m_first0 if c == 0 else m_std
        m["m_swa"] = np.concatenate([msk0, m_std], axis=0).astype(np.uint8)
        sl = slice(256 * c, 256 * (c + 1))
        m["wq_ff"] = np.ascontiguousarray(Wq_f[sl].reshape(256, CH, 128).transpose(2, 1, 0))
        m["wk_ff"] = np.ascontiguousarray(Wk_f[sl].reshape(256, CH, 128).transpose(2, 1, 0))
        m["wv_ff"] = np.ascontiguousarray(Wv_f[sl].reshape(256, CH, 128).transpose(2, 1, 0))
        m["wo_ff"] = np.ascontiguousarray(Wo_f[:, sl].reshape(CH, 128, 256).transpose(1, 0, 2))
        cores.append(m)
    return cores


def kernel(**inp):
    cores = host_inputs(inp)
    nc, _ = build_nc(upto="full")
    res = run_bass_kernel_spmd(nc, cores, core_ids=list(range(NCORE)))
    y = np.concatenate([res.results[c]["y"] for c in range(NCORE)], axis=0)
    return y.reshape(1, 8192, D)


# version 10
# speedup vs baseline: 1.1964x; 1.1964x over previous
import numpy as np
import ml_dtypes
import concourse.bass as bass
import concourse.mybir as mybir
from concourse.bass_utils import run_bass_kernel_spmd
from contextlib import ExitStack

F32 = mybir.dt.float32
BF16 = mybir.dt.bfloat16
U8 = mybir.dt.uint8
AF = mybir.ActivationFunctionType
AX = mybir.AxisListType
ALU = mybir.AluOpType

NCORE = 8
D = 2048
TOK = 1024
NTH = 9
NT = 8
CH = 16
F_ = 8192
NFC = 64
SIG = float(1.0 / np.sqrt(128.0))
LN_EPS = 1e-5
RMS2_EPS = 1e-6
QK_EPS = 1e-8
SWA_FILL = -10000.0
MIN_FILL = -1.0e38

ENGS = ("pe", "dve", "act", "pool", "sync")


class Env:

    def __init__(self, dry, sems=None, milestones=None):
        self.dry = dry
        self.ctr = {}
        self.sems = sems or {}
        self.ms = {} if milestones is None else milestones

    def E(self, f, sem=None, k=1):
        if sem is not None:
            self.ctr[sem] = self.ctr.get(sem, 0) + k
        if self.dry:
            return None
        inst = f()
        if sem is not None:
            inst.then_inc(self.sems[sem], k)
        return inst

    def MS(self, name, sem):
        if self.dry:
            assert name not in self.ms, f"dup milestone {name}"
            self.ms[name] = (sem, self.ctr.get(sem, 0))

    def W(self, eng, name):
        if not self.dry:
            sem, v = self.ms[name]
            if v > 0:
                eng.wait_ge(self.sems[sem], v)

    def bar(self, eng, which, tag):
        if which in ("sync", "pe"):
            self.E(lambda: eng.nop(), f"bs_{which}", 1)
        else:
            self.E(lambda: eng.drain(), f"bs_{which}", 1)
        self.MS(f"bar_{tag}_{which}", f"bs_{which}")
        for o in ENGS:
            if o != which:
                self.W(eng, f"bar_{tag}_{o}")


class FakeBlock:
    def _r(self, f):
        f(None)
    def sync(self, f): self._r(f)
    def vector(self, f): self._r(f)
    def scalar(self, f): self._r(f)
    def tensor(self, f): self._r(f)
    def gpsimd(self, f): self._r(f)


F32_TOT = 19688
BF_TOT = 61500
U8_TOT = 4608

FO = dict(
    ident=0, ln1w=128, ln2w=144, scale_bc=160, onesr=161,
    stats=292, fillswa=356, fillmin=612,
    qqn=1124, kkn=1636, b_bk=2148, b_bv=2660,
    b_bq=3172, b_bo=5220,
    w0=7268, w1=9316, w2=11364, w3=13412, w4=15460,
    cs0=17508, cs1=18020,
)
UO = dict(mask_swa=0, mask_p1=512, mask_p2=2560)
FILL2 = 18532
NEGONES = 19556
BO = dict(identb=0, a_bf=128, woap=384)

SEM_NAMES = [
    "d_c", "d_x0", "d_x1", "d_cs0", "d_cs1",
    "d_tz0", "d_tz1", "d_tx0", "d_tx1", "d_tq0", "d_tq1", "d_tk0", "d_tk1",
    "d_xs0", "d_xs1", "d_yp0", "d_yp1",
    "d_ag", "d_w0", "d_sw", "d_wo",
    "d_z0", "d_z1", "d_zs0", "d_zs1", "d_zt4",
    "d_mp", "d_bb", "d_xr0", "d_xr1",
    "d_wi0", "d_wi1", "d_wu0", "d_wu1", "d_wd0", "d_wd1",
    "d_wi2", "d_wi3", "d_wu2", "d_wu3", "d_wd2", "d_wd3",
    "d_wv0", "d_wv1", "d_wv2", "d_wv3", "d_wx0", "d_wx1", "d_wx2", "d_wx3",
    "d_we0", "d_we1", "d_we2", "d_we3",
    "d_yb0", "d_yb1", "d_yw", "d_out",
    "cc1", "cc2",
    "pe", "dve", "act", "pool",
    "bs_pe", "bs_dve", "bs_act", "bs_pool", "bs_sync",
]

B_ZT = 448
B_ZBF = (16832, 18880)
B_XNBF = (20928, 22976)
B_ZTHS = 25024
B_XNT = 448
B_KFM = 18880
B_VSB = 23488
B_WKS = 28096
B_WVS = 36288
B_XN2 = (44480, 46528)
B_KB = (48576, 49088)
B_WQS = 28096
B_QFM = 44480
B_QB = (36288, 36800)
B_PN = 28096
B_PT = 28352
B_HEADS = 448
B_WOL = 16832
B_WOH = 33216
B_KF3 = 448
B_ZTD = (16832, 25024)
B_WKF = 33216
B_WVF = 37312
B_VA = 41408
B_VT = 43584
B_WQF = 33216
B_QS = (37312, 38336)
B_PTB = (43584, 44608)
B_ZTS = 16832
B_WOFB = 448
B_MLPIN = 448
B_ZTHB = 33216
B_WIN = (41408, 43456)
B_WUP = (45504, 47552)
B_BBF = 49600
B_WDN = (41408, 45504)
B_HAMT = 57792
B_MPB = 58816


def build_nc(upto="full"):
    nc = bass.Bass("TRN2", target_bir_lowering=False)
    d_in = {}

    def IN(name, shape, dt=F32):
        d_in[name] = nc.dram_tensor(name, list(shape), dt, kind="ExternalInput").ap()
        return d_in[name]

    x_sl = IN("x_sl", (NTH * 128, D))
    cos_t = IN("cos_t", (NTH * 128, 256))
    sin_t = IN("sin_t", (NTH * 128, 256))
    m_swa = IN("m_swa", (2 * 128, 256), U8)
    m_p1 = IN("m_p1", (4 * 128, 512), U8)
    m_p2 = IN("m_p2", (4 * 128, 512), U8)
    identf_i = IN("identf", (128, 128))
    identb_i = IN("identb", (128, 128), BF16)
    onesr_i = IN("onesr", (1, 128))
    ln1w_i = IN("ln1w", (128, 16))
    ln2w_i = IN("ln2w", (128, 16))
    scale_i = IN("scale_bc", (128, 1))
    b_bq_i = IN("b_bq", (128, D))
    b_bk_i = IN("b_bk", (128, 512))
    b_bv_i = IN("b_bv", (128, 512))
    b_bo_i = IN("b_bo", (128, D))
    qqn_i = IN("qqn", (128, 512))
    kkn_i = IN("kkn", (128, 512))
    wq_sw = IN("wq_sw", (4, 128, CH, 512))
    wk_sw = IN("wk_sw", (128, CH, 512))
    wv_sw = IN("wv_sw", (128, CH, 512))
    wo_sw = IN("wo_sw", (16, 128, 4, 512))
    wq_ff = IN("wq_ff", (128, CH, 256))
    wk_ff = IN("wk_ff", (128, CH, 256))
    wv_ff = IN("wv_ff", (128, CH, 256))
    wo_ff = IN("wo_ff", (128, CH, 256))
    a_t = IN("a_t", (128, CH, 16))
    b_t = IN("b_t", (16, F_))
    w_in_t = IN("w_in_t", (NFC, 128, CH, 128))
    w_up_t = IN("w_up_t", (NFC, 128, CH, 128))
    w_dn_t = IN("w_dn_t", (4, 128, NFC, 512))

    y_out = nc.dram_tensor("y", [TOK, D], F32, kind="ExternalOutput").ap()

    DBG_P2 = False
    DBG_P4 = False
    if DBG_P4:
        dbg_mlpin = nc.dram_tensor("dbg_mlpin", [128, 32768], BF16, kind="ExternalOutput")
        dbg_hamt = nc.dram_tensor("dbg_hamt", [16, 1024], BF16, kind="ExternalOutput")
        dbg_ha = nc.dram_tensor("dbg_ha", [128, 128], F32, kind="ExternalOutput")
    dbgkind = "ExternalOutput" if upto in ("p1",) else "Internal"
    if upto == "p2" and DBG_P2:
        dbg_heads = nc.dram_tensor("dbg_heads", [128, 16384], BF16, kind="ExternalOutput")
        dbg_qfm = nc.dram_tensor("dbg_qfm", [128, 16384], BF16, kind="ExternalOutput")
        dbg_kfm = nc.dram_tensor("dbg_kfm", [128, 4608], BF16, kind="ExternalOutput")
        dbg_vsb = nc.dram_tensor("dbg_vsb", [128, 4608], BF16, kind="ExternalOutput")
        dbg_kfm_a = nc.dram_tensor("dbg_kfm_a", [128, 4608], BF16, kind="ExternalOutput")
        dbg_kfm_b = nc.dram_tensor("dbg_kfm_b", [128, 4608], BF16, kind="ExternalOutput")
    ag_in = nc.dram_tensor("ag_in", [NT * 128, D], BF16, kind=dbgkind)
    ag_out_t = nc.dram_tensor("ag_out", [NCORE * NT * 128, D], BF16, addr_space="Shared")
    ar_in = nc.dram_tensor("ar_in", [64, 2048], F32,
                           kind="ExternalOutput" if upto == "p3" else "Internal")
    ar_out = nc.dram_tensor("ar_out", [8, 2048], F32)
    y_part = nc.dram_tensor("y_part", [TOK, D], F32,
                            kind="ExternalOutput" if upto in ("p1", "p2") else "Internal")
    xn_dram = nc.dram_tensor("xn_dram", [NTH * 128, D], BF16, kind=dbgkind)
    zth_dram = nc.dram_tensor("zth_dram", [2, 128, CH * 512], BF16, kind=dbgkind)
    ag_out = ag_out_t.ap()

    ctx = ExitStack()
    AR_F = ctx.enter_context(nc.sbuf_tensor("arena_f", [128, F32_TOT], F32))
    AR_B = ctx.enter_context(nc.sbuf_tensor("arena_b", [128, BF_TOT], BF16))
    AR_U = ctx.enter_context(nc.sbuf_tensor("arena_u", [128, U8_TOT], U8))
    PBW01 = ctx.enter_context(nc.psum_tensor("pbw01", [128, 1024], F32))
    PBW56 = ctx.enter_context(nc.psum_tensor("pbw56", [128, 1024], F32))
    PB3T = ctx.enter_context(nc.psum_tensor("pb3", [128, 512], F32))
    PB4T = ctx.enter_context(nc.psum_tensor("pb4", [128, 512], F32))
    PBF = ctx.enter_context(nc.psum_tensor("pbf", [128, 1024], BF16))
    PBF2 = ctx.enter_context(nc.psum_tensor("pbf2", [128, 1024], BF16))
    PBFS = (PBF, PBF2)

    class _Half:
        def __init__(self, t, off):
            self.t, self.off = t, off
        def __getitem__(self, idx):
            ps, cs = idx
            a = cs.start or 0
            b_ = 512 if cs.stop is None else cs.stop
            return self.t[ps, self.off + a:self.off + b_]

    PB = {0: _Half(PBW01, 0), 1: _Half(PBW01, 512),
          3: _Half(PB3T, 0), 4: _Half(PB4T, 0),
          5: _Half(PBW56, 0), 6: _Half(PBW56, 512)}
    sems = {}
    for s_ in SEM_NAMES:
        sems[s_] = ctx.enter_context(nc.semaphore(s_))

    def FA(off, n, p0=0, p=128):
        return AR_F[p0:p0 + p, off:off + n]

    def BA(off, n, p0=0, p=128):
        return AR_B[p0:p0 + p, off:off + n]

    def UA(off, n):
        return AR_U[:, off:off + n]

    V_ident = FA(FO["ident"], 128)
    V_identb = BA(BO["identb"], 128)
    V_ln2w = FA(FO["ln2w"], 16)
    V_ln1w = FA(FO["ln1w"], 16)
    V_scale = FA(FO["scale_bc"], 1)
    V_onesr = FA(FO["onesr"], 128, p0=0, p=1)
    V_fillswa = FA(FO["fillswa"], 256)
    V_fillmin = FA(FO["fillmin"], 512)
    V_qqn = FA(FO["qqn"], 512)
    V_kkn = FA(FO["kkn"], 512)
    V_bbk = FA(FO["b_bk"], 512)
    V_bbv = FA(FO["b_bv"], 512)
    V_bbq = FA(FO["b_bq"], D)
    V_bbo = FA(FO["b_bo"], D)
    V_abf = BA(BO["a_bf"], 256)
    V_woap = BA(BO["woap"], 32)
    W0, W1, W2, W3, W4 = (FO["w0"], FO["w1"], FO["w2"], FO["w3"], FO["w4"])
    CS = (FO["cs0"], FO["cs1"])

    def stat(i, n=1, p=128, p0=0):
        return FA(FO["stats"] + i, n, p0=p0, p=p)

    def emit(env, block):
        E, W, MS = env.E, env.W, env.MS

        def emit_zev(sc, j):
            tt = j - 1
            for c in range(CH):
                W(sc, f"ztrp_{j}_{c}")
                E(lambda c=c, tt=tt: sc.copy(
                    out=BA(B_ZT + 2048 * tt + 128 * c, 128),
                    in_=PBFS[c % 2][:, 0:128]), "act", 1)
                MS(f"zev_{j}_{c}", "act")
            MS(f"ztr_{j}", "act")

        def sync_p1_tail(q, j):
            par = j % 2
            W(q, f"xnb_{j}")
            E(lambda j=j, par=par: q.dma_start(
                out=xn_dram.ap()[128 * j:128 * (j + 1), :],
                in_=BA(B_XNBF[par], D)), f"d_xs{par}", 16)
            MS(f"xnsp_{j}", f"d_xs{par}")
            if j >= 1:
                tt = j - 1
                W(q, f"y_{j}")
                E(lambda tt=tt: q.dma_start(
                    out=y_part.ap()[128 * tt:128 * (tt + 1), :], in_=FA(W4, D)),
                  f"d_yp{j%2}", 16)
                MS(f"yst_{j}", f"d_yp{j%2}")
                if upto == "p1":
                    E(lambda tt=tt: q.dma_start(
                        out=y_out[128 * tt:128 * (tt + 1), :], in_=FA(W4, D)), "d_out", 16)


        @block.sync
        def _(q):
            for f in [
                lambda: q.dma_start(out=V_ident, in_=identf_i),
                lambda: q.dma_start(out=V_identb, in_=identb_i),
                lambda: q.dma_start(out=V_ln1w, in_=ln1w_i),
                lambda: q.dma_start(out=V_ln2w, in_=ln2w_i),
                lambda: q.dma_start(out=V_scale, in_=scale_i),
                lambda: q.dma_start(out=V_onesr, in_=onesr_i),
                lambda: q.dma_start(out=V_qqn, in_=qqn_i),
                lambda: q.dma_start(out=V_kkn, in_=kkn_i),
                lambda: q.dma_start(out=V_bbk, in_=b_bk_i),
                lambda: q.dma_start(out=V_bbv, in_=b_bv_i),
                lambda: q.dma_start(out=V_bbq, in_=b_bq_i),
                lambda: q.dma_start(out=V_bbo, in_=b_bo_i),
                lambda: q.dma_start(out=UA(UO["mask_swa"], 512).rearrange("p (m n) -> p m n", m=2),
                                    in_=m_swa.rearrange("(m p) n -> p m n", p=128)),
                lambda: q.dma_start(out=UA(UO["mask_p1"], 2048).rearrange("p (m n) -> p m n", m=4),
                                    in_=m_p1.rearrange("(m p) n -> p m n", p=128)),
                lambda: q.dma_start(out=UA(UO["mask_p2"], 2048).rearrange("p (m n) -> p m n", m=4),
                                    in_=m_p2.rearrange("(m p) n -> p m n", p=128)),
            ]:
                E(f, "d_c", 16)
            MS("consts", "d_c")
            for i in range(NTH):
                par = i % 2
                if i >= 2:
                    W(q, f"p1x_free_{i-2}")
                E(lambda i=i, par=par: q.dma_start(
                    out=FA((W0, W1)[par], D), in_=x_sl[128 * i:128 * (i + 1), :]),
                  f"d_x{par}", 16)
                MS(f"x_{i}", f"d_x{par}")
                if i >= 1:
                    sync_p1_tail(q, i - 1)
            sync_p1_tail(q, NTH - 1)
            W(q, "ztr_8")
            E(lambda: q.dma_start(
                out=ag_in.ap().rearrange("(t p) n -> p t n", p=128),
                in_=BA(B_ZT, NT * D).rearrange("p (t n) -> p t n", t=NT)), "d_ag", 16)
            MS("ag_in_done", "d_ag")
            for c in range(CH):
                W(q, f"zths_{c}")
                E(lambda c=c: q.dma_start(
                    out=zth_dram.ap()[0, :, 512 * c:512 * (c + 1)],
                    in_=BA(B_ZTHS, 512)), "d_mp", 16)
                E(lambda c=c: q.dma_start(
                    out=zth_dram.ap()[1, :, 512 * c:512 * (c + 1)],
                    in_=BA(B_ZTHS + 512, 512)), "d_mp", 16)
                MS(f"zthsp_{c}", "d_mp")

        @block.vector
        def _(v):
            E(lambda: v.memset(V_fillswa, SWA_FILL))
            E(lambda: v.memset(stat(8), 0.0))
            E(lambda: v.memset(stat(9), LN_EPS))
            E(lambda: v.memset(stat(10), RMS2_EPS))
            E(lambda: v.memset(stat(11), QK_EPS))
            E(lambda: v.memset(FA(FILL2, 1024), MIN_FILL))
            E(lambda: v.memset(FA(NEGONES, 128, p=1), -1.0))
            E(lambda: v.memset(V_fillmin, MIN_FILL), "dve", 1)
            MS("fills", "dve")
            W(v, "consts")
            for i in range(NTH):
                par = i % 2
                own = i >= 1
                W(v, f"x_{i}")
                xb = FA((W0, W1)[par], D)
                E(lambda xb=xb: v.reduce_sum(out=stat(0), in_=xb, axis=AX.X))
                E(lambda: v.drain())
                E(lambda: v.tensor_scalar_mul(out=stat(1), in0=stat(0),
                                              scalar1=-1.0 / D))
                E(lambda: v.drain(), "dve", 1)
                MS(f"nm_{i}", "dve")
                W(v, f"sd_{i}")
                if i >= 1:
                    W(v, f"sd2_{i-1}" if i - 1 >= 1 else "xnb_0")
                E(lambda: v.reciprocal(out=stat(3), in_=stat(2)))
                E(lambda: v.drain())
                E(lambda xb=xb: v.tensor_scalar_add(out=FA(W3, D), in0=xb, scalar1=stat(1)))
                E(lambda: v.tensor_scalar_mul(out=FA(W2, D), in0=FA(W3, D), scalar1=stat(3)))
                if own:
                    if i >= 2:
                        W(v, f"yst_{i-1}")
                    E(lambda xb=xb: v.tensor_add(out=FA(W4, D), in0=xb, in1=FA(W2, D)))
                    E(lambda: v.tensor_add(out=FA(W4, D), in0=FA(W4, D), in1=V_bbo))
                E(lambda: v.drain(), "dve", 1)
                MS(f"xn_{i}", "dve")
                MS(f"y_{i}", "dve")
                MS(f"p1x_free_{i}", "dve")
                if own:
                    W(v, f"sd2_{i}")
                    if i >= 3:
                        W(v, f"ztr_{i-2}")
                    E(lambda: v.reciprocal(out=stat(5), in_=stat(4)))
                    E(lambda: v.drain())
                    E(lambda par=par: v.tensor_scalar_mul(
                        out=BA(B_ZBF[par], D), in0=FA(W2, D), scalar1=stat(5)), "dve", 1)
                    MS(f"z_{i}", "dve")
            W(v, "ztr_8")
            zt3 = BA(B_ZT, NT * D).rearrange("p (t n) -> p t n", t=NT)
            for c in range(CH):
                if c >= 1:
                    W(v, f"zthsp_{c-1}")
                E(lambda c=c: v.tensor_scalar_mul(
                    out=BA(B_ZTHS, 1024).rearrange("p (b n) -> p b n", b=8),
                    in0=zt3[:, :, 128 * c:128 * (c + 1)],
                    scalar1=V_ln2w[:, c:c + 1]), "dve", 1)
                MS(f"zths_{c}", "dve")

        @block.scalar
        def _(sc):
            for i in range(NTH):
                par = i % 2
                own = i >= 1
                xb = FA((W0, W1)[par], D)
                W(sc, f"nm_{i}")
                E(lambda xb=xb: sc.activation(out=FA(W3, D), in_=xb, func=AF.Square,
                                              bias=stat(1), scale=1.0, accum_out=stat(6)))
                E(lambda: sc.drain())
                E(lambda: sc.activation(out=stat(2), in_=stat(6), func=AF.Sqrt,
                                        bias=stat(9), scale=1.0 / D))
                E(lambda: sc.drain(), "act", 1)
                MS(f"sd_{i}", "act")
                W(sc, f"xn_{i}")
                if i >= 2:
                    W(sc, f"xnsp_{i-2}")
                E(lambda par=par: sc.copy(out=BA(B_XNBF[par], D), in_=FA(W2, D)))
                if own:
                    E(lambda: sc.activation(out=FA(W3, D), in_=FA(W2, D), func=AF.Square,
                                            bias=stat(8), scale=1.0, accum_out=stat(7)))
                    E(lambda: sc.drain())
                    E(lambda: sc.activation(out=stat(4), in_=stat(7), func=AF.Sqrt,
                                            bias=stat(10), scale=1.0 / D))
                E(lambda: sc.drain(), "act", 1)
                MS(f"xnb_{i}", "act")
                if own:
                    MS(f"sd2_{i}", "act")
                if i >= 2:
                    emit_zev(sc, i - 1)
            emit_zev(sc, NTH - 1)

        @block.tensor
        def _(pe):
            for j in range(1, NTH):
                par = j % 2
                W(pe, f"z_{j}")
                for c in range(CH):
                    if (j, c) != (1, 0) and not (j == 1 and c == 1):
                        pc_, cc_ = (j, c - 2) if c >= 2 else (j - 1, CH - 2 + c)
                        W(pe, f"zev_{pc_}_{cc_}")
                    E(lambda c=c, par=par: pe.transpose(
                        PBFS[c % 2][:, 0:128], BA(B_ZBF[par] + 128 * c, 128),
                        V_identb), "pe", 1)
                    MS(f"ztrp_{j}_{c}", "pe")

        for name in ENGS:
            dec = dict(pe=block.tensor, dve=block.vector, act=block.scalar,
                       pool=block.gpsimd, sync=block.sync)[name]
            def mk(name):
                return lambda e: env.bar(e, name, "p1")
            dec(mk(name))

        if upto == "p1":
            @block.gpsimd
            def _(g):
                n = env.ctr.get("d_out", 0)
                if not env.dry and n:
                    g.wait_ge(sems["d_out"], n)
            return

        emit_p2(env, block)
        if upto == "p2":
            @block.gpsimd
            def _(g):
                n = env.ctr.get("d_yw", 0)
                if not env.dry and n:
                    g.wait_ge(sems["d_yw"], n)
            return
        emit_p3(env, block)
        if upto in ("p3", "p3r"):
            @block.gpsimd
            def _(g):
                if upto == "p3r":
                    W(g, "rs_trig")
                W(g, "ar_in_done")
                n = env.ctr.get("d_yw", 0)
                if not env.dry and n:
                    g.wait_ge(sems["d_yw"], n)
            return
        emit_p4(env, block, stop=(upto if upto in ("p4a", "p4z", "p4y") else None))
        @block.gpsimd
        def _(g):
            if upto == "p4a":
                W(g, "ham_done")
            elif upto == "p4z":
                W(g, "mod_in")
            elif upto == "p4y":
                W(g, "hAev_7")
            else:
                W(g, "y_done")

    ST_MC = 33
    ST_CS3 = 35
    ST_NEGB = 37
    ST_BIAS = 39
    ST_RZ3 = 41

    def emit_p3(env, block):
        E, W, MS = env.E, env.W, env.MS

        def zt_view(buf_off):
            return BA(buf_off, 8192).rearrange("p (b c l) -> p b c l", b=4, c=CH)

        def nkt(qt):
            return 4 * qt + 4

        def p1_iters(qt):
            out = []
            for sub in range(4):
                nf = (128 * (4 * qt + sub)) // 512
                for kc in range(nf + 1):
                    out.append((sub, kc, kc == nf))
            return out

        @block.gpsimd
        def _(g):
            E(lambda: g.dma_start(out=BA(448, 4096).rearrange("p (c j) -> p c j", c=CH),
                                  in_=wo_ff), "d_w0", 16)
            E(lambda: g.dma_start(out=V_abf.rearrange("p (c r) -> p c r", c=CH),
                                  in_=a_t), "d_w0", 16)
            E(lambda: g.dma_start(out=BA(B_WKF, 4096).rearrange("p (c o) -> p c o", c=CH),
                                  in_=wk_ff), "d_w0", 16)
            E(lambda: g.dma_start(out=BA(B_WVF, 4096).rearrange("p (c o) -> p c o", c=CH),
                                  in_=wv_ff), "d_w0", 16)
            MS("p3w", "d_w0")
            E(lambda: g.memset(BA(B_VA, 2176), 1.0))
            E(lambda: g.drain(), "pool", 1)
            MS("va_ones", "pool")
            W(g, "p3kv_pe_done")
            E(lambda: g.dma_start(out=BA(B_WQF, 4096).rearrange("p (c o) -> p c o", c=CH),
                                  in_=wq_ff), "d_w0", 16)
            MS("p3wq", "d_w0")
            if upto not in ("p3",):
                W(g, "ar_in_done")
                E(lambda: g.collective_compute(
                    "ReduceScatter", ALU.add, replica_groups=[list(range(NCORE))],
                    ins=[ar_in.ap().opt()], outs=[ar_out.ap().opt()]), "cc2", 1)
                MS("rs_trig", "cc2")

        @block.tensor
        def _(pe):
            W(pe, "p3w")
            for hh in range(2):
                if hh == 1:
                    W(pe, "woa_ev_0")
                for c in range(CH):
                    E(lambda hh=hh, c=c: pe.matmul(
                        PB[0][:, 0:16], BA(448 + 256 * c + 128 * hh, 128),
                        V_abf[:, 16 * c:16 * (c + 1)],
                        start=(c == 0), stop=(c == CH - 1)),
                      "pe" if c == CH - 1 else None, 1)
                MS(f"woa_{hh}", "pe")
            MS("woa_pe_done", "pe")

        @block.scalar
        def _(sc):
            for hh in range(2):
                W(sc, f"woa_{hh}")
                E(lambda hh=hh: sc.copy(out=V_woap[:, 16 * hh:16 * (hh + 1)],
                                        in_=PB[0][:, 0:16]))
                E(lambda: sc.drain(), "act", 1)
                MS(f"woa_ev_{hh}", "act")

        @block.vector
        def _(v):
            W(v, "p3w")
            for c in range(CH):
                E(lambda c=c: v.tensor_scalar_mul(
                    out=BA(B_WKF + 256 * c, 256), in0=BA(B_WKF + 256 * c, 256),
                    scalar1=V_ln1w[:, c:c + 1]))
                E(lambda c=c: v.tensor_scalar_mul(
                    out=BA(B_WVF + 256 * c, 256), in0=BA(B_WVF + 256 * c, 256),
                    scalar1=V_ln1w[:, c:c + 1]))
            E(lambda: v.drain(), "dve", 1)
            MS("kvf_folded", "dve")

        @block.sync
        def _(q):
            for cc in range(16):
                par = cc % 2
                if cc >= 2:
                    W(q, f"p3kv_used_{cc-2}")
                if cc == 0:
                    W(q, "ag_trig")
                E(lambda cc=cc, par=par: q.dma_start(
                    out=zt_view(B_ZTD[par]),
                    in_=ag_out[512 * cc:512 * (cc + 1), :]
                    .rearrange("(b p) n -> p b n", p=128)), f"d_z{par}", 16)
                MS(f"zt_{cc}", f"d_z{par}")

        @block.tensor
        def _(pe):
            W(pe, "kvf_folded")
            W(pe, "woa_ev_1")
            for cc in range(16):
                par = cc % 2
                W(pe, f"zt_{cc}")
                for h in range(2):
                    if not (cc == 0 and h == 0):
                        pcc, ph = (cc, 0) if h == 1 else (cc - 1, 1)
                        W(pe, f"kev_{pcc}_{ph}")
                        W(pe, f"vev_{pcc}_{ph}")
                        W(pe, f"vaev_{pcc}_{ph}")
                    for c in range(CH):
                        E(lambda h=h, c=c, par=par: pe.matmul(
                            PB[0][:, :], BA(B_WKF + 256 * c + 128 * h, 128),
                            zt_view(B_ZTD[par])[:, :, c, :],
                            start=(c == 0), stop=(c == CH - 1)),
                          "pe" if c == CH - 1 else None, 1)
                    MS(f"kpe3_{cc}_{h}", "pe")
                    for c in range(CH):
                        E(lambda h=h, c=c, par=par: pe.matmul(
                            PB[1][:, :], BA(B_WVF + 256 * c + 128 * h, 128),
                            zt_view(B_ZTD[par])[:, :, c, :],
                            start=(c == 0), stop=(c == CH - 1)),
                          "pe" if c == CH - 1 else None, 1)
                    MS(f"vpe3_{cc}_{h}", "pe")
                    W(pe, f"vev_{cc}_{h}")
                    for j in range(4):
                        E(lambda h=h, j=j: pe.matmul(
                            PB[3][:, 16 * j:16 * (j + 1)], BA(B_VT + 128 * j, 128),
                            V_woap[:, 16 * h:16 * (h + 1)], start=True, stop=True),
                          "pe" if j == 3 else None, 1)
                    MS(f"vape_{cc}_{h}", "pe")
                MS(f"p3kv_used_{cc}", "pe")
            MS("p3kv_pe_done", "pe")

        @block.scalar
        def _(sc):
            W(sc, "va_ones")
            for cc in range(16):
                for h in range(2):
                    W(sc, f"kpe3_{cc}_{h}")
                    E(lambda cc=cc, h=h: sc.copy(
                        out=BA(B_KF3 + 8192 * h + 512 * cc, 512), in_=PB[0][:, :]), "act", 1)
                    MS(f"kev_{cc}_{h}", "act")
                    W(sc, f"vpe3_{cc}_{h}")
                    E(lambda h=h: sc.copy(out=BA(B_VT, 512), in_=PB[1][:, :]), "act", 1)
                    MS(f"vev_{cc}_{h}", "act")
                    W(sc, f"vape_{cc}_{h}")
                    for j in range(4):
                        E(lambda cc=cc, h=h, j=j: sc.copy(
                            out=BA(B_VA + 1088 * h + 17 * (4 * cc + j), 16),
                            in_=PB[3][:, 16 * j:16 * (j + 1)]),
                          "act" if j == 3 else None, 1)
                    MS(f"vaev_{cc}_{h}", "act")

        PW = (PBW01, PBW56)
        s_last = {0: "vev_15_1", 1: None}
        C0 = FO["cs0"]
        KCH = C0
        KPF = C0 + 32
        QM2 = C0 + 96
        QK2 = C0 + 100
        LTS = C0 + 104
        CVS = C0 + 108
        K2B = 45632
        Q2B = 47680

        @block.sync
        def _(q):
            for qt in range(16):
                if qt >= 1:
                    W(q, f"qproj_done_{qt-1}")
                else:
                    W(q, "p3kv_pe_done")
                E(lambda qt=qt: q.dma_start(
                    out=zt_view(B_ZTS),
                    in_=ag_out[512 * qt:512 * (qt + 1), :]
                    .rearrange("(b p) n -> p b n", p=128)), f"d_zs{qt%2}", 16)
                MS(f"zts_{qt}", f"d_zs{qt%2}")
            W(q, "modpreT_ready")
            E(lambda: q.dma_start(
                out=ar_in.ap().rearrange("s (r p) -> r s p", r=16),
                in_=FA(W0, 8192, p=16).rearrange("r (s p) -> r s p", s=64)),
              "d_mp", 16)
            MS("ar_in_done", "d_mp")

        @block.tensor
        def _(pe):
            W(pe, "qf_folded")
            for i in range(32):
                if i == 0:
                    W(pe, "vaev_15_1")
                if i >= 2:
                    W(pe, f"km_{i-2}")
                W(pe, f"k2k_{i}")
                E(lambda i=i: pe.matmul(
                    PB[(3, 4)[i % 2]][0:1, 0:512], BA(B_VA + 16, 1),
                    BA(K2B + 512 * (i % 4), 512),
                    start=True, stop=True), "pe", 1)
                MS(f"csk_{i}", "pe")
            W(pe, "km_30")
            W(pe, "km_31")
            for qt in range(16):
                W(pe, f"zts_{qt}")
                for h in range(2):
                    if qt >= 1 or h >= 1:
                        pqt, ph = (qt, 0) if h == 1 else (qt - 1, 1)
                        W(pe, f"qev_{pqt}_{ph}")
                    for c in range(CH):
                        E(lambda h=h, c=c: pe.matmul(
                            PB[3][:, :], BA(B_WQF + 256 * c + 128 * h, 128),
                            zt_view(B_ZTS)[:, :, c, :],
                            start=(c == 0), stop=(c == CH - 1)),
                          "pe" if c == CH - 1 else None, 1)
                    MS(f"qpe3_{qt}_{h}", "pe")
                for h in range(2):
                    W(pe, f"q2_{qt}_{h}")
                    E(lambda h=h: pe.matmul(
                        PB[(4, 3)[h]][0:1, 0:512], BA(B_VA + 16, 1),
                        BA(Q2B + 512 * h, 512),
                        start=True, stop=True), "pe", 1)
                    MS(f"csq_{qt}_{h}", "pe")
                for h in range(2):
                    np2 = nkt(qt) // 2
                    order = list(range(2 * qt, np2)) + list(range(2 * qt))

                    def emit_av(oj, order=order, np2=np2, qt=qt, h=h):
                        pj = order[oj]
                        W(pe, f"exppt_{qt}_{h}_{pj}")
                        for half in range(2):
                            kt = 2 * pj + half
                            E(lambda kt=kt, pj=pj, half=half, qt=qt, h=h,
                              st=(oj == 0 and half == 0),
                              sp=(oj == np2 - 1 and half == 1): pe.matmul(
                                PB[4][0:17, :], BA(B_VA + 1088 * h + 17 * kt, 17),
                                BA(B_PTB[pj % 2] + 512 * half, 512),
                                start=st, stop=sp),
                              "pe", 1)
                        MS(f"av_{qt}_{h}_{pj}", "pe")

                    for oi, pi in enumerate(order):
                        wb = pi % 2
                        if s_last[wb]:
                            W(pe, s_last[wb])
                        for half in range(2):
                            kt = 2 * pi + half
                            E(lambda qt=qt, h=h, kt=kt, wb=wb, half=half: pe.matmul(
                                PW[wb][:, 512 * half:512 * (half + 1)],
                                BA(B_KF3 + 8192 * h + 128 * kt, 128),
                                BA(B_QS[qt % 2] + 512 * h, 512),
                                start=True, stop=True), "pe", 1)
                        MS(f"s2_{qt}_{h}_{pi}", "pe")
                        s_last[wb] = f"exp_{qt}_{h}_{pi}"
                        if oi == 1:
                            if h == 0:
                                W(pe, f"qev_{qt}_1")
                                W(pe, f"qk2_{qt}_1")
                            W(pe, f"cq_{qt}_{h}")
                            E(lambda h=h: pe.matmul(
                                PB[3][:, 256 + 32 * h:257 + 32 * h],
                                FA(NEGONES, 128, p=1),
                                FA(CVS + h, 1, p=1), start=True, stop=True),
                              "pe", 1)
                            MS(f"biasmm_{qt}_{h}", "pe")
                        if oi >= 1:
                            emit_av(oi - 1)
                    emit_av(np2 - 1)
                    MS(f"avdone_{qt}_{h}", "pe")
                    W(pe, f"oev_{qt}_{h}")
                    for jj in range(4):
                        E(lambda jj=jj: pe.transpose(
                            PB[3][:, 32 * jj:32 * jj + 17],
                            FA(W2 + 128 * jj, 128, p=17), V_ident[0:17, 0:17]),
                          "pe" if jj == 3 else None, 1)
                    MS(f"otr_{qt}_{h}", "pe")
                    W(pe, f"modp_{qt}_{h}")
            if s_last[0]:
                W(pe, s_last[0])
            for s in range(64):
                if s >= 2:
                    W(pe, f"mptev_{s-2}")
                E(lambda s=s: pe.transpose(PB[s % 2][0:16, 0:128],
                                           FA(W4 + 16 * s, 16), V_ident),
                  "pe", 1)
                MS(f"mptr_{s}", "pe")

        @block.vector
        def _(v):
            W(v, "p3wq")
            for c in range(CH):
                E(lambda c=c: v.tensor_scalar_mul(
                    out=BA(B_WQF + 256 * c, 256), in0=BA(B_WQF + 256 * c, 256),
                    scalar1=V_ln1w[:, c:c + 1]))
            E(lambda: v.drain(), "dve", 1)
            MS("qf_folded", "dve")

            def emit_km(i):
                W(v, f"csk_{i}")
                E(lambda i=i: v.reduce_max(
                    out=FA(KCH + 16 * (i % 2) + i // 2, 1, p=1),
                    in_=PB[(3, 4)[i % 2]][0:1, 0:512], axis=AX.X))
                E(lambda: v.drain(), "dve", 1)
                MS(f"km_{i}", "dve")

            for i in range(32):
                cc, h = i // 2, i % 2
                W(v, f"kev_{cc}_{h}")
                if i >= 4:
                    W(v, f"csk_{i-4}")
                E(lambda cc=cc, h=h, i=i: v.tensor_mul(
                    out=BA(K2B + 512 * (i % 4), 512),
                    in0=BA(B_KF3 + 8192 * h + 512 * cc, 512),
                    in1=BA(B_KF3 + 8192 * h + 512 * cc, 512)), "dve", 1)
                MS(f"k2k_{i}", "dve")
                if i >= 2:
                    emit_km(i - 2)
            for i in (30, 31):
                emit_km(i)
            E(lambda: v.drain())
            for h in range(2):
                for qt in range(16):
                    E(lambda h=h, qt=qt: v.reduce_max(
                        out=FA(KPF + 16 * h + qt, 1, p=1),
                        in_=FA(KCH + 16 * h, qt + 1, p=1), axis=AX.X))
                E(lambda: v.drain())
                E(lambda h=h: v.tensor_scalar_mul(
                    out=FA(KPF + 16 * h, 16, p=1),
                    in0=FA(KPF + 16 * h, 16, p=1), scalar1=SIG * SIG))
            E(lambda: v.drain(), "dve", 1)
            MS("kpf_done", "dve")
            for qt in range(16):
                for h in range(2):
                    W(v, f"qev_{qt}_{h}")
                    if qt >= 1:
                        W(v, f"csq_{qt-1}_{h}")
                    E(lambda h=h, qt=qt: v.tensor_mul(
                        out=BA(Q2B + 512 * h, 512),
                        in0=BA(B_QS[qt % 2] + 512 * h, 512),
                        in1=BA(B_QS[qt % 2] + 512 * h, 512)), "dve", 1)
                    MS(f"q2_{qt}_{h}", "dve")
                for h in range(2):
                    W(v, f"csq_{qt}_{h}")
                    E(lambda h=h: v.reduce_max(
                        out=FA(QM2 + h, 1, p=1),
                        in_=PB[(4, 3)[h]][0:1, 0:512], axis=AX.X))
                    E(lambda: v.drain())
                    E(lambda h=h, qt=qt: v.tensor_mul(
                        out=FA(QK2 + h, 1, p=1), in0=FA(QM2 + h, 1, p=1),
                        in1=FA(KPF + 16 * h + qt, 1, p=1)))
                    E(lambda: v.drain(), "dve", 1)
                    MS(f"qk2_{qt}_{h}", "dve")
                for h in range(2):
                    np2 = nkt(qt) // 2
                    for pi in range(2 * qt, np2):
                        wb = pi % 2
                        W(v, f"s2_{qt}_{h}_{pi}")
                        if pi - 1 >= 2 * qt:
                            W(v, f"exp_{qt}_{h}_{pi-1}")
                        E(lambda wb=wb: v.tensor_copy(out=FA(W1, 1024),
                                                      in_=PW[wb][:, 0:1024]))
                        E(lambda qt=qt, pi=pi: v.copy_predicated(
                            out=FA(W1, 1024),
                            mask=UA(UO["mask_p2"] + 1024 * (pi - 2 * qt), 1024),
                            data=FA(FILL2, 1024)), "dve", 1)
                        MS(f"sel2_{qt}_{h}_{pi}", "dve")
                    W(v, f"otr_{qt}_{h}")
                    for jj in range(4):
                        s = 4 * qt + jj
                        E(lambda jj=jj: v.reciprocal(
                            out=stat(ST_RZ3),
                            in_=PB[3][:, 32 * jj + 16:32 * jj + 17]))
                        E(lambda: v.drain())
                        if h == 0:
                            E(lambda jj=jj, s=s: v.tensor_scalar_mul(
                                out=FA(W4 + 16 * s, 16),
                                in0=PB[3][:, 32 * jj:32 * jj + 16],
                                scalar1=stat(ST_RZ3)))
                        else:
                            E(lambda jj=jj, s=s: v.scalar_tensor_tensor(
                                out=FA(W4 + 16 * s, 16),
                                in0=PB[3][:, 32 * jj:32 * jj + 16],
                                scalar=stat(ST_RZ3), in1=FA(W4 + 16 * s, 16),
                                op0=ALU.mult, op1=ALU.add))
                    E(lambda: v.drain(), "dve", 1)
                    MS(f"modp_{qt}_{h}", "dve")
            for s in range(64):
                W(v, f"mptr_{s}")
                E(lambda s=s: v.tensor_copy(out=FA(W0 + 128 * s, 128, p=16),
                                            in_=PB[s % 2][0:16, 0:128]), "dve", 1)
                MS(f"mptev_{s}", "dve")
            MS("modpreT_ready", "dve")

        @block.scalar
        def _(sc):
            for qt in range(16):
                for h in range(2):
                    W(sc, f"qpe3_{qt}_{h}")
                    if qt >= 2 and h == 0:
                        W(sc, f"avdone_{qt-2}_1")
                    E(lambda qt=qt, h=h: sc.copy(
                        out=BA(B_QS[qt % 2] + 512 * h, 512), in_=PB[3][:, :]), "act", 1)
                    MS(f"qev_{qt}_{h}", "act")
                MS(f"qproj_done_{qt}", "act")
                for h in range(2):
                    W(sc, f"qk2_{qt}_{h}")
                    E(lambda h=h: sc.activation(
                        out=FA(LTS + h, 1, p=1), in_=FA(QK2 + h, 1, p=1),
                        func=AF.Ln, bias=stat(8, 1, p=1), scale=1.0))
                    E(lambda: sc.drain())
                    E(lambda h=h: sc.activation(
                        out=FA(CVS + h, 1, p=1), in_=FA(LTS + h, 1, p=1),
                        func=AF.Exp, bias=stat(8, 1, p=1), scale=0.5))
                    E(lambda: sc.drain(), "act", 1)
                    MS(f"cq_{qt}_{h}", "act")
                for h in range(2):
                    W(sc, f"biasmm_{qt}_{h}")
                    E(lambda h=h: sc.copy(out=stat(ST_BIAS + h),
                                          in_=PB[3][:, 256 + 32 * h:257 + 32 * h]))
                    E(lambda: sc.drain(), "act", 1)
                    MS(f"bias_{qt}_{h}", "act")
                    np2 = nkt(qt) // 2
                    order = list(range(2 * qt, np2)) + list(range(2 * qt))
                    for oi, pi in enumerate(order):
                        diag = pi >= 2 * qt
                        if oi >= 2:
                            W(sc, f"av_{qt}_{h}_{order[oi-2]}")
                        if diag:
                            W(sc, f"sel2_{qt}_{h}_{pi}")
                            E(lambda pi=pi, h=h: sc.activation(
                                out=BA(B_PTB[pi % 2], 1024), in_=FA(W1, 1024),
                                func=AF.Exp, bias=stat(ST_BIAS + h), scale=SIG), "act", 1)
                        else:
                            W(sc, f"s2_{qt}_{h}_{pi}")
                            E(lambda pi=pi, h=h, wb=pi % 2: sc.activation(
                                out=BA(B_PTB[pi % 2], 1024),
                                in_=PW[wb][:, 0:1024],
                                func=AF.Exp, bias=stat(ST_BIAS + h), scale=SIG), "act", 1)
                        MS(f"exp_{qt}_{h}_{pi}", "act")
                        MS(f"exppt_{qt}_{h}_{pi}", "act")
                    W(sc, f"avdone_{qt}_{h}")
                    E(lambda: sc.copy(out=FA(W2, 512, p=17), in_=PB[4][0:17, :]), "act", 1)
                    MS(f"oev_{qt}_{h}", "act")

        for name in ENGS:
            dec = dict(pe=block.tensor, dve=block.vector, act=block.scalar,
                       pool=block.gpsimd, sync=block.sync)[name]
            def mk(name):
                return lambda e: env.bar(e, name, "p3")
            dec(mk(name))

    HA_F = FO["w4"] + 1536
    MPRE_F = HA_F + 128
    GS_F = FO["w4"]
    GL_F = FO["w4"] + 512
    MS_F = FO["w4"] + 1024
    OWNX_F = FO["b_bq"]

    def emit_p4(env, block, stop=None):
        E, W, MS = env.E, env.W, env.MS

        @block.sync
        def _(q):
            for b in range(2):
                if stop == "p4z":
                    break
                if b == 1:
                    W(q, "hA_pe_0")
                E(lambda b=b: q.dma_start(out=BA(B_ZTHB, 8192),
                                          in_=zth_dram.ap()[b]), "d_zt4", 16)
                MS(f"zthA_{b}", "d_zt4")
            W(q, "rs_trig")
            E(lambda: q.dma_start(out=FA(OWNX_F, 2048, p=8), in_=ar_out.ap()),
              "d_mp", 16)
            MS("mod_in", "d_mp")
            if stop in ("p4a", "p4z", "p4y"):
                return
            for b in range(2):
                if b == 1 and DBG_P4:
                    W(q, "mlpin_0_63")
                    E(lambda: q.dma_start(out=dbg_mlpin.ap(), in_=BA(B_MLPIN, 32768)), "d_bb", 16)
                    E(lambda: q.dma_start(out=dbg_hamt.ap(), in_=BA(B_HAMT, 1024, p=16)), "d_bb", 16)
                    E(lambda: q.dma_start(out=dbg_ha.ap(), in_=FA(HA_F, 128)), "d_bb", 16)
                    MS("dbg_p4_done", "d_bb")
                W(q, "hA_pe_1" if b == 0 else f"gate_pe_{b-1}")
                E(lambda b=b: q.dma_start(out=BA(B_ZTHB, 8192),
                                          in_=zth_dram.ap()[b]), "d_zt4", 16)
                MS(f"zthB_{b}", "d_zt4")
                for ts in range(4):
                    W(q, "ham_done" if b == 0 else f"yfin_st_{b-1}_{ts}")
                    E(lambda b=b, ts=ts: q.dma_start(
                        out=FA((W0, W1, W2, W3)[ts], D),
                        in_=y_part.ap()[512 * b + 128 * ts:512 * b + 128 * (ts + 1), :]),
                      f"d_yb{ts%2}", 16)
                    MS(f"ybk4_{b}_{ts}", f"d_yb{ts%2}")
                for ts in range(4):
                    for dc in range(4):
                        W(q, f"yadd4_{b}_{dc}_{ts}")
                    E(lambda b=b, ts=ts: q.dma_start(
                        out=y_out[512 * b + 128 * ts:512 * b + 128 * (ts + 1), :],
                        in_=FA((W0, W1, W2, W3)[ts], D)), "d_out", 16)
                    MS(f"yfin_st_{b}_{ts}", "d_out")
            MS("y_done", "d_out")

        @block.gpsimd
        def _(g):
            E(lambda: g.dma_start(out=BA(B_BBF, F_, p=16), in_=b_t), "d_bb", 16)
            MS("bbf", "d_bb")
            if stop in ("p4a", "p4z", "p4y"):
                return

            for b in range(2):
                for fc in range(NFC):
                    par = fc % 2
                    if (b, fc) != (0, 0) and not (b == 1 and fc == 0):
                        if fc >= 2:
                            W(g, f"mlpin_{b}_{fc-2}")
                    if b == 1 and fc < 2:
                        W(g, f"mlpin_{0}_{NFC-2+fc}")
                    if b == 1 and fc == 0:
                        W(g, "wdd_0_3")
                    E(lambda b=b, fc=fc, par=par: g.dma_start(
                        out=BA(B_WIN[par], 2048).rearrange("p (c f) -> p c f", c=CH),
                        in_=w_in_t[fc]), f"d_wi{par}", 16)
                    E(lambda b=b, fc=fc, par=par: g.dma_start(
                        out=BA(B_WUP[par], 2048).rearrange("p (c f) -> p c f", c=CH),
                        in_=w_up_t[fc]), f"d_wu{par}", 16)
                    MS(f"wio_{b}_{fc}", f"d_wi{par}")
                    MS(f"wup_{b}_{fc}", f"d_wu{par}")
                for dc in range(4):
                    for pc in range(8):
                        par = pc % 2
                        k = dc * 8 + pc
                        if k == 0:
                            W(g, f"gate_pe_{b}")
                        if k >= 2:
                            W(g, f"wdn_used_{b}_{k-2}")
                        E(lambda b=b, dc=dc, pc=pc, par=par: g.dma_start(
                            out=BA(B_WDN[par], 4096).rearrange("p (f n) -> p f n", f=8),
                            in_=w_dn_t[dc, :, 8 * pc:8 * (pc + 1), :]),
                          f"d_wd{par}", 16)
                        MS(f"wdn_{b}_{k}", f"d_wd{par}")

        @block.tensor
        def _(pe):
            if stop == "p4z":
                return
            for b in range(2):
                W(pe, f"zthA_{b}")
                for tsl in range(4):
                    tt = 4 * b + tsl
                    if tt >= 1:
                        W(pe, f"hAev_{tt-1}")
                    for c in range(CH):
                        E(lambda c=c, tsl=tsl: pe.matmul(
                            PB[6][:, 0:16], BA(B_ZTHB + 512 * c + 128 * tsl, 128),
                            V_abf[:, 16 * c:16 * (c + 1)],
                            start=(c == 0), stop=(c == CH - 1)),
                          "pe" if c == CH - 1 else None, 1)
                    MS(f"hA_{tt}", "pe")
                MS(f"hA_pe_{b}", "pe")
            if stop == "p4y":
                return
            W(pe, "mod_in")
            for r in range(16):
                if r >= 2:
                    W(pe, f"mprev_{r-2}")
                E(lambda r=r: pe.transpose(PB[r % 2][:, 0:8],
                                           FA(OWNX_F + 128 * r, 128, p=8),
                                           V_ident[0:8, 0:8]), "pe", 1)
                MS(f"mptr4_{r}", "pe")
            W(pe, "hambf")
            for tt in range(8):
                if tt >= 2:
                    W(pe, f"hamtev_{tt-2}")
                E(lambda tt=tt: pe.transpose(
                    PBFS[tt % 2][0:16, 0:128],
                    BA(B_MPB + 16 * tt, 16), V_identb), "pe", 1)
                MS(f"hamtr_{tt}", "pe")
            if stop == "p4a":
                return
            for b in range(2):
                W(pe, f"zthB_{b}")
                W(pe, "ham_done")
                if b == 1:
                    for ts in range(4):
                        W(pe, f"yadd4_0_3_{ts}")
                    if DBG_P4:
                        W(pe, "dbg_p4_done")
                for fc in range(NFC):
                    par = fc % 2
                    W(pe, f"wio_{b}_{fc}")
                    W(pe, f"wup_{b}_{fc}")
                    if fc >= 2:
                        W(pe, f"gadd_{b}_{fc-2}")
                        W(pe, f"mlpin_{b}_{fc-2}")
                        W(pe, f"mev_{b}_{fc-2}")
                    for c in range(CH):
                        E(lambda c=c, par=par: pe.matmul(
                            PB[0 + par][:, :], BA(B_WIN[par] + 128 * c, 128),
                            BA(B_ZTHB + 512 * c, 512),
                            start=(c == 0), stop=(c == CH - 1)),
                          "pe" if c == CH - 1 else None, 1)
                    E(lambda b=b, fc=fc, par=par: pe.matmul(
                        PB[5 + par][:, :], BA(B_BBF + 128 * fc, 128, p=16),
                        BA(B_HAMT + 512 * b, 512, p=16), start=True, stop=True),
                      "pe", 1)
                    for c in range(CH):
                        E(lambda c=c, par=par: pe.matmul(
                            PB[3 + par][:, :], BA(B_WUP[par] + 128 * c, 128),
                            BA(B_ZTHB + 512 * c, 512),
                            start=(c == 0), stop=(c == CH - 1)),
                          "pe" if c == CH - 1 else None, 1)
                    MS(f"gup_{b}_{fc}", "pe")
                MS(f"gate_pe_{b}", "pe")
                for dc in range(4):
                    for pc in range(8):
                        k = dc * 8 + pc
                        par = pc % 2
                        W(pe, f"wdn_{b}_{k}")
                        if k == 0:
                            W(pe, f"mlpin_{b}_{NFC-1}")
                            W(pe, f"gadd_{b}_{NFC-1}")
                            W(pe, f"gadd_{b}_{NFC-2}")
                            W(pe, f"mev_{b}_{NFC-1}")
                            W(pe, f"mev_{b}_{NFC-2}")
                        if pc == 0 and dc >= 1:
                            for ts in range(4):
                                W(pe, f"yadd4_{b}_{dc-1}_{ts}")
                        for ts in range(4):
                            for fcl in range(8):
                                fc = 8 * pc + fcl
                                E(lambda ts=ts, fcl=fcl, fc=fc, par=par: pe.matmul(
                                    PB[(0, 1, 3, 4)[ts]][:, :],
                                    BA(B_MLPIN + 512 * fc + 128 * ts, 128),
                                    BA(B_WDN[par] + 512 * fcl, 512),
                                    start=(fc == 0), stop=(fc == NFC - 1)),
                                  "pe" if fcl == 7 else None, 1)
                        MS(f"wdnp_{b}_{k}", "pe")
                        MS(f"wdn_used_{b}_{k}", "pe")
                    MS(f"wdd_{b}_{dc}", "pe")

        @block.vector
        def _(v):
            if stop == "p4z":
                return
            for b in range(2):
                for tsl in range(4):
                    tt = 4 * b + tsl
                    W(v, f"hA_{tt}")
                    E(lambda tt=tt: v.tensor_copy(out=FA(HA_F + 16 * tt, 16),
                                                  in_=PB[6][:, 0:16]), "dve", 1)
                    MS(f"hAev_{tt}", "dve")
            if stop == "p4y":
                return
            for r in range(16):
                W(v, f"mptr4_{r}")
                E(lambda r=r: v.tensor_copy(
                    out=AR_F[:, MPRE_F + r:MPRE_F + 128:16],
                    in_=PB[r % 2][:, 0:8]), "dve", 1)
                MS(f"mprev_{r}", "dve")
            W(v, "tanh_done")
            E(lambda: v.scalar_tensor_tensor(out=FA(MPRE_F, 128), in0=FA(FO["w3"], 128),
                                             scalar=0.1, in1=FA(HA_F, 128),
                                             op0=ALU.mult, op1=ALU.mult))
            E(lambda: v.tensor_copy(out=BA(B_MPB, 128), in_=FA(MPRE_F, 128)))
            E(lambda: v.drain(), "dve", 1)
            MS("hambf", "dve")
            for tt in range(8):
                W(v, f"hamtr_{tt}")
                E(lambda tt=tt: v.tensor_copy(
                    out=BA(B_HAMT + 128 * tt, 128, p=16),
                    in_=PBFS[tt % 2][0:16, 0:128]), "dve", 1)
                MS(f"hamtev_{tt}", "dve")
            MS("ham_done", "dve")
            if stop == "p4a":
                return
            for b in range(2):
                for fc in range(NFC):
                    par = fc % 2
                    W(v, f"gup_{b}_{fc}")
                    W(v, f"mev_{b}_{fc}")
                    if fc >= 1:
                        W(v, f"silu_{b}_{fc-1}")
                    elif b == 1:
                        W(v, f"silu_0_{NFC-1}")
                    E(lambda par=par: v.tensor_add(out=FA(GS_F, 512),
                                                   in0=PB[0 + par][:, :],
                                                   in1=FA(MS_F, 512)), "dve", 1)
                    MS(f"gadd_{b}_{fc}", "dve")
                    W(v, f"silu_{b}_{fc}")
                    E(lambda fc=fc, par=par: v.tensor_mul(
                        out=BA(B_MLPIN + 512 * fc, 512), in0=PB[3 + par][:, :],
                        in1=FA(GL_F, 512)), "dve", 1)
                    MS(f"mlpin_{b}_{fc}", "dve")
                for dc in range(4):
                    for ts in range(4):
                        W(v, f"wdd_{b}_{dc}") if ts == 0 else None
                        W(v, f"ybk4_{b}_{ts}")
                        E(lambda b=b, dc=dc, ts=ts: v.tensor_add(
                            out=FA((W0, W1, W2, W3)[ts] + 512 * dc, 512),
                            in0=PB[(0, 1, 3, 4)[ts]][:, :],
                            in1=FA((W0, W1, W2, W3)[ts] + 512 * dc, 512)), "dve", 1)
                        MS(f"yadd4_{b}_{dc}_{ts}", "dve")

        @block.scalar
        def _(sc):
            if stop in ("p4z", "p4y"):
                return
            W(sc, "mprev_15")
            E(lambda: sc.activation(out=FA(FO["w3"], 128), in_=FA(MPRE_F, 128),
                                    func=AF.Tanh, bias=stat(8), scale=1.0))
            E(lambda: sc.drain(), "act", 1)
            MS("tanh_done", "act")
            if stop == "p4a":
                return
            for b in range(2):
                for fc in range(NFC):
                    par = fc % 2
                    W(sc, f"gup_{b}_{fc}")
                    E(lambda par=par: sc.copy(out=FA(MS_F, 512), in_=PB[5 + par][:, :]), "act", 1)
                    MS(f"mev_{b}_{fc}", "act")
                    W(sc, f"gadd_{b}_{fc}")
                    E(lambda: sc.activation(out=FA(GL_F, 512), in_=FA(GS_F, 512),
                                            func=AF.Silu, bias=stat(8), scale=1.0), "act", 1)
                    MS(f"silu_{b}_{fc}", "act")

    def emit_p2(env, block):
        E, W, MS = env.E, env.W, env.MS
        ST_KS = 16
        ST_SD4 = 20
        ST_R4 = 24
        ST_NEGM = 28
        ST_Z = 29
        ST_RZ = 30

        def rope_dve(v, src_off, cs_off, dst_off, n4):
            se = AR_F[:, src_off:src_off + 512:2]
            so = AR_F[:, src_off + 1:src_off + 512:2]
            de = AR_F[:, dst_off:dst_off + 512:2]
            do = AR_F[:, dst_off + 1:dst_off + 512:2]
            cc_ = FA(cs_off, 256)
            ss_ = FA(cs_off + 256, 256)
            t1 = FA(W3, 256)
            t2 = FA(W3 + 256, 256)
            E(lambda: v.tensor_mul(out=t1, in0=se, in1=cc_))
            E(lambda: v.tensor_mul(out=t2, in0=so, in1=ss_))
            E(lambda: v.tensor_sub(out=de, in0=t1, in1=t2))
            E(lambda: v.tensor_mul(out=t1, in0=se, in1=ss_))
            E(lambda: v.tensor_mul(out=t2, in0=so, in1=cc_))
            E(lambda: v.tensor_add(out=do, in0=t1, in1=t2))

        @block.gpsimd
        def _(g):
            W(g, "ag_in_done")
            E(lambda: g.collective_compute(
                "AllGather", ALU.bypass, replica_groups=[list(range(NCORE))],
                ins=[ag_in.ap().opt()], outs=[ag_out.opt()]), "cc1", 1)
            MS("ag_trig", "cc1")
            E(lambda: g.dma_start(out=BA(B_WKS, 8192).rearrange("p (c o) -> p c o", c=CH),
                                  in_=wk_sw), "d_sw", 16)
            E(lambda: g.dma_start(out=BA(B_WVS, 8192).rearrange("p (c o) -> p c o", c=CH),
                                  in_=wv_sw), "d_sw", 16)
            MS("kvw", "d_sw")
            for hg in range(4):
                if hg >= 1:
                    W(g, f"qpe_all_{hg-1}")
                else:
                    W(g, "p2kv_pe_done")
                E(lambda hg=hg: g.dma_start(
                    out=BA(B_WQS, 8192).rearrange("p (c o) -> p c o", c=CH),
                    in_=wq_sw[hg]), "d_sw", 16)
                MS(f"wq_{hg}", "d_sw")
            W(g, "attn_all")
            if upto == "p2" and DBG_P2:
                E(lambda: g.dma_start(out=dbg_heads.ap(), in_=BA(B_HEADS, 16384)), "d_out", 16)
                E(lambda: g.dma_start(out=dbg_qfm.ap(), in_=BA(B_QFM, 16384)), "d_out", 16)
                E(lambda: g.dma_start(out=dbg_kfm.ap(), in_=BA(B_KFM, 4608)), "d_out", 16)
                E(lambda: g.dma_start(out=dbg_vsb.ap(), in_=BA(B_VSB, 4608)), "d_out", 16)
                MS("dbgp2_done", "d_out")
                W(g, "dbgp2_done")
            E(lambda: g.dma_start(
                out=BA(B_WOL, 16384).rearrange("p (h n) -> p h n", h=8),
                in_=wo_sw[0:8].rearrange("h p a o -> p h (a o)")), "d_wo", 16)
            E(lambda: g.dma_start(
                out=BA(B_WOH, 16384).rearrange("p (h n) -> p h n", h=8),
                in_=wo_sw[8:16].rearrange("h p a o -> p h (a o)")), "d_wo", 16)
            MS("wo_ready", "d_wo")

        @block.sync
        def _(q):
            for i in range(NTH):
                par = i % 2
                if i >= 2:
                    W(q, f"xnT_{i-2}")
                    W(q, f"krope_{i-2}")
                E(lambda i=i, par=par: q.dma_start(
                    out=BA(B_XN2[par], D), in_=xn_dram.ap()[128 * i:128 * (i + 1), :]),
                  f"d_xr{par}", 16)
                MS(f"xr_{i}", f"d_xr{par}")
                E(lambda i=i, par=par: q.dma_start(
                    out=FA(CS[par], 256), in_=cos_t[128 * i:128 * (i + 1), :]),
                  f"d_cs{par}", 16)
                E(lambda i=i, par=par: q.dma_start(
                    out=FA(CS[par] + 256, 256), in_=sin_t[128 * i:128 * (i + 1), :]),
                  f"d_cs{par}", 16)
                MS(f"cs_{i}", f"d_cs{par}")


        pbfu = {0: None, 1: None}

        def pbf_tr(pe, idx, msname, f):
            if pbfu[idx]:
                W(pe, pbfu[idx])
            E(f, "pe", 1)
            MS(msname + "_p", "pe")

        @block.tensor
        def _(pe):
            for i in range(NTH):
                par = i % 2
                W(pe, f"xr_{i}")
                for c in range(CH):
                    pbf_tr(pe, c % 2, f"xtr_{i}_{c}",
                           lambda c=c, par=par: pe.transpose(
                               PBFS[c % 2][:, 0:128],
                               BA(B_XN2[par] + 128 * c, 128), V_identb))
                    pbfu[c % 2] = f"xtr_{i}_{c}_e"
                W(pe, "kvw")
                W(pe, f"xnT_{i}")
                if i >= 1:
                    W(pe, f"kc_{i-1}")
                    W(pe, f"vc_{i-1}")
                for c in range(CH):
                    E(lambda i=i, c=c: pe.matmul(
                        PB[0][:, :], BA(B_XNT + 1152 * c + 128 * i, 128),
                        BA(B_WKS + 512 * c, 512), start=(c == 0), stop=(c == CH - 1)),
                      "pe" if c == CH - 1 else None, 1)
                MS(f"kpe_{i}", "pe")
                for c in range(CH):
                    E(lambda i=i, c=c: pe.matmul(
                        PB[1][:, :], BA(B_XNT + 1152 * c + 128 * i, 128),
                        BA(B_WVS + 512 * c, 512), start=(c == 0), stop=(c == CH - 1)),
                      "pe" if c == CH - 1 else None, 1)
                MS(f"vpe_{i}", "pe")
                if i >= 1:
                    W(pe, f"krope_{i-1}")
                    for hv in range(4):
                        pbf_tr(pe, hv % 2, f"ktr_{i-1}_{hv}",
                               lambda hv=hv, p2=(i - 1) % 2: pe.transpose(
                                   PBFS[hv % 2][:, 0:128],
                                   BA(B_KB[p2] + 128 * hv, 128), V_identb))
                        pbfu[hv % 2] = f"ktr_{i-1}_{hv}_e"
            W(pe, f"krope_{NTH-1}")
            for hv in range(4):
                pbf_tr(pe, hv % 2, f"ktr_{NTH-1}_{hv}",
                       lambda hv=hv, p2=(NTH - 1) % 2: pe.transpose(
                           PBFS[hv % 2][:, 0:128],
                           BA(B_KB[p2] + 128 * hv, 128), V_identb))
                pbfu[hv % 2] = f"ktr_{NTH-1}_{hv}_e"
            MS("p2kv_pe_done", "pe")

        @block.vector
        def _(v):
            for i in range(NTH):
                par = i % 2
                W(v, f"kpe_{i}")
                E(lambda: v.tensor_add(out=FA(W1, 512), in0=PB[0][:, :], in1=V_bbk), "dve", 1)
                MS(f"kc_{i}", "dve")
                W(v, f"ksd_{i}")
                if i >= 2:
                    W(v, f"ktr_{i-2}")
                if i >= 1:
                    W(v, f"vsb_{i-1}")
                E(lambda: v.reciprocal(out=stat(ST_R4, 4), in_=stat(ST_SD4, 4)))
                E(lambda: v.drain())
                for hv in range(4):
                    E(lambda hv=hv: v.tensor_scalar_mul(
                        out=FA(W2 + 128 * hv, 128), in0=FA(W1 + 128 * hv, 128),
                        scalar1=stat(ST_R4 + hv)))
                E(lambda: v.tensor_mul(out=FA(W2, 512), in0=FA(W2, 512), in1=V_kkn))
                W(v, f"cs_{i}")
                rope_dve(v, W2, CS[par], W1, 4)
                E(lambda par=par: v.tensor_copy(out=BA(B_KB[par], 512), in_=FA(W1, 512)), "dve", 1)
                MS(f"krope_{i}", "dve")
                W(v, f"vpe_{i}")
                E(lambda: v.tensor_add(out=FA(W2, 512), in0=PB[1][:, :], in1=V_bbv), "dve", 1)
                MS(f"vc_{i}", "dve")

        @block.scalar
        def _(sc):
            for i in range(NTH):
                for c in range(CH):
                    W(sc, f"xtr_{i}_{c}_p")
                    E(lambda i=i, c=c: sc.copy(
                        out=BA(B_XNT + 1152 * c + 128 * i, 128),
                        in_=PBFS[c % 2][:, 0:128]), "act", 1)
                    MS(f"xtr_{i}_{c}_e", "act")
                MS(f"xnT_{i}", "act")
                W(sc, f"kc_{i}")
                for hv in range(4):
                    E(lambda hv=hv: sc.activation(
                        out=FA(W3, 128), in_=FA(W1 + 128 * hv, 128), func=AF.Square,
                        bias=stat(8), scale=1.0, accum_out=stat(ST_KS + hv)))
                E(lambda: sc.drain())
                E(lambda: sc.activation(out=stat(ST_SD4, 4), in_=stat(ST_KS, 4),
                                        func=AF.Sqrt, bias=stat(11), scale=1.0 / 128))
                E(lambda: sc.drain(), "act", 1)
                MS(f"ksd_{i}", "act")
                W(sc, f"vc_{i}")
                E(lambda i=i: sc.copy(out=BA(B_VSB + 512 * i, 512), in_=FA(W2, 512)), "act", 1)
                MS(f"vsb_{i}", "act")
                if i >= 1:
                    for hv in range(4):
                        W(sc, f"ktr_{i-1}_{hv}_p")
                        E(lambda i=i, hv=hv: sc.copy(
                            out=BA(B_KFM + 1152 * hv + 128 * (i - 1), 128),
                            in_=PBFS[hv % 2][:, 0:128]), "act", 1)
                        MS(f"ktr_{i-1}_{hv}_e", "act")
                    MS(f"ktr_{i-1}", "act")
            for hv in range(4):
                W(sc, f"ktr_{NTH-1}_{hv}_p")
                E(lambda hv=hv: sc.copy(
                    out=BA(B_KFM + 1152 * hv + 128 * (NTH - 1), 128),
                    in_=PBFS[hv % 2][:, 0:128]), "act", 1)
                MS(f"ktr_{NTH-1}_{hv}_e", "act")
            MS(f"ktr_{NTH-1}", "act")

        @block.sync
        def _(q):
            for hg in range(4):
                for ti, i in enumerate(range(1, NTH)):
                    k = hg * 8 + ti
                    par = k % 2
                    if k >= 2:
                        W(q, f"qrope_{k-2}")
                    else:
                        W(q, f"krope_{NTH-1}")
                    E(lambda i=i, par=par: q.dma_start(
                        out=FA(CS[par], 256), in_=cos_t[128 * i:128 * (i + 1), :]),
                      f"d_cs{par}", 16)
                    E(lambda i=i, par=par: q.dma_start(
                        out=FA(CS[par] + 256, 256), in_=sin_t[128 * i:128 * (i + 1), :]),
                      f"d_cs{par}", 16)
                    MS(f"qcs_{k}", f"d_cs{par}")

        @block.tensor
        def _(pe):
            for hg in range(4):
                W(pe, f"wq_{hg}")
                for ti, i in enumerate(range(1, NTH)):
                    k = hg * 8 + ti
                    if k >= 1:
                        W(pe, f"qc_{k-1}")
                    for c in range(CH):
                        E(lambda i=i, c=c: pe.matmul(
                            PB[3][:, :], BA(B_XNT + 1152 * c + 128 * i, 128),
                            BA(B_WQS + 512 * c, 512), start=(c == 0), stop=(c == CH - 1)),
                          "pe" if c == CH - 1 else None, 1)
                    MS(f"qpe_{k}", "pe")
                    if k >= 1:
                        W(pe, f"qrope_{k-1}")
                        for hh in range(4):
                            pbf_tr(pe, hh % 2, f"qtr_{k-1}_{hh}",
                                   lambda hh=hh, p2=(k - 1) % 2: pe.transpose(
                                       PBFS[hh % 2][:, 0:128],
                                       BA(B_QB[p2] + 128 * hh, 128), V_identb))
                            pbfu[hh % 2] = f"qtr_{k-1}_{hh}_e"
                MS(f"qpe_all_{hg}", "pe")
            W(pe, "qrope_31")
            for hh in range(4):
                pbf_tr(pe, hh % 2, f"qtr_31_{hh}",
                       lambda hh=hh: pe.transpose(
                           PBFS[hh % 2][:, 0:128],
                           BA(B_QB[31 % 2] + 128 * hh, 128), V_identb))
                pbfu[hh % 2] = f"qtr_31_{hh}_e"

        @block.vector
        def _(v):
            for hg in range(4):
                for ti, i in enumerate(range(1, NTH)):
                    k = hg * 8 + ti
                    par = k % 2
                    W(v, f"qpe_{k}")
                    E(lambda hg=hg: v.tensor_add(out=FA(W0, 512), in0=PB[3][:, :],
                                                 in1=V_bbq[:, 512 * hg:512 * (hg + 1)]), "dve", 1)
                    MS(f"qc_{k}", "dve")
                    W(v, f"qsd_{k}")
                    if k >= 2:
                        W(v, f"qtr_{k-2}")
                    E(lambda: v.reciprocal(out=stat(ST_R4, 4), in_=stat(ST_SD4, 4)))
                    E(lambda: v.drain())
                    for hh in range(4):
                        E(lambda hh=hh: v.tensor_scalar_mul(
                            out=FA(W2 + 128 * hh, 128), in0=FA(W0 + 128 * hh, 128),
                            scalar1=stat(ST_R4 + hh)))
                    E(lambda: v.tensor_mul(out=FA(W2, 512), in0=FA(W2, 512), in1=V_qqn))
                    W(v, f"qcs_{k}")
                    rope_dve(v, W2, CS[par], W0, 4)
                    E(lambda: v.tensor_scalar_mul(out=FA(W0, 512), in0=FA(W0, 512),
                                                  scalar1=V_scale))
                    E(lambda par=par: v.tensor_copy(out=BA(B_QB[par], 512), in_=FA(W0, 512)), "dve", 1)
                    MS(f"qrope_{k}", "dve")

        @block.scalar
        def _(sc):
            for hg in range(4):
                for ti in range(8):
                    k = hg * 8 + ti
                    W(sc, f"qc_{k}")
                    for hh in range(4):
                        E(lambda hh=hh: sc.activation(
                            out=FA(W3, 128), in_=FA(W0 + 128 * hh, 128), func=AF.Square,
                            bias=stat(8), scale=1.0, accum_out=stat(ST_KS + hh)))
                    E(lambda: sc.drain())
                    E(lambda: sc.activation(out=stat(ST_SD4, 4), in_=stat(ST_KS, 4),
                                            func=AF.Sqrt, bias=stat(11), scale=1.0 / 128))
                    E(lambda: sc.drain(), "act", 1)
                    MS(f"qsd_{k}", "act")
                    if k >= 1:
                        kk = k - 1
                        hgp, tip = kk // 8, kk % 8
                        for hh in range(4):
                            W(sc, f"qtr_{kk}_{hh}_p")
                            E(lambda hgp=hgp, hh=hh, tip=tip: sc.copy(
                                out=BA(B_QFM + 1024 * (4 * hgp + hh) + 128 * tip, 128),
                                in_=PBFS[hh % 2][:, 0:128]), "act", 1)
                            MS(f"qtr_{kk}_{hh}_e", "act")
                        MS(f"qtr_{kk}", "act")
            kk = 31
            for hh in range(4):
                W(sc, f"qtr_{kk}_{hh}_p")
                E(lambda hh=hh: sc.copy(
                    out=BA(B_QFM + 1024 * (4 * 3 + hh) + 128 * 7, 128),
                    in_=PBFS[hh % 2][:, 0:128]), "act", 1)
                MS(f"qtr_{kk}_{hh}_e", "act")
            MS(f"qtr_{kk}", "act")

        @block.tensor
        def _(pe):
            first = True
            for h in range(16):
                hv = h // 4
                for qt in range(8):
                    k = h * 8 + qt
                    if first:
                        W(pe, "qtr_31")
                        W(pe, "qtr_30")
                        W(pe, "ktr_8")
                        W(pe, "ktr_7")
                        W(pe, "vsb_8")
                        first = False
                    par = k % 2
                    if k >= 2:
                        W(pe, f"sel_{k-2}")
                    E(lambda h=h, qt=qt, hv=hv, par=par: pe.matmul(
                        PB[3 + par][:, 0:256], BA(B_QFM + 1024 * h + 128 * qt, 128),
                        BA(B_KFM + 1152 * hv + 128 * qt, 256),
                        start=True, stop=True), "pe", 1)
                    MS(f"s_{k}", "pe")
                    W(pe, f"pn_{k}")
                    if k >= 2:
                        W(pe, f"pt_{k-2}")
                    E(lambda par=par: pe.transpose(PBFS[par][:, 0:128], BA(B_PN, 128), V_identb))
                    E(lambda par=par: pe.transpose(PBFS[par][:, 128:256], BA(B_PN + 128, 128), V_identb),
                      "pe", 1)
                    MS(f"ptr_{k}", "pe")
                    W(pe, f"pt_{k}")
                    if k >= 2:
                        W(pe, f"hv_{k-2}")
                    E(lambda qt=qt, hv=hv: pe.matmul(
                        PB[5 + par][:, 0:128], BA(B_VSB + 512 * qt + 128 * hv, 128),
                        BA(B_PT, 128), start=True, stop=False))
                    E(lambda qt=qt, hv=hv: pe.matmul(
                        PB[5 + par][:, 0:128], BA(B_VSB + 512 * (qt + 1) + 128 * hv, 128),
                        BA(B_PT + 128, 128), start=False, stop=True), "pe", 1)
                    MS(f"av_{k}", "pe")

        @block.vector
        def _(v):
            for h in range(16):
                for qt in range(8):
                    k = h * 8 + qt
                    par = k % 2
                    W(v, f"s_{k}")
                    E(lambda par=par: v.tensor_copy(out=FA(W2, 256), in_=PB[3 + par][:, 0:256]))
                    moff = 0 if qt == 0 else 256
                    E(lambda moff=moff: v.copy_predicated(
                        out=FA(W2, 256), mask=UA(UO["mask_swa"] + moff, 256),
                        data=V_fillswa))
                    E(lambda: v.reduce_max(out=stat(ST_NEGM), in_=FA(W2, 256),
                                           axis=AX.X, negate=True))
                    E(lambda: v.drain(), "dve", 1)
                    MS(f"sel_{k}", "dve")
                    W(v, f"pf_{k}")
                    E(lambda: v.reciprocal(out=stat(ST_RZ), in_=stat(ST_Z)))
                    E(lambda: v.drain())
                    E(lambda: v.tensor_scalar_mul(out=BA(B_PN, 256), in0=FA(W3, 256),
                                                  scalar1=stat(ST_RZ)), "dve", 1)
                    MS(f"pn_{k}", "dve")
                    W(v, f"ptr_{k}")
                    if k >= 1:
                        W(v, f"av_{k-1}")
                    E(lambda par=par: v.tensor_copy(out=BA(B_PT, 256), in_=PBFS[par][:, 0:256]), "dve", 1)
                    MS(f"pt_{k}", "dve")

        @block.scalar
        def _(sc):
            for h in range(16):
                for qt in range(8):
                    k = h * 8 + qt
                    par = k % 2
                    W(sc, f"sel_{k}")
                    E(lambda: sc.activation(out=FA(W3, 256), in_=FA(W2, 256), func=AF.Exp,
                                            bias=stat(ST_NEGM), scale=1.0,
                                            accum_out=stat(ST_Z)))
                    E(lambda: sc.drain(), "act", 1)
                    MS(f"pf_{k}", "act")
                    W(sc, f"av_{k}")
                    E(lambda h=h, qt=qt, par=par: sc.copy(
                        out=BA(B_HEADS + 1024 * h + 128 * qt, 128),
                        in_=PB[5 + par][:, 0:128]), "act", 1)
                    MS(f"hv_{k}", "act")
            MS("attn_all", "act")

        @block.sync
        def _(q):
            for tt in range(8):
                par = tt % 2
                if tt >= 2:
                    W(q, f"yadd_{tt-2}_3")
                E(lambda tt=tt: q.dma_start(
                    out=FA(W4, D), in_=y_part.ap()[128 * tt:128 * (tt + 1), :]),
                  f"d_yb{par}", 16)
                MS(f"ywb_{tt}", f"d_yb{par}")
                for oc in range(4):
                    W(q, f"yadd_{tt}_{oc}")
                E(lambda tt=tt: q.dma_start(
                    out=y_part.ap()[128 * tt:128 * (tt + 1), :], in_=FA(W4, D)), "d_yw", 16)
                if upto == "p2":
                    E(lambda tt=tt: q.dma_start(
                        out=y_out[128 * tt:128 * (tt + 1), :], in_=FA(W4, D)), "d_yw", 16)
                MS(f"yso_{tt}", "d_yw")

        @block.tensor
        def _(pe):
            W(pe, "wo_ready")
            W(pe, "attn_all")
            for tt in range(8):
                for oc in range(4):
                    kk = tt * 4 + oc
                    if kk >= 2:
                        W(pe, f"yadd_{(kk-2)//4}_{(kk-2)%4}")
                    for h in range(16):
                        base = B_WOL if h < 8 else B_WOH
                        E(lambda tt=tt, oc=oc, h=h, base=base, kk=kk: pe.matmul(
                            PB[0 + kk % 2][:, :],
                            BA(B_HEADS + 1024 * h + 128 * tt, 128),
                            BA(base + 2048 * (h % 8) + 512 * oc, 512),
                            start=(h == 0), stop=(h == 15)),
                          "pe" if h == 15 else None, 1)
                    MS(f"wop_{tt}_{oc}", "pe")

        @block.vector
        def _(v):
            for tt in range(8):
                W(v, f"ywb_{tt}")
                for oc in range(4):
                    kk = tt * 4 + oc
                    W(v, f"wop_{tt}_{oc}")
                    E(lambda oc=oc, kk=kk: v.tensor_add(
                        out=FA(W4 + 512 * oc, 512), in0=PB[0 + kk % 2][:, :],
                        in1=FA(W4 + 512 * oc, 512)))
                    E(lambda: v.drain(), "dve", 1)
                    MS(f"yadd_{tt}_{oc}", "dve")

        if upto == "p2" and DBG_P2:
            @block.sync
            def _(q):
                W(q, f"ktr_{NTH-1}")
                E(lambda: q.dma_start(out=dbg_kfm_a.ap(), in_=BA(B_KFM, 4608)), "d_c", 16)
                W(q, "qtr_31")
                E(lambda: q.dma_start(out=dbg_kfm_b.ap(), in_=BA(B_KFM, 4608)), "d_c", 16)

        for name in ENGS:
            dec = dict(pe=block.tensor, dve=block.vector, act=block.scalar,
                       pool=block.gpsimd, sync=block.sync)[name]
            def mk(name):
                return lambda e: env.bar(e, name, "p2")
            dec(mk(name))

    env_dry = Env(True)
    emit(env_dry, FakeBlock())
    env = Env(False, sems=sems, milestones=env_dry.ms)
    with nc.Block() as block:
        emit(env, block)
    assert env.ctr == env_dry.ctr
    ctx.close()
    return nc, d_in


def host_inputs(inp):
    f32 = np.float32
    x = np.asarray(inp["L0_attn_x"]).reshape(8192, D)
    cores = []
    t_glob = np.arange(-128, 8192 + 0, dtype=np.int64)
    inv = (1.0 / 10000.0 ** (np.arange(0, 128, 2, dtype=f32) / 128.0)).astype(f32)

    identf = np.eye(128, dtype=f32)
    identb = np.eye(128, dtype=ml_dtypes.bfloat16)
    onesr = np.ones((1, 128), dtype=f32)
    ln1w = np.ascontiguousarray(np.asarray(inp["L0_ffn_ln1_w"], f32).reshape(CH, 128).T)
    ln2w = np.ascontiguousarray(np.asarray(inp["L0_ffn_ln2_w"], f32).reshape(CH, 128).T)
    scale_bc = np.broadcast_to(np.asarray(inp["L0_attn_scale"], f32).reshape(1, 1), (128, 1)).copy()
    b_bq = np.broadcast_to(np.asarray(inp["L0_attn_bq"], f32)[None, :], (128, D)).copy()
    b_bk = np.broadcast_to(np.asarray(inp["L0_attn_bk"], f32)[None, :], (128, 512)).copy()
    b_bv = np.broadcast_to(np.asarray(inp["L0_attn_bv"], f32)[None, :], (128, 512)).copy()
    b_bo = np.broadcast_to(np.asarray(inp["L0_attn_bo"], f32)[None, :], (128, D)).copy()
    qqn = np.broadcast_to(np.tile(np.asarray(inp["L0_attn_qn_w"], f32), 4)[None, :], (128, 512)).copy()
    kkn = np.broadcast_to(np.tile(np.asarray(inp["L0_attn_kn_w"], f32), 4)[None, :], (128, 512)).copy()

    ii = np.arange(128)[:, None]
    jj = np.arange(256)[None, :]
    m_std = ~((jj >= ii + 64) & (jj <= ii + 128))
    m_first0 = ~((jj >= np.maximum(ii + 64, 128)) & (jj <= ii + 128))
    m_p1 = np.zeros((4, 128, 512), np.uint8)
    m_p2 = np.zeros((4, 128, 512), np.uint8)
    qq = np.arange(128)[:, None]
    kk = np.arange(512)[None, :]
    for pat in range(4):
        m_p1[pat] = (kk > qq + 128 * pat).astype(np.uint8)
    kk2 = np.arange(128)[:, None]
    qq2 = np.arange(512)[None, :]
    for pat in range(4):
        m_p2[pat] = (128 * pat + kk2 > qq2).astype(np.uint8)
    m_p1 = m_p1.reshape(4 * 128, 512)
    m_p2 = m_p2.reshape(4 * 128, 512)

    Wq_a = np.asarray(inp["L0_attn_Wq"], f32)
    Wk_a = np.asarray(inp["L0_attn_Wk"], f32)
    Wv_a = np.asarray(inp["L0_attn_Wv"], f32)
    Wo_a = np.asarray(inp["L0_attn_Wo"], f32)
    wq_sw = np.ascontiguousarray(
        Wq_a.reshape(4, 512, CH, 128).transpose(0, 3, 2, 1))
    wk_sw = np.ascontiguousarray(Wk_a.reshape(512, CH, 128).transpose(2, 1, 0))
    wv_sw = np.ascontiguousarray(Wv_a.reshape(512, CH, 128).transpose(2, 1, 0))
    wo_sw = np.ascontiguousarray(
        Wo_a.reshape(4, 512, 16, 128).transpose(2, 3, 0, 1))

    Wq_f = np.asarray(inp["L0_ffn_Wq"], f32)
    Wk_f = np.asarray(inp["L0_ffn_Wk"], f32)
    Wv_f = np.asarray(inp["L0_ffn_Wv"], f32)
    Wo_f = np.asarray(inp["L0_ffn_Wo"], f32)
    A_m = np.asarray(inp["L0_ffn_A"], f32)
    B_m = np.asarray(inp["L0_ffn_B"], f32)
    W_in = np.asarray(inp["L0_ffn_W_in_base"], f32)
    W_up = np.asarray(inp["L0_ffn_W_up_base"], f32)
    W_dn = np.asarray(inp["L0_ffn_W_down"], f32)

    a_t = np.ascontiguousarray(A_m.reshape(CH, 128, 16).transpose(1, 0, 2))
    b_t = np.ascontiguousarray(B_m)
    w_in_t = np.ascontiguousarray(
        W_in.reshape(NFC, 128, CH, 128).transpose(0, 3, 2, 1))
    w_up_t = np.ascontiguousarray(
        W_up.reshape(NFC, 128, CH, 128).transpose(0, 3, 2, 1))
    w_dn_t = np.ascontiguousarray(
        W_dn.reshape(4, 512, NFC, 128).transpose(0, 3, 2, 1))
    shared = dict(
        m_p1=m_p1, m_p2=m_p2, identf=identf, identb=identb, onesr=onesr,
        ln1w=ln1w, ln2w=ln2w, scale_bc=scale_bc, b_bq=b_bq, b_bk=b_bk,
        b_bv=b_bv, b_bo=b_bo, qqn=qqn, kkn=kkn,
        wq_sw=wq_sw, wk_sw=wk_sw, wv_sw=wv_sw, wo_sw=wo_sw,
        a_t=a_t, b_t=b_t, w_in_t=w_in_t, w_up_t=w_up_t, w_dn_t=w_dn_t,
    )

    for c in range(NCORE):
        m = dict(shared)
        lo = 1024 * c - 128
        if c == 0:
            xs = np.concatenate([np.zeros((128, D), f32), x[0:1024]], axis=0)
        else:
            xs = x[lo:lo + NTH * 128]
        m["x_sl"] = np.ascontiguousarray(xs)
        pos = np.arange(lo, lo + NTH * 128, dtype=f32)
        fr = pos[:, None] * inv[None, :]
        m["cos_t"] = np.ascontiguousarray(np.tile(np.cos(fr), (1, 4)).astype(f32))
        m["sin_t"] = np.ascontiguousarray(np.tile(np.sin(fr), (1, 4)).astype(f32))
        msk0 = m_first0 if c == 0 else m_std
        m["m_swa"] = np.concatenate([msk0, m_std], axis=0).astype(np.uint8)
        sl = slice(256 * c, 256 * (c + 1))
        m["wq_ff"] = np.ascontiguousarray(Wq_f[sl].reshape(256, CH, 128).transpose(2, 1, 0))
        m["wk_ff"] = np.ascontiguousarray(Wk_f[sl].reshape(256, CH, 128).transpose(2, 1, 0))
        m["wv_ff"] = np.ascontiguousarray(Wv_f[sl].reshape(256, CH, 128).transpose(2, 1, 0))
        m["wo_ff"] = np.ascontiguousarray(Wo_f[:, sl].reshape(CH, 128, 256).transpose(1, 0, 2))
        cores.append(m)
    return cores


def kernel(**inp):
    cores = host_inputs(inp)
    nc, _ = build_nc(upto="full")
    res = run_bass_kernel_spmd(nc, cores, core_ids=list(range(NCORE)))
    y = np.concatenate([res.results[c]["y"] for c in range(NCORE)], axis=0)
    return y.reshape(1, 8192, D)


# version 11 (final)
# speedup vs baseline: 1.2304x; 1.0284x over previous
import numpy as np
import ml_dtypes
import concourse.bass as bass
import concourse.mybir as mybir
from concourse.bass_utils import run_bass_kernel_spmd
from contextlib import ExitStack

F32 = mybir.dt.float32
BF16 = mybir.dt.bfloat16
U8 = mybir.dt.uint8
AF = mybir.ActivationFunctionType
AX = mybir.AxisListType
ALU = mybir.AluOpType

NCORE = 8
D = 2048
TOK = 1024
NTH = 9
NT = 8
CH = 16
F_ = 8192
NFC = 64
SIG = float(1.0 / np.sqrt(128.0))
LN_EPS = 1e-5
RMS2_EPS = 1e-6
QK_EPS = 1e-8
SWA_FILL = -10000.0
MIN_FILL = -1.0e38

ENGS = ("pe", "dve", "act", "pool", "sync")


class Env:

    def __init__(self, dry, sems=None, milestones=None):
        self.dry = dry
        self.ctr = {}
        self.sems = sems or {}
        self.ms = {} if milestones is None else milestones

    def E(self, f, sem=None, k=1):
        if sem is not None:
            self.ctr[sem] = self.ctr.get(sem, 0) + k
        if self.dry:
            return None
        inst = f()
        if sem is not None:
            inst.then_inc(self.sems[sem], k)
        return inst

    def MS(self, name, sem):
        if self.dry:
            assert name not in self.ms, f"dup milestone {name}"
            self.ms[name] = (sem, self.ctr.get(sem, 0))

    def W(self, eng, name):
        if not self.dry:
            sem, v = self.ms[name]
            if v > 0:
                eng.wait_ge(self.sems[sem], v)

    def bar(self, eng, which, tag):
        if which in ("sync", "pe"):
            self.E(lambda: eng.nop(), f"bs_{which}", 1)
        else:
            self.E(lambda: eng.drain(), f"bs_{which}", 1)
        self.MS(f"bar_{tag}_{which}", f"bs_{which}")
        for o in ENGS:
            if o != which:
                self.W(eng, f"bar_{tag}_{o}")


class FakeBlock:
    def _r(self, f):
        f(None)
    def sync(self, f): self._r(f)
    def vector(self, f): self._r(f)
    def scalar(self, f): self._r(f)
    def tensor(self, f): self._r(f)
    def gpsimd(self, f): self._r(f)


F32_TOT = 19688
BF_TOT = 61500
U8_TOT = 4608

FO = dict(
    ident=0, ln1w=128, ln2w=144, scale_bc=160, onesr=161,
    stats=292, fillswa=356, fillmin=612,
    qqn=1124, kkn=1636, b_bk=2148, b_bv=2660,
    b_bq=3172, b_bo=5220,
    w0=7268, w1=9316, w2=11364, w3=13412, w4=15460,
    cs0=17508, cs1=18020,
)
UO = dict(mask_swa=0, mask_p1=512, mask_p2=2560)
FILL2 = 18532
NEGONES = 19556
BO = dict(identb=0, a_bf=128, woap=384)

SEM_NAMES = [
    "d_c", "d_x0", "d_x1", "d_cs0", "d_cs1",
    "d_tz0", "d_tz1", "d_tx0", "d_tx1", "d_tq0", "d_tq1", "d_tk0", "d_tk1",
    "d_xs0", "d_xs1", "d_yp0", "d_yp1",
    "d_ag", "d_w0", "d_sw", "d_wo",
    "d_z0", "d_z1", "d_zs0", "d_zs1", "d_zt4",
    "d_mp", "d_bb", "d_xr0", "d_xr1",
    "d_wi0", "d_wi1", "d_wu0", "d_wu1", "d_wd0", "d_wd1",
    "d_wi2", "d_wi3", "d_wu2", "d_wu3", "d_wd2", "d_wd3",
    "d_wv0", "d_wv1", "d_wv2", "d_wv3", "d_wx0", "d_wx1", "d_wx2", "d_wx3",
    "d_we0", "d_we1", "d_we2", "d_we3",
    "d_yb0", "d_yb1", "d_yw", "d_out",
    "cc1", "cc2",
    "pe", "dve", "act", "pool",
    "bs_pe", "bs_dve", "bs_act", "bs_pool", "bs_sync",
]

B_ZT = 448
B_ZBF = (16832, 18880)
B_XNBF = (20928, 22976)
B_ZTHS = 25024
B_XNT = 448
B_KFM = 18880
B_VSB = 23488
B_WKS = 28096
B_WVS = 36288
B_XN2 = (44480, 46528)
B_KB = (48576, 49088)
B_WQS = 28096
B_QFM = 44480
B_QB = (36288, 36800)
B_PN = 28096
B_PT = 28352
B_HEADS = 448
B_WOL = 16832
B_WOH = 33216
B_KF3 = 448
B_ZTD = (16832, 25024)
B_WKF = 33216
B_WVF = 37312
B_VA = 41408
B_VT = 43584
B_WQF = 33216
B_QS = (37312, 38336)
B_PTB = (43584, 44608)
B_ZTS = 16832
B_WOFB = 448
B_MLPIN = 448
B_ZTHB = 33216
B_WIN = (41408, 43456)
B_WUP = (45504, 47552)
B_BBF = 49600
B_WDN = (41408, 45504)
B_HAMT = 57792
B_MPB = 58816


def build_nc(upto="full"):
    nc = bass.Bass("TRN2", target_bir_lowering=False)
    d_in = {}

    def IN(name, shape, dt=F32):
        d_in[name] = nc.dram_tensor(name, list(shape), dt, kind="ExternalInput").ap()
        return d_in[name]

    x_sl = IN("x_sl", (NTH * 128, D))
    cos_t = IN("cos_t", (NTH * 128, 256))
    sin_t = IN("sin_t", (NTH * 128, 256))
    m_swa = IN("m_swa", (2 * 128, 256), U8)
    m_p1 = IN("m_p1", (4 * 128, 512), U8)
    m_p2 = IN("m_p2", (4 * 128, 512), U8)
    identf_i = IN("identf", (128, 128))
    identb_i = IN("identb", (128, 128), BF16)
    onesr_i = IN("onesr", (1, 128))
    ln1w_i = IN("ln1w", (128, 16))
    ln2w_i = IN("ln2w", (128, 16))
    scale_i = IN("scale_bc", (128, 1))
    b_bq_i = IN("b_bq", (128, D))
    b_bk_i = IN("b_bk", (128, 512))
    b_bv_i = IN("b_bv", (128, 512))
    b_bo_i = IN("b_bo", (128, D))
    qqn_i = IN("qqn", (128, 512))
    kkn_i = IN("kkn", (128, 512))
    wq_sw = IN("wq_sw", (4, 128, CH, 512))
    wk_sw = IN("wk_sw", (128, CH, 512))
    wv_sw = IN("wv_sw", (128, CH, 512))
    wo_sw = IN("wo_sw", (16, 128, 4, 512))
    wq_ff = IN("wq_ff", (128, CH, 256))
    wk_ff = IN("wk_ff", (128, CH, 256))
    wv_ff = IN("wv_ff", (128, CH, 256))
    wo_ff = IN("wo_ff", (128, CH, 256))
    a_t = IN("a_t", (128, CH, 16))
    b_t = IN("b_t", (16, F_))
    w_in_t = IN("w_in_t", (NFC, 128, CH, 128))
    w_up_t = IN("w_up_t", (NFC, 128, CH, 128))
    w_dn_t = IN("w_dn_t", (4, 128, NFC, 512))

    y_out = nc.dram_tensor("y", [TOK, D], F32, kind="ExternalOutput").ap()

    DBG_P2 = False
    DBG_P4 = False
    if DBG_P4:
        dbg_mlpin = nc.dram_tensor("dbg_mlpin", [128, 32768], BF16, kind="ExternalOutput")
        dbg_hamt = nc.dram_tensor("dbg_hamt", [16, 1024], BF16, kind="ExternalOutput")
        dbg_ha = nc.dram_tensor("dbg_ha", [128, 128], F32, kind="ExternalOutput")
    dbgkind = "ExternalOutput" if upto in ("p1",) else "Internal"
    if upto == "p2" and DBG_P2:
        dbg_heads = nc.dram_tensor("dbg_heads", [128, 16384], BF16, kind="ExternalOutput")
        dbg_qfm = nc.dram_tensor("dbg_qfm", [128, 16384], BF16, kind="ExternalOutput")
        dbg_kfm = nc.dram_tensor("dbg_kfm", [128, 4608], BF16, kind="ExternalOutput")
        dbg_vsb = nc.dram_tensor("dbg_vsb", [128, 4608], BF16, kind="ExternalOutput")
        dbg_kfm_a = nc.dram_tensor("dbg_kfm_a", [128, 4608], BF16, kind="ExternalOutput")
        dbg_kfm_b = nc.dram_tensor("dbg_kfm_b", [128, 4608], BF16, kind="ExternalOutput")
    ag_in = nc.dram_tensor("ag_in", [NT * 128, D], BF16, kind=dbgkind)
    ag_out_t = nc.dram_tensor("ag_out", [NCORE * NT * 128, D], BF16, addr_space="Shared")
    ar_in = nc.dram_tensor("ar_in", [64, 2048], F32,
                           kind="ExternalOutput" if upto == "p3" else "Internal")
    ar_out = nc.dram_tensor("ar_out", [8, 2048], F32)
    y_part = nc.dram_tensor("y_part", [TOK, D], F32,
                            kind="ExternalOutput" if upto in ("p1", "p2") else "Internal")
    xn_dram = nc.dram_tensor("xn_dram", [NTH * 128, D], BF16, kind=dbgkind)
    zth_dram = nc.dram_tensor("zth_dram", [2, 128, CH * 512], BF16, kind=dbgkind)
    ag_out = ag_out_t.ap()

    ctx = ExitStack()
    AR_F = ctx.enter_context(nc.sbuf_tensor("arena_f", [128, F32_TOT], F32))
    AR_B = ctx.enter_context(nc.sbuf_tensor("arena_b", [128, BF_TOT], BF16))
    AR_U = ctx.enter_context(nc.sbuf_tensor("arena_u", [128, U8_TOT], U8))
    PBW01 = ctx.enter_context(nc.psum_tensor("pbw01", [128, 1024], F32))
    PBW56 = ctx.enter_context(nc.psum_tensor("pbw56", [128, 1024], F32))
    PB3T = ctx.enter_context(nc.psum_tensor("pb3", [128, 512], F32))
    PB4T = ctx.enter_context(nc.psum_tensor("pb4", [128, 512], F32))
    PBF = ctx.enter_context(nc.psum_tensor("pbf", [128, 1024], BF16))
    PBF2 = ctx.enter_context(nc.psum_tensor("pbf2", [128, 1024], BF16))
    PBFS = (PBF, PBF2)

    class _Half:
        def __init__(self, t, off):
            self.t, self.off = t, off
        def __getitem__(self, idx):
            ps, cs = idx
            a = cs.start or 0
            b_ = 512 if cs.stop is None else cs.stop
            return self.t[ps, self.off + a:self.off + b_]

    PB = {0: _Half(PBW01, 0), 1: _Half(PBW01, 512),
          3: _Half(PB3T, 0), 4: _Half(PB4T, 0),
          5: _Half(PBW56, 0), 6: _Half(PBW56, 512)}
    sems = {}
    for s_ in SEM_NAMES:
        sems[s_] = ctx.enter_context(nc.semaphore(s_))

    def FA(off, n, p0=0, p=128):
        return AR_F[p0:p0 + p, off:off + n]

    def BA(off, n, p0=0, p=128):
        return AR_B[p0:p0 + p, off:off + n]

    def UA(off, n):
        return AR_U[:, off:off + n]

    V_ident = FA(FO["ident"], 128)
    V_identb = BA(BO["identb"], 128)
    V_ln2w = FA(FO["ln2w"], 16)
    V_ln1w = FA(FO["ln1w"], 16)
    V_scale = FA(FO["scale_bc"], 1)
    V_onesr = FA(FO["onesr"], 128, p0=0, p=1)
    V_fillswa = FA(FO["fillswa"], 256)
    V_fillmin = FA(FO["fillmin"], 512)
    V_qqn = FA(FO["qqn"], 512)
    V_kkn = FA(FO["kkn"], 512)
    V_bbk = FA(FO["b_bk"], 512)
    V_bbv = FA(FO["b_bv"], 512)
    V_bbq = FA(FO["b_bq"], D)
    V_bbo = FA(FO["b_bo"], D)
    V_abf = BA(BO["a_bf"], 256)
    V_woap = BA(BO["woap"], 32)
    W0, W1, W2, W3, W4 = (FO["w0"], FO["w1"], FO["w2"], FO["w3"], FO["w4"])
    CS = (FO["cs0"], FO["cs1"])

    def stat(i, n=1, p=128, p0=0):
        return FA(FO["stats"] + i, n, p0=p0, p=p)

    def emit(env, block):
        E, W, MS = env.E, env.W, env.MS

        def emit_zev(sc, j):
            tt = j - 1
            for c in range(CH):
                W(sc, f"ztrp_{j}_{c}")
                E(lambda c=c, tt=tt: sc.copy(
                    out=BA(B_ZT + 2048 * tt + 128 * c, 128),
                    in_=PBFS[c % 2][:, 0:128]), "act", 1)
                MS(f"zev_{j}_{c}", "act")
            MS(f"ztr_{j}", "act")

        def sync_p1_tail(q, j):
            par = j % 2
            W(q, f"xnb_{j}")
            E(lambda j=j, par=par: q.dma_start(
                out=xn_dram.ap()[128 * j:128 * (j + 1), :],
                in_=BA(B_XNBF[par], D)), f"d_xs{par}", 16)
            MS(f"xnsp_{j}", f"d_xs{par}")
            if j >= 1:
                tt = j - 1
                W(q, f"y_{j}")
                E(lambda tt=tt: q.dma_start(
                    out=y_part.ap()[128 * tt:128 * (tt + 1), :], in_=FA(W4, D)),
                  f"d_yp{j%2}", 16)
                MS(f"yst_{j}", f"d_yp{j%2}")
                if upto == "p1":
                    E(lambda tt=tt: q.dma_start(
                        out=y_out[128 * tt:128 * (tt + 1), :], in_=FA(W4, D)), "d_out", 16)


        @block.sync
        def _(q):
            for f in [
                lambda: q.dma_start(out=V_ident, in_=identf_i),
                lambda: q.dma_start(out=V_identb, in_=identb_i),
                lambda: q.dma_start(out=V_ln1w, in_=ln1w_i),
                lambda: q.dma_start(out=V_ln2w, in_=ln2w_i),
                lambda: q.dma_start(out=V_scale, in_=scale_i),
                lambda: q.dma_start(out=V_onesr, in_=onesr_i),
                lambda: q.dma_start(out=V_qqn, in_=qqn_i),
                lambda: q.dma_start(out=V_kkn, in_=kkn_i),
                lambda: q.dma_start(out=V_bbk, in_=b_bk_i),
                lambda: q.dma_start(out=V_bbv, in_=b_bv_i),
                lambda: q.dma_start(out=V_bbq, in_=b_bq_i),
                lambda: q.dma_start(out=V_bbo, in_=b_bo_i),
                lambda: q.dma_start(out=UA(UO["mask_swa"], 512).rearrange("p (m n) -> p m n", m=2),
                                    in_=m_swa.rearrange("(m p) n -> p m n", p=128)),
                lambda: q.dma_start(out=UA(UO["mask_p1"], 2048).rearrange("p (m n) -> p m n", m=4),
                                    in_=m_p1.rearrange("(m p) n -> p m n", p=128)),
                lambda: q.dma_start(out=UA(UO["mask_p2"], 2048).rearrange("p (m n) -> p m n", m=4),
                                    in_=m_p2.rearrange("(m p) n -> p m n", p=128)),
            ]:
                E(f, "d_c", 16)
            MS("consts", "d_c")
            for i in range(NTH):
                par = i % 2
                if i >= 2:
                    W(q, f"p1x_free_{i-2}")
                E(lambda i=i, par=par: q.dma_start(
                    out=FA((W0, W1)[par], D), in_=x_sl[128 * i:128 * (i + 1), :]),
                  f"d_x{par}", 16)
                MS(f"x_{i}", f"d_x{par}")
                if i >= 1:
                    sync_p1_tail(q, i - 1)
            sync_p1_tail(q, NTH - 1)
            W(q, "ztr_8")
            E(lambda: q.dma_start(
                out=ag_in.ap().rearrange("(t p) n -> p t n", p=128),
                in_=BA(B_ZT, NT * D).rearrange("p (t n) -> p t n", t=NT)), "d_ag", 16)
            MS("ag_in_done", "d_ag")
            for c in range(CH):
                W(q, f"zths_{c}")
                E(lambda c=c: q.dma_start(
                    out=zth_dram.ap()[0, :, 512 * c:512 * (c + 1)],
                    in_=BA(B_ZTHS, 512)), "d_mp", 16)
                E(lambda c=c: q.dma_start(
                    out=zth_dram.ap()[1, :, 512 * c:512 * (c + 1)],
                    in_=BA(B_ZTHS + 512, 512)), "d_mp", 16)
                MS(f"zthsp_{c}", "d_mp")

        @block.vector
        def _(v):
            E(lambda: v.memset(V_fillswa, SWA_FILL))
            E(lambda: v.memset(stat(8), 0.0))
            E(lambda: v.memset(stat(9), LN_EPS))
            E(lambda: v.memset(stat(10), RMS2_EPS))
            E(lambda: v.memset(stat(11), QK_EPS))
            E(lambda: v.memset(FA(FILL2, 1024), MIN_FILL))
            E(lambda: v.memset(FA(NEGONES, 128, p=1), -1.0))
            E(lambda: v.memset(V_fillmin, MIN_FILL), "dve", 1)
            MS("fills", "dve")
            W(v, "consts")
            for i in range(NTH):
                par = i % 2
                own = i >= 1
                W(v, f"x_{i}")
                xb = FA((W0, W1)[par], D)
                E(lambda xb=xb: v.reduce_sum(out=stat(0), in_=xb, axis=AX.X))
                E(lambda: v.drain())
                E(lambda: v.tensor_scalar_mul(out=stat(1), in0=stat(0),
                                              scalar1=-1.0 / D))
                E(lambda: v.drain(), "dve", 1)
                MS(f"nm_{i}", "dve")
                W(v, f"sd_{i}")
                if i >= 1:
                    W(v, f"sd2_{i-1}" if i - 1 >= 1 else "xnb_0")
                E(lambda: v.reciprocal(out=stat(3), in_=stat(2)))
                E(lambda: v.drain())
                E(lambda xb=xb: v.tensor_scalar_add(out=FA(W3, D), in0=xb, scalar1=stat(1)))
                E(lambda: v.tensor_scalar_mul(out=FA(W2, D), in0=FA(W3, D), scalar1=stat(3)))
                if own:
                    if i >= 2:
                        W(v, f"yst_{i-1}")
                    E(lambda xb=xb: v.tensor_add(out=FA(W4, D), in0=xb, in1=FA(W2, D)))
                    E(lambda: v.tensor_add(out=FA(W4, D), in0=FA(W4, D), in1=V_bbo))
                E(lambda: v.drain(), "dve", 1)
                MS(f"xn_{i}", "dve")
                MS(f"y_{i}", "dve")
                MS(f"p1x_free_{i}", "dve")
                if own:
                    W(v, f"sd2_{i}")
                    if i >= 3:
                        W(v, f"ztr_{i-2}")
                    E(lambda: v.reciprocal(out=stat(5), in_=stat(4)))
                    E(lambda: v.drain())
                    E(lambda par=par: v.tensor_scalar_mul(
                        out=BA(B_ZBF[par], D), in0=FA(W2, D), scalar1=stat(5)), "dve", 1)
                    MS(f"z_{i}", "dve")
            W(v, "ztr_8")
            zt3 = BA(B_ZT, NT * D).rearrange("p (t n) -> p t n", t=NT)
            for c in range(CH):
                if c >= 1:
                    W(v, f"zthsp_{c-1}")
                E(lambda c=c: v.tensor_scalar_mul(
                    out=BA(B_ZTHS, 1024).rearrange("p (b n) -> p b n", b=8),
                    in0=zt3[:, :, 128 * c:128 * (c + 1)],
                    scalar1=V_ln2w[:, c:c + 1]), "dve", 1)
                MS(f"zths_{c}", "dve")

        @block.scalar
        def _(sc):
            for i in range(NTH):
                par = i % 2
                own = i >= 1
                xb = FA((W0, W1)[par], D)
                W(sc, f"nm_{i}")
                E(lambda xb=xb: sc.activation(out=FA(W3, D), in_=xb, func=AF.Square,
                                              bias=stat(1), scale=1.0, accum_out=stat(6)))
                E(lambda: sc.drain())
                E(lambda: sc.activation(out=stat(2), in_=stat(6), func=AF.Sqrt,
                                        bias=stat(9), scale=1.0 / D))
                E(lambda: sc.drain(), "act", 1)
                MS(f"sd_{i}", "act")
                W(sc, f"xn_{i}")
                if i >= 2:
                    W(sc, f"xnsp_{i-2}")
                E(lambda par=par: sc.copy(out=BA(B_XNBF[par], D), in_=FA(W2, D)))
                if own:
                    E(lambda: sc.activation(out=FA(W3, D), in_=FA(W2, D), func=AF.Square,
                                            bias=stat(8), scale=1.0, accum_out=stat(7)))
                    E(lambda: sc.drain())
                    E(lambda: sc.activation(out=stat(4), in_=stat(7), func=AF.Sqrt,
                                            bias=stat(10), scale=1.0 / D))
                E(lambda: sc.drain(), "act", 1)
                MS(f"xnb_{i}", "act")
                if own:
                    MS(f"sd2_{i}", "act")
                if i >= 2:
                    emit_zev(sc, i - 1)
            emit_zev(sc, NTH - 1)

        @block.tensor
        def _(pe):
            for j in range(1, NTH):
                par = j % 2
                W(pe, f"z_{j}")
                for c in range(CH):
                    if (j, c) != (1, 0) and not (j == 1 and c == 1):
                        pc_, cc_ = (j, c - 2) if c >= 2 else (j - 1, CH - 2 + c)
                        W(pe, f"zev_{pc_}_{cc_}")
                    E(lambda c=c, par=par: pe.transpose(
                        PBFS[c % 2][:, 0:128], BA(B_ZBF[par] + 128 * c, 128),
                        V_identb), "pe", 1)
                    MS(f"ztrp_{j}_{c}", "pe")

        for name in ENGS:
            dec = dict(pe=block.tensor, dve=block.vector, act=block.scalar,
                       pool=block.gpsimd, sync=block.sync)[name]
            def mk(name):
                return lambda e: env.bar(e, name, "p1")
            dec(mk(name))

        if upto == "p1":
            @block.gpsimd
            def _(g):
                n = env.ctr.get("d_out", 0)
                if not env.dry and n:
                    g.wait_ge(sems["d_out"], n)
            return

        emit_p2(env, block)
        if upto == "p2":
            @block.gpsimd
            def _(g):
                n = env.ctr.get("d_yw", 0)
                if not env.dry and n:
                    g.wait_ge(sems["d_yw"], n)
            return
        emit_p3(env, block)
        if upto in ("p3", "p3r"):
            @block.gpsimd
            def _(g):
                if upto == "p3r":
                    W(g, "rs_trig")
                W(g, "ar_in_done")
                n = env.ctr.get("d_yw", 0)
                if not env.dry and n:
                    g.wait_ge(sems["d_yw"], n)
            return
        emit_p4(env, block, stop=(upto if upto in ("p4a", "p4z", "p4y") else None))
        @block.gpsimd
        def _(g):
            if upto == "p4a":
                W(g, "ham_done")
            elif upto == "p4z":
                W(g, "mod_in")
            elif upto == "p4y":
                W(g, "hAev_7")
            else:
                W(g, "y_done")

    ST_MC = 33
    ST_CS3 = 35
    ST_NEGB = 37
    ST_BIAS = 39
    ST_RZ3 = 41

    def emit_p3(env, block):
        E, W, MS = env.E, env.W, env.MS

        def zt_view(buf_off):
            return BA(buf_off, 8192).rearrange("p (b c l) -> p b c l", b=4, c=CH)

        def nkt(qt):
            return 4 * qt + 4

        def p1_iters(qt):
            out = []
            for sub in range(4):
                nf = (128 * (4 * qt + sub)) // 512
                for kc in range(nf + 1):
                    out.append((sub, kc, kc == nf))
            return out

        @block.gpsimd
        def _(g):
            E(lambda: g.dma_start(out=BA(448, 4096).rearrange("p (c j) -> p c j", c=CH),
                                  in_=wo_ff), "d_w0", 16)
            E(lambda: g.dma_start(out=V_abf.rearrange("p (c r) -> p c r", c=CH),
                                  in_=a_t), "d_w0", 16)
            E(lambda: g.dma_start(out=BA(B_WKF, 4096).rearrange("p (c o) -> p c o", c=CH),
                                  in_=wk_ff), "d_w0", 16)
            E(lambda: g.dma_start(out=BA(B_WVF, 4096).rearrange("p (c o) -> p c o", c=CH),
                                  in_=wv_ff), "d_w0", 16)
            MS("p3w", "d_w0")
            E(lambda: g.memset(BA(B_VA, 2176), 1.0))
            E(lambda: g.drain(), "pool", 1)
            MS("va_ones", "pool")
            W(g, "p3kv_pe_done")
            E(lambda: g.dma_start(out=BA(B_WQF, 4096).rearrange("p (c o) -> p c o", c=CH),
                                  in_=wq_ff), "d_w0", 16)
            MS("p3wq", "d_w0")
            if upto not in ("p3",):
                W(g, "ar_in_done")
                E(lambda: g.collective_compute(
                    "ReduceScatter", ALU.add, replica_groups=[list(range(NCORE))],
                    ins=[ar_in.ap().opt()], outs=[ar_out.ap().opt()]), "cc2", 1)
                MS("rs_trig", "cc2")

        @block.tensor
        def _(pe):
            W(pe, "p3w")
            for hh in range(2):
                if hh == 1:
                    W(pe, "woa_ev_0")
                for c in range(CH):
                    E(lambda hh=hh, c=c: pe.matmul(
                        PB[0][:, 0:16], BA(448 + 256 * c + 128 * hh, 128),
                        V_abf[:, 16 * c:16 * (c + 1)],
                        start=(c == 0), stop=(c == CH - 1)),
                      "pe" if c == CH - 1 else None, 1)
                MS(f"woa_{hh}", "pe")
            MS("woa_pe_done", "pe")

        @block.scalar
        def _(sc):
            for hh in range(2):
                W(sc, f"woa_{hh}")
                E(lambda hh=hh: sc.copy(out=V_woap[:, 16 * hh:16 * (hh + 1)],
                                        in_=PB[0][:, 0:16]))
                E(lambda: sc.drain(), "act", 1)
                MS(f"woa_ev_{hh}", "act")

        @block.vector
        def _(v):
            W(v, "p3w")
            for c in range(CH):
                E(lambda c=c: v.tensor_scalar_mul(
                    out=BA(B_WKF + 256 * c, 256), in0=BA(B_WKF + 256 * c, 256),
                    scalar1=V_ln1w[:, c:c + 1]))
                E(lambda c=c: v.tensor_scalar_mul(
                    out=BA(B_WVF + 256 * c, 256), in0=BA(B_WVF + 256 * c, 256),
                    scalar1=V_ln1w[:, c:c + 1]))
            E(lambda: v.drain(), "dve", 1)
            MS("kvf_folded", "dve")

        @block.sync
        def _(q):
            for cc in range(16):
                par = cc % 2
                if cc >= 2:
                    W(q, f"p3kv_used_{cc-2}")
                if cc == 0:
                    W(q, "ag_trig")
                E(lambda cc=cc, par=par: q.dma_start(
                    out=zt_view(B_ZTD[par]),
                    in_=ag_out[512 * cc:512 * (cc + 1), :]
                    .rearrange("(b p) n -> p b n", p=128)), f"d_z{par}", 16)
                MS(f"zt_{cc}", f"d_z{par}")

        @block.tensor
        def _(pe):
            W(pe, "kvf_folded")
            W(pe, "woa_ev_1")
            for cc in range(16):
                par = cc % 2
                W(pe, f"zt_{cc}")
                for h in range(2):
                    if not (cc == 0 and h == 0):
                        pcc, ph = (cc, 0) if h == 1 else (cc - 1, 1)
                        W(pe, f"kev_{pcc}_{ph}")
                        W(pe, f"vev_{pcc}_{ph}")
                        W(pe, f"vaev_{pcc}_{ph}")
                    for c in range(CH):
                        E(lambda h=h, c=c, par=par: pe.matmul(
                            PB[0][:, :], BA(B_WKF + 256 * c + 128 * h, 128),
                            zt_view(B_ZTD[par])[:, :, c, :],
                            start=(c == 0), stop=(c == CH - 1)),
                          "pe" if c == CH - 1 else None, 1)
                    MS(f"kpe3_{cc}_{h}", "pe")
                    for c in range(CH):
                        E(lambda h=h, c=c, par=par: pe.matmul(
                            PB[1][:, :], BA(B_WVF + 256 * c + 128 * h, 128),
                            zt_view(B_ZTD[par])[:, :, c, :],
                            start=(c == 0), stop=(c == CH - 1)),
                          "pe" if c == CH - 1 else None, 1)
                    MS(f"vpe3_{cc}_{h}", "pe")
                    W(pe, f"vev_{cc}_{h}")
                    for j in range(4):
                        E(lambda h=h, j=j: pe.matmul(
                            PB[3][:, 16 * j:16 * (j + 1)], BA(B_VT + 128 * j, 128),
                            V_woap[:, 16 * h:16 * (h + 1)], start=True, stop=True),
                          "pe" if j == 3 else None, 1)
                    MS(f"vape_{cc}_{h}", "pe")
                MS(f"p3kv_used_{cc}", "pe")
            MS("p3kv_pe_done", "pe")

        @block.scalar
        def _(sc):
            W(sc, "va_ones")
            for cc in range(16):
                for h in range(2):
                    W(sc, f"kpe3_{cc}_{h}")
                    E(lambda cc=cc, h=h: sc.copy(
                        out=BA(B_KF3 + 8192 * h + 512 * cc, 512), in_=PB[0][:, :]), "act", 1)
                    MS(f"kev_{cc}_{h}", "act")
                    W(sc, f"vpe3_{cc}_{h}")
                    E(lambda h=h: sc.copy(out=BA(B_VT, 512), in_=PB[1][:, :]), "act", 1)
                    MS(f"vev_{cc}_{h}", "act")
                    W(sc, f"vape_{cc}_{h}")
                    for j in range(4):
                        E(lambda cc=cc, h=h, j=j: sc.copy(
                            out=BA(B_VA + 1088 * h + 17 * (4 * cc + j), 16),
                            in_=PB[3][:, 16 * j:16 * (j + 1)]),
                          "act" if j == 3 else None, 1)
                    MS(f"vaev_{cc}_{h}", "act")

        PW = (PBW01, PBW56)
        s_last = {0: "vev_15_1", 1: None}
        C0 = FO["cs0"]
        KCH = C0
        KPF = C0 + 32
        QM2 = C0 + 96
        QK2 = C0 + 100
        LTS = C0 + 104
        CVS = C0 + 108
        K2B = 45632
        Q2B = 47680

        @block.sync
        def _(q):
            for qt in range(16):
                if qt >= 1:
                    W(q, f"qproj_done_{qt-1}")
                else:
                    W(q, "p3kv_pe_done")
                E(lambda qt=qt: q.dma_start(
                    out=zt_view(B_ZTS),
                    in_=ag_out[512 * qt:512 * (qt + 1), :]
                    .rearrange("(b p) n -> p b n", p=128)), f"d_zs{qt%2}", 16)
                MS(f"zts_{qt}", f"d_zs{qt%2}")
            W(q, "modpreT_ready")
            E(lambda: q.dma_start(
                out=ar_in.ap().rearrange("s (r p) -> r s p", r=16),
                in_=FA(W0, 8192, p=16).rearrange("r (s p) -> r s p", s=64)),
              "d_mp", 16)
            MS("ar_in_done", "d_mp")

        @block.tensor
        def _(pe):
            W(pe, "qf_folded")
            for i in range(32):
                if i == 0:
                    W(pe, "vaev_15_1")
                if i >= 2:
                    W(pe, f"km_{i-2}")
                W(pe, f"k2k_{i}")
                E(lambda i=i: pe.matmul(
                    PB[(3, 4)[i % 2]][0:1, 0:512], BA(B_VA + 16, 1),
                    BA(K2B + 512 * (i % 4), 512),
                    start=True, stop=True), "pe", 1)
                MS(f"csk_{i}", "pe")
            W(pe, "km_30")
            W(pe, "km_31")
            for qt in range(16):
                W(pe, f"zts_{qt}")
                for h in range(2):
                    if qt >= 1 or h >= 1:
                        pqt, ph = (qt, 0) if h == 1 else (qt - 1, 1)
                        W(pe, f"qev_{pqt}_{ph}")
                    for c in range(CH):
                        E(lambda h=h, c=c: pe.matmul(
                            PB[3][:, :], BA(B_WQF + 256 * c + 128 * h, 128),
                            zt_view(B_ZTS)[:, :, c, :],
                            start=(c == 0), stop=(c == CH - 1)),
                          "pe" if c == CH - 1 else None, 1)
                    MS(f"qpe3_{qt}_{h}", "pe")
                for h in range(2):
                    W(pe, f"q2_{qt}_{h}")
                    E(lambda h=h: pe.matmul(
                        PB[(4, 3)[h]][0:1, 0:512], BA(B_VA + 16, 1),
                        BA(Q2B + 512 * h, 512),
                        start=True, stop=True), "pe", 1)
                    MS(f"csq_{qt}_{h}", "pe")
                for h in range(2):
                    np2 = nkt(qt) // 2
                    order = list(range(2 * qt, np2)) + list(range(2 * qt))

                    def emit_av(oj, order=order, np2=np2, qt=qt, h=h):
                        pj = order[oj]
                        W(pe, f"exppt_{qt}_{h}_{pj}")
                        for half in range(2):
                            kt = 2 * pj + half
                            E(lambda kt=kt, pj=pj, half=half, qt=qt, h=h,
                              st=(oj == 0 and half == 0),
                              sp=(oj == np2 - 1 and half == 1): pe.matmul(
                                PB[4][0:17, :], BA(B_VA + 1088 * h + 17 * kt, 17),
                                BA(B_PTB[pj % 2] + 512 * half, 512),
                                start=st, stop=sp),
                              "pe", 1)
                        MS(f"av_{qt}_{h}_{pj}", "pe")

                    for oi, pi in enumerate(order):
                        wb = pi % 2
                        if s_last[wb]:
                            W(pe, s_last[wb])
                        for half in range(2):
                            kt = 2 * pi + half
                            E(lambda qt=qt, h=h, kt=kt, wb=wb, half=half: pe.matmul(
                                PW[wb][:, 512 * half:512 * (half + 1)],
                                BA(B_KF3 + 8192 * h + 128 * kt, 128),
                                BA(B_QS[qt % 2] + 512 * h, 512),
                                start=True, stop=True), "pe", 1)
                        MS(f"s2_{qt}_{h}_{pi}", "pe")
                        s_last[wb] = f"exp_{qt}_{h}_{pi}"
                        if oi == 1:
                            if h == 0:
                                W(pe, f"qev_{qt}_1")
                                W(pe, f"qk2_{qt}_1")
                            W(pe, f"cq_{qt}_{h}")
                            E(lambda h=h: pe.matmul(
                                PB[3][:, 256 + 32 * h:257 + 32 * h],
                                FA(NEGONES, 128, p=1),
                                FA(CVS + h, 1, p=1), start=True, stop=True),
                              "pe", 1)
                            MS(f"biasmm_{qt}_{h}", "pe")
                        if oi >= 1:
                            emit_av(oi - 1)
                    emit_av(np2 - 1)
                    MS(f"avdone_{qt}_{h}", "pe")
                    W(pe, f"oev_{qt}_{h}")
                    for jj in range(4):
                        E(lambda jj=jj: pe.transpose(
                            PB[3][:, 32 * jj:32 * jj + 17],
                            FA(W2 + 128 * jj, 128, p=17), V_ident[0:17, 0:17]),
                          "pe" if jj == 3 else None, 1)
                    MS(f"otr_{qt}_{h}", "pe")
                    W(pe, f"modp_{qt}_{h}")
            if s_last[0]:
                W(pe, s_last[0])
            for s in range(64):
                if s >= 2:
                    W(pe, f"mptev_{s-2}")
                E(lambda s=s: pe.transpose(PB[s % 2][0:16, 0:128],
                                           FA(W4 + 16 * s, 16), V_ident),
                  "pe", 1)
                MS(f"mptr_{s}", "pe")

        @block.vector
        def _(v):
            W(v, "p3wq")
            for c in range(CH):
                E(lambda c=c: v.tensor_scalar_mul(
                    out=BA(B_WQF + 256 * c, 256), in0=BA(B_WQF + 256 * c, 256),
                    scalar1=V_ln1w[:, c:c + 1]))
            E(lambda: v.drain(), "dve", 1)
            MS("qf_folded", "dve")

            def emit_km(i):
                W(v, f"csk_{i}")
                E(lambda i=i: v.reduce_max(
                    out=FA(KCH + 16 * (i % 2) + i // 2, 1, p=1),
                    in_=PB[(3, 4)[i % 2]][0:1, 0:512], axis=AX.X))
                E(lambda: v.drain(), "dve", 1)
                MS(f"km_{i}", "dve")

            for i in range(32):
                cc, h = i // 2, i % 2
                W(v, f"kev_{cc}_{h}")
                if i >= 4:
                    W(v, f"csk_{i-4}")
                E(lambda cc=cc, h=h, i=i: v.tensor_mul(
                    out=BA(K2B + 512 * (i % 4), 512),
                    in0=BA(B_KF3 + 8192 * h + 512 * cc, 512),
                    in1=BA(B_KF3 + 8192 * h + 512 * cc, 512)), "dve", 1)
                MS(f"k2k_{i}", "dve")
                if i >= 2:
                    emit_km(i - 2)
            for i in (30, 31):
                emit_km(i)
            E(lambda: v.drain())
            for h in range(2):
                for qt in range(16):
                    E(lambda h=h, qt=qt: v.reduce_max(
                        out=FA(KPF + 16 * h + qt, 1, p=1),
                        in_=FA(KCH + 16 * h, qt + 1, p=1), axis=AX.X))
                E(lambda: v.drain())
                E(lambda h=h: v.tensor_scalar_mul(
                    out=FA(KPF + 16 * h, 16, p=1),
                    in0=FA(KPF + 16 * h, 16, p=1), scalar1=SIG * SIG))
            E(lambda: v.drain(), "dve", 1)
            MS("kpf_done", "dve")
            for qt in range(16):
                for h in range(2):
                    W(v, f"qev_{qt}_{h}")
                    if qt >= 1:
                        W(v, f"csq_{qt-1}_{h}")
                    E(lambda h=h, qt=qt: v.tensor_mul(
                        out=BA(Q2B + 512 * h, 512),
                        in0=BA(B_QS[qt % 2] + 512 * h, 512),
                        in1=BA(B_QS[qt % 2] + 512 * h, 512)), "dve", 1)
                    MS(f"q2_{qt}_{h}", "dve")
                for h in range(2):
                    W(v, f"csq_{qt}_{h}")
                    E(lambda h=h: v.reduce_max(
                        out=FA(QM2 + h, 1, p=1),
                        in_=PB[(4, 3)[h]][0:1, 0:512], axis=AX.X))
                    E(lambda: v.drain())
                    E(lambda h=h, qt=qt: v.tensor_mul(
                        out=FA(QK2 + h, 1, p=1), in0=FA(QM2 + h, 1, p=1),
                        in1=FA(KPF + 16 * h + qt, 1, p=1)))
                    E(lambda: v.drain(), "dve", 1)
                    MS(f"qk2_{qt}_{h}", "dve")
                for h in range(2):
                    np2 = nkt(qt) // 2
                    for pi in range(2 * qt, np2):
                        wb = pi % 2
                        W(v, f"s2_{qt}_{h}_{pi}")
                        if pi - 1 >= 2 * qt:
                            W(v, f"exp_{qt}_{h}_{pi-1}")
                        E(lambda wb=wb: v.tensor_copy(out=FA(W1, 1024),
                                                      in_=PW[wb][:, 0:1024]))
                        E(lambda qt=qt, pi=pi: v.copy_predicated(
                            out=FA(W1, 1024),
                            mask=UA(UO["mask_p2"] + 1024 * (pi - 2 * qt), 1024),
                            data=FA(FILL2, 1024)), "dve", 1)
                        MS(f"sel2_{qt}_{h}_{pi}", "dve")
                    W(v, f"otr_{qt}_{h}")
                    for jj in range(4):
                        s = 4 * qt + jj
                        E(lambda jj=jj: v.reciprocal(
                            out=stat(ST_RZ3),
                            in_=PB[3][:, 32 * jj + 16:32 * jj + 17]))
                        E(lambda: v.drain())
                        if h == 0:
                            E(lambda jj=jj, s=s: v.tensor_scalar_mul(
                                out=FA(W4 + 16 * s, 16),
                                in0=PB[3][:, 32 * jj:32 * jj + 16],
                                scalar1=stat(ST_RZ3)))
                        else:
                            E(lambda jj=jj, s=s: v.scalar_tensor_tensor(
                                out=FA(W4 + 16 * s, 16),
                                in0=PB[3][:, 32 * jj:32 * jj + 16],
                                scalar=stat(ST_RZ3), in1=FA(W4 + 16 * s, 16),
                                op0=ALU.mult, op1=ALU.add))
                    E(lambda: v.drain(), "dve", 1)
                    MS(f"modp_{qt}_{h}", "dve")
            for s in range(64):
                W(v, f"mptr_{s}")
                E(lambda s=s: v.tensor_copy(out=FA(W0 + 128 * s, 128, p=16),
                                            in_=PB[s % 2][0:16, 0:128]), "dve", 1)
                MS(f"mptev_{s}", "dve")
            MS("modpreT_ready", "dve")

        @block.scalar
        def _(sc):
            for qt in range(16):
                for h in range(2):
                    W(sc, f"qpe3_{qt}_{h}")
                    if qt >= 2 and h == 0:
                        W(sc, f"avdone_{qt-2}_1")
                    E(lambda qt=qt, h=h: sc.copy(
                        out=BA(B_QS[qt % 2] + 512 * h, 512), in_=PB[3][:, :]), "act", 1)
                    MS(f"qev_{qt}_{h}", "act")
                MS(f"qproj_done_{qt}", "act")
                for h in range(2):
                    W(sc, f"qk2_{qt}_{h}")
                    E(lambda h=h: sc.activation(
                        out=FA(LTS + h, 1, p=1), in_=FA(QK2 + h, 1, p=1),
                        func=AF.Ln, bias=stat(8, 1, p=1), scale=1.0))
                    E(lambda: sc.drain())
                    E(lambda h=h: sc.activation(
                        out=FA(CVS + h, 1, p=1), in_=FA(LTS + h, 1, p=1),
                        func=AF.Exp, bias=stat(8, 1, p=1), scale=0.5))
                    E(lambda: sc.drain(), "act", 1)
                    MS(f"cq_{qt}_{h}", "act")
                for h in range(2):
                    W(sc, f"biasmm_{qt}_{h}")
                    E(lambda h=h: sc.copy(out=stat(ST_BIAS + h),
                                          in_=PB[3][:, 256 + 32 * h:257 + 32 * h]))
                    E(lambda: sc.drain(), "act", 1)
                    MS(f"bias_{qt}_{h}", "act")
                    np2 = nkt(qt) // 2
                    order = list(range(2 * qt, np2)) + list(range(2 * qt))
                    for oi, pi in enumerate(order):
                        diag = pi >= 2 * qt
                        if oi >= 2:
                            W(sc, f"av_{qt}_{h}_{order[oi-2]}")
                        if diag:
                            W(sc, f"sel2_{qt}_{h}_{pi}")
                            E(lambda pi=pi, h=h: sc.activation(
                                out=BA(B_PTB[pi % 2], 1024), in_=FA(W1, 1024),
                                func=AF.Exp, bias=stat(ST_BIAS + h), scale=SIG), "act", 1)
                        else:
                            W(sc, f"s2_{qt}_{h}_{pi}")
                            E(lambda pi=pi, h=h, wb=pi % 2: sc.activation(
                                out=BA(B_PTB[pi % 2], 1024),
                                in_=PW[wb][:, 0:1024],
                                func=AF.Exp, bias=stat(ST_BIAS + h), scale=SIG), "act", 1)
                        MS(f"exp_{qt}_{h}_{pi}", "act")
                        MS(f"exppt_{qt}_{h}_{pi}", "act")
                    W(sc, f"avdone_{qt}_{h}")
                    E(lambda: sc.copy(out=FA(W2, 512, p=17), in_=PB[4][0:17, :]), "act", 1)
                    MS(f"oev_{qt}_{h}", "act")

        for name in ENGS:
            dec = dict(pe=block.tensor, dve=block.vector, act=block.scalar,
                       pool=block.gpsimd, sync=block.sync)[name]
            def mk(name):
                return lambda e: env.bar(e, name, "p3")
            dec(mk(name))

    HA_F = FO["w4"] + 1536
    MPRE_F = HA_F + 128
    GS_F = FO["w4"]
    GL_F = FO["w4"] + 512
    MS_F = FO["w4"] + 1024
    OWNX_F = FO["b_bq"]

    def emit_p4(env, block, stop=None):
        E, W, MS = env.E, env.W, env.MS

        @block.sync
        def _(q):
            for b in range(2):
                if stop == "p4z":
                    break
                if b == 1:
                    W(q, "hA_pe_0")
                E(lambda b=b: q.dma_start(out=BA(B_ZTHB, 8192),
                                          in_=zth_dram.ap()[b]), "d_zt4", 16)
                MS(f"zthA_{b}", "d_zt4")
            W(q, "rs_trig")
            E(lambda: q.dma_start(out=FA(OWNX_F, 2048, p=8), in_=ar_out.ap()),
              "d_mp", 16)
            MS("mod_in", "d_mp")
            if stop in ("p4a", "p4z", "p4y"):
                return
            for b in range(2):
                if b == 1 and DBG_P4:
                    W(q, "mlpin_0_63")
                    E(lambda: q.dma_start(out=dbg_mlpin.ap(), in_=BA(B_MLPIN, 32768)), "d_bb", 16)
                    E(lambda: q.dma_start(out=dbg_hamt.ap(), in_=BA(B_HAMT, 1024, p=16)), "d_bb", 16)
                    E(lambda: q.dma_start(out=dbg_ha.ap(), in_=FA(HA_F, 128)), "d_bb", 16)
                    MS("dbg_p4_done", "d_bb")
                W(q, "hA_pe_1" if b == 0 else f"gate_pe_{b-1}")
                E(lambda b=b: q.dma_start(out=BA(B_ZTHB, 8192),
                                          in_=zth_dram.ap()[b]), "d_zt4", 16)
                MS(f"zthB_{b}", "d_zt4")
                for ts in range(4):
                    W(q, "ham_done" if b == 0 else f"yfin_st_{b-1}_{ts}")
                    E(lambda b=b, ts=ts: q.dma_start(
                        out=FA((W0, W1, W2, W3)[ts], D),
                        in_=y_part.ap()[512 * b + 128 * ts:512 * b + 128 * (ts + 1), :]),
                      f"d_yb{ts%2}", 16)
                    MS(f"ybk4_{b}_{ts}", f"d_yb{ts%2}")
                for ts in range(4):
                    for dc in range(4):
                        W(q, f"yadd4_{b}_{dc}_{ts}")
                    E(lambda b=b, ts=ts: q.dma_start(
                        out=y_out[512 * b + 128 * ts:512 * b + 128 * (ts + 1), :],
                        in_=FA((W0, W1, W2, W3)[ts], D)), "d_out", 16)
                    MS(f"yfin_st_{b}_{ts}", "d_out")
            MS("y_done", "d_out")

        @block.gpsimd
        def _(g):
            E(lambda: g.dma_start(out=BA(B_BBF, F_, p=16), in_=b_t), "d_bb", 16)
            MS("bbf", "d_bb")
            if stop in ("p4a", "p4z", "p4y"):
                return

            for b in range(2):
                for fc in range(NFC):
                    par = fc % 2
                    if (b, fc) != (0, 0) and not (b == 1 and fc == 0):
                        if fc >= 2:
                            W(g, f"mlpin_{b}_{fc-2}")
                    if b == 1 and fc < 2:
                        W(g, f"mlpin_{0}_{NFC-2+fc}")
                    if b == 1 and fc == 0:
                        W(g, "wdd_0_3")
                    E(lambda b=b, fc=fc, par=par: g.dma_start(
                        out=BA(B_WIN[par], 2048).rearrange("p (c f) -> p c f", c=CH),
                        in_=w_in_t[fc]), f"d_wi{par}", 16)
                    E(lambda b=b, fc=fc, par=par: g.dma_start(
                        out=BA(B_WUP[par], 2048).rearrange("p (c f) -> p c f", c=CH),
                        in_=w_up_t[fc]), f"d_wu{par}", 16)
                    MS(f"wio_{b}_{fc}", f"d_wi{par}")
                    MS(f"wup_{b}_{fc}", f"d_wu{par}")
                for dc in range(4):
                    for pc in range(8):
                        par = pc % 2
                        k = dc * 8 + pc
                        if k == 0:
                            W(g, f"gate_pe_{b}")
                        if k >= 2:
                            W(g, f"wdn_used_{b}_{k-2}")
                        E(lambda b=b, dc=dc, pc=pc, par=par: g.dma_start(
                            out=BA(B_WDN[par], 4096).rearrange("p (f n) -> p f n", f=8),
                            in_=w_dn_t[dc, :, 8 * pc:8 * (pc + 1), :]),
                          f"d_wd{par}", 16)
                        MS(f"wdn_{b}_{k}", f"d_wd{par}")

        @block.tensor
        def _(pe):
            if stop == "p4z":
                return
            for b in range(2):
                W(pe, f"zthA_{b}")
                for tsl in range(4):
                    tt = 4 * b + tsl
                    if tt >= 1:
                        W(pe, f"hAev_{tt-1}")
                    for c in range(CH):
                        E(lambda c=c, tsl=tsl: pe.matmul(
                            PB[6][:, 0:16], BA(B_ZTHB + 512 * c + 128 * tsl, 128),
                            V_abf[:, 16 * c:16 * (c + 1)],
                            start=(c == 0), stop=(c == CH - 1)),
                          "pe" if c == CH - 1 else None, 1)
                    MS(f"hA_{tt}", "pe")
                MS(f"hA_pe_{b}", "pe")
            if stop == "p4y":
                return
            W(pe, "mod_in")
            for r in range(16):
                if r >= 2:
                    W(pe, f"mprev_{r-2}")
                E(lambda r=r: pe.transpose(PB[r % 2][:, 0:8],
                                           FA(OWNX_F + 128 * r, 128, p=8),
                                           V_ident[0:8, 0:8]), "pe", 1)
                MS(f"mptr4_{r}", "pe")
            W(pe, "hambf")
            for tt in range(8):
                if tt >= 2:
                    W(pe, f"hamtev_{tt-2}")
                E(lambda tt=tt: pe.transpose(
                    PBFS[tt % 2][0:16, 0:128],
                    BA(B_MPB + 16 * tt, 16), V_identb), "pe", 1)
                MS(f"hamtr_{tt}", "pe")
            if stop == "p4a":
                return
            for b in range(2):
                W(pe, f"zthB_{b}")
                W(pe, "ham_done")
                if b == 1:
                    for ts in range(4):
                        W(pe, f"yadd4_0_3_{ts}")
                    if DBG_P4:
                        W(pe, "dbg_p4_done")
                for fc in range(NFC):
                    par = fc % 2
                    W(pe, f"wio_{b}_{fc}")
                    W(pe, f"wup_{b}_{fc}")
                    if fc >= 2:
                        W(pe, f"gadd_{b}_{fc-2}")
                        W(pe, f"mlpin_{b}_{fc-2}")
                        W(pe, f"mev_{b}_{fc-2}")
                    for c in range(CH):
                        E(lambda c=c, par=par: pe.matmul(
                            PB[0 + par][:, :], BA(B_WIN[par] + 128 * c, 128),
                            BA(B_ZTHB + 512 * c, 512),
                            start=(c == 0), stop=(c == CH - 1)),
                          "pe" if c == CH - 1 else None, 1)
                    E(lambda b=b, fc=fc, par=par: pe.matmul(
                        PB[5 + par][:, :], BA(B_BBF + 128 * fc, 128, p=16),
                        BA(B_HAMT + 512 * b, 512, p=16), start=True, stop=True),
                      "pe", 1)
                    for c in range(CH):
                        E(lambda c=c, par=par: pe.matmul(
                            PB[3 + par][:, :], BA(B_WUP[par] + 128 * c, 128),
                            BA(B_ZTHB + 512 * c, 512),
                            start=(c == 0), stop=(c == CH - 1)),
                          "pe" if c == CH - 1 else None, 1)
                    MS(f"gup_{b}_{fc}", "pe")
                MS(f"gate_pe_{b}", "pe")
                for dc in range(4):
                    for pc in range(8):
                        k = dc * 8 + pc
                        par = pc % 2
                        W(pe, f"wdn_{b}_{k}")
                        if k == 0:
                            W(pe, f"mlpin_{b}_{NFC-1}")
                            W(pe, f"gadd_{b}_{NFC-1}")
                            W(pe, f"gadd_{b}_{NFC-2}")
                            W(pe, f"mev_{b}_{NFC-1}")
                            W(pe, f"mev_{b}_{NFC-2}")
                        if pc == 0 and dc >= 1:
                            for ts in range(4):
                                W(pe, f"yadd4_{b}_{dc-1}_{ts}")
                        for ts in range(4):
                            for fcl in range(8):
                                fc = 8 * pc + fcl
                                E(lambda ts=ts, fcl=fcl, fc=fc, par=par: pe.matmul(
                                    PB[(0, 1, 3, 4)[ts]][:, :],
                                    BA(B_MLPIN + 512 * fc + 128 * ts, 128),
                                    BA(B_WDN[par] + 512 * fcl, 512),
                                    start=(fc == 0), stop=(fc == NFC - 1)),
                                  "pe" if fcl == 7 else None, 1)
                        MS(f"wdnp_{b}_{k}", "pe")
                        MS(f"wdn_used_{b}_{k}", "pe")
                    MS(f"wdd_{b}_{dc}", "pe")

        @block.vector
        def _(v):
            if stop == "p4z":
                return
            for b in range(2):
                for tsl in range(4):
                    tt = 4 * b + tsl
                    W(v, f"hA_{tt}")
                    E(lambda tt=tt: v.tensor_copy(out=FA(HA_F + 16 * tt, 16),
                                                  in_=PB[6][:, 0:16]), "dve", 1)
                    MS(f"hAev_{tt}", "dve")
            if stop == "p4y":
                return
            for r in range(16):
                W(v, f"mptr4_{r}")
                E(lambda r=r: v.tensor_copy(
                    out=AR_F[:, MPRE_F + r:MPRE_F + 128:16],
                    in_=PB[r % 2][:, 0:8]), "dve", 1)
                MS(f"mprev_{r}", "dve")
            W(v, "tanh_done")
            E(lambda: v.scalar_tensor_tensor(out=FA(MPRE_F, 128), in0=FA(FO["w3"], 128),
                                             scalar=0.1, in1=FA(HA_F, 128),
                                             op0=ALU.mult, op1=ALU.mult))
            E(lambda: v.tensor_copy(out=BA(B_MPB, 128), in_=FA(MPRE_F, 128)))
            E(lambda: v.drain(), "dve", 1)
            MS("hambf", "dve")
            for tt in range(8):
                W(v, f"hamtr_{tt}")
                E(lambda tt=tt: v.tensor_copy(
                    out=BA(B_HAMT + 128 * tt, 128, p=16),
                    in_=PBFS[tt % 2][0:16, 0:128]), "dve", 1)
                MS(f"hamtev_{tt}", "dve")
            MS("ham_done", "dve")
            if stop == "p4a":
                return
            for b in range(2):
                for fc in range(NFC):
                    par = fc % 2
                    W(v, f"gup_{b}_{fc}")
                    W(v, f"mev_{b}_{fc}")
                    if fc >= 1:
                        W(v, f"silu_{b}_{fc-1}")
                    elif b == 1:
                        W(v, f"silu_0_{NFC-1}")
                    E(lambda par=par: v.tensor_add(out=FA(GS_F, 512),
                                                   in0=PB[0 + par][:, :],
                                                   in1=FA(MS_F, 512)), "dve", 1)
                    MS(f"gadd_{b}_{fc}", "dve")
                    W(v, f"silu_{b}_{fc}")
                    E(lambda fc=fc, par=par: v.tensor_mul(
                        out=BA(B_MLPIN + 512 * fc, 512), in0=PB[3 + par][:, :],
                        in1=FA(GL_F, 512)), "dve", 1)
                    MS(f"mlpin_{b}_{fc}", "dve")
                for dc in range(4):
                    for ts in range(4):
                        W(v, f"wdd_{b}_{dc}") if ts == 0 else None
                        W(v, f"ybk4_{b}_{ts}")
                        E(lambda b=b, dc=dc, ts=ts: v.tensor_add(
                            out=FA((W0, W1, W2, W3)[ts] + 512 * dc, 512),
                            in0=PB[(0, 1, 3, 4)[ts]][:, :],
                            in1=FA((W0, W1, W2, W3)[ts] + 512 * dc, 512)), "dve", 1)
                        MS(f"yadd4_{b}_{dc}_{ts}", "dve")

        @block.scalar
        def _(sc):
            if stop in ("p4z", "p4y"):
                return
            W(sc, "mprev_15")
            E(lambda: sc.activation(out=FA(FO["w3"], 128), in_=FA(MPRE_F, 128),
                                    func=AF.Tanh, bias=stat(8), scale=1.0))
            E(lambda: sc.drain(), "act", 1)
            MS("tanh_done", "act")
            if stop == "p4a":
                return
            for b in range(2):
                for fc in range(NFC):
                    par = fc % 2
                    W(sc, f"gup_{b}_{fc}")
                    E(lambda par=par: sc.copy(out=FA(MS_F, 512), in_=PB[5 + par][:, :]), "act", 1)
                    MS(f"mev_{b}_{fc}", "act")
                    W(sc, f"gadd_{b}_{fc}")
                    E(lambda: sc.activation(out=FA(GL_F, 512), in_=FA(GS_F, 512),
                                            func=AF.Silu, bias=stat(8), scale=1.0), "act", 1)
                    MS(f"silu_{b}_{fc}", "act")

    def emit_p2(env, block):
        E, W, MS = env.E, env.W, env.MS
        ST_KS = 16
        ST_SD4 = 20
        ST_R4 = 24
        ST_NEGM = 28
        ST_Z = 29
        ST_RZ = 30

        def rope_dve(v, src_off, cs_off, dst_off, n4):
            se = AR_F[:, src_off:src_off + 512:2]
            so = AR_F[:, src_off + 1:src_off + 512:2]
            de = AR_F[:, dst_off:dst_off + 512:2]
            do = AR_F[:, dst_off + 1:dst_off + 512:2]
            cc_ = FA(cs_off, 256)
            ss_ = FA(cs_off + 256, 256)
            t1 = FA(W3, 256)
            t2 = FA(W3 + 256, 256)
            E(lambda: v.tensor_mul(out=t1, in0=se, in1=cc_))
            E(lambda: v.tensor_mul(out=t2, in0=so, in1=ss_))
            E(lambda: v.tensor_sub(out=de, in0=t1, in1=t2))
            E(lambda: v.tensor_mul(out=t1, in0=se, in1=ss_))
            E(lambda: v.tensor_mul(out=t2, in0=so, in1=cc_))
            E(lambda: v.tensor_add(out=do, in0=t1, in1=t2))

        @block.gpsimd
        def _(g):
            W(g, "ag_in_done")
            E(lambda: g.collective_compute(
                "AllGather", ALU.bypass, replica_groups=[list(range(NCORE))],
                ins=[ag_in.ap().opt()], outs=[ag_out.opt()]), "cc1", 1)
            MS("ag_trig", "cc1")
            E(lambda: g.dma_start(out=BA(B_WKS, 8192).rearrange("p (c o) -> p c o", c=CH),
                                  in_=wk_sw), "d_sw", 16)
            E(lambda: g.dma_start(out=BA(B_WVS, 8192).rearrange("p (c o) -> p c o", c=CH),
                                  in_=wv_sw), "d_sw", 16)
            MS("kvw", "d_sw")
            for hg in range(4):
                if hg >= 1:
                    W(g, f"qpe_all_{hg-1}")
                else:
                    W(g, "p2kv_pe_done")
                E(lambda hg=hg: g.dma_start(
                    out=BA(B_WQS, 8192).rearrange("p (c o) -> p c o", c=CH),
                    in_=wq_sw[hg]), "d_sw", 16)
                MS(f"wq_{hg}", "d_sw")
            W(g, "attn_all")
            if upto == "p2" and DBG_P2:
                E(lambda: g.dma_start(out=dbg_heads.ap(), in_=BA(B_HEADS, 16384)), "d_out", 16)
                E(lambda: g.dma_start(out=dbg_qfm.ap(), in_=BA(B_QFM, 16384)), "d_out", 16)
                E(lambda: g.dma_start(out=dbg_kfm.ap(), in_=BA(B_KFM, 4608)), "d_out", 16)
                E(lambda: g.dma_start(out=dbg_vsb.ap(), in_=BA(B_VSB, 4608)), "d_out", 16)
                MS("dbgp2_done", "d_out")
                W(g, "dbgp2_done")
            E(lambda: g.dma_start(
                out=BA(B_WOL, 16384).rearrange("p (h n) -> p h n", h=8),
                in_=wo_sw[0:8].rearrange("h p a o -> p h (a o)")), "d_wo", 16)
            E(lambda: g.dma_start(
                out=BA(B_WOH, 16384).rearrange("p (h n) -> p h n", h=8),
                in_=wo_sw[8:16].rearrange("h p a o -> p h (a o)")), "d_wo", 16)
            MS("wo_ready", "d_wo")

        @block.sync
        def _(q):
            for i in range(NTH):
                par = i % 2
                if i >= 2:
                    W(q, f"xnT_{i-2}")
                    W(q, f"krope_{i-2}")
                E(lambda i=i, par=par: q.dma_start(
                    out=BA(B_XN2[par], D), in_=xn_dram.ap()[128 * i:128 * (i + 1), :]),
                  f"d_xr{par}", 16)
                MS(f"xr_{i}", f"d_xr{par}")
                E(lambda i=i, par=par: q.dma_start(
                    out=FA(CS[par], 256), in_=cos_t[128 * i:128 * (i + 1), :]),
                  f"d_cs{par}", 16)
                E(lambda i=i, par=par: q.dma_start(
                    out=FA(CS[par] + 256, 256), in_=sin_t[128 * i:128 * (i + 1), :]),
                  f"d_cs{par}", 16)
                MS(f"cs_{i}", f"d_cs{par}")


        pbfu = {0: None, 1: None}

        def pbf_tr(pe, idx, msname, f):
            if pbfu[idx]:
                W(pe, pbfu[idx])
            E(f, "pe", 1)
            MS(msname + "_p", "pe")

        @block.tensor
        def _(pe):
            for i in range(NTH):
                par = i % 2
                W(pe, f"xr_{i}")
                for c in range(CH):
                    pbf_tr(pe, c % 2, f"xtr_{i}_{c}",
                           lambda c=c, par=par: pe.transpose(
                               PBFS[c % 2][:, 0:128],
                               BA(B_XN2[par] + 128 * c, 128), V_identb))
                    pbfu[c % 2] = f"xtr_{i}_{c}_e"
                W(pe, "kvw")
                W(pe, f"xnT_{i}")
                if i >= 1:
                    W(pe, f"kc_{i-1}")
                    W(pe, f"vc_{i-1}")
                for c in range(CH):
                    E(lambda i=i, c=c: pe.matmul(
                        PB[0][:, :], BA(B_XNT + 1152 * c + 128 * i, 128),
                        BA(B_WKS + 512 * c, 512), start=(c == 0), stop=(c == CH - 1)),
                      "pe" if c == CH - 1 else None, 1)
                MS(f"kpe_{i}", "pe")
                for c in range(CH):
                    E(lambda i=i, c=c: pe.matmul(
                        PB[1][:, :], BA(B_XNT + 1152 * c + 128 * i, 128),
                        BA(B_WVS + 512 * c, 512), start=(c == 0), stop=(c == CH - 1)),
                      "pe" if c == CH - 1 else None, 1)
                MS(f"vpe_{i}", "pe")
                if i >= 1:
                    W(pe, f"krope_{i-1}")
                    for hv in range(4):
                        pbf_tr(pe, hv % 2, f"ktr_{i-1}_{hv}",
                               lambda hv=hv, p2=(i - 1) % 2: pe.transpose(
                                   PBFS[hv % 2][:, 0:128],
                                   BA(B_KB[p2] + 128 * hv, 128), V_identb))
                        pbfu[hv % 2] = f"ktr_{i-1}_{hv}_e"
            W(pe, f"krope_{NTH-1}")
            for hv in range(4):
                pbf_tr(pe, hv % 2, f"ktr_{NTH-1}_{hv}",
                       lambda hv=hv, p2=(NTH - 1) % 2: pe.transpose(
                           PBFS[hv % 2][:, 0:128],
                           BA(B_KB[p2] + 128 * hv, 128), V_identb))
                pbfu[hv % 2] = f"ktr_{NTH-1}_{hv}_e"
            MS("p2kv_pe_done", "pe")

        @block.vector
        def _(v):
            for i in range(NTH):
                par = i % 2
                W(v, f"kpe_{i}")
                E(lambda: v.tensor_add(out=FA(W1, 512), in0=PB[0][:, :], in1=V_bbk), "dve", 1)
                MS(f"kc_{i}", "dve")
                W(v, f"ksd_{i}")
                if i >= 2:
                    W(v, f"ktr_{i-2}")
                if i >= 1:
                    W(v, f"vsb_{i-1}")
                E(lambda: v.reciprocal(out=stat(ST_R4, 4), in_=stat(ST_SD4, 4)))
                E(lambda: v.drain())
                for hv in range(4):
                    E(lambda hv=hv: v.tensor_scalar_mul(
                        out=FA(W2 + 128 * hv, 128), in0=FA(W1 + 128 * hv, 128),
                        scalar1=stat(ST_R4 + hv)))
                E(lambda: v.tensor_mul(out=FA(W2, 512), in0=FA(W2, 512), in1=V_kkn))
                W(v, f"cs_{i}")
                rope_dve(v, W2, CS[par], W1, 4)
                E(lambda par=par: v.tensor_copy(out=BA(B_KB[par], 512), in_=FA(W1, 512)), "dve", 1)
                MS(f"krope_{i}", "dve")
                W(v, f"vpe_{i}")
                E(lambda: v.tensor_add(out=FA(W2, 512), in0=PB[1][:, :], in1=V_bbv), "dve", 1)
                MS(f"vc_{i}", "dve")

        @block.scalar
        def _(sc):
            for i in range(NTH):
                for c in range(CH):
                    W(sc, f"xtr_{i}_{c}_p")
                    E(lambda i=i, c=c: sc.copy(
                        out=BA(B_XNT + 1152 * c + 128 * i, 128),
                        in_=PBFS[c % 2][:, 0:128]), "act", 1)
                    MS(f"xtr_{i}_{c}_e", "act")
                MS(f"xnT_{i}", "act")
                W(sc, f"kc_{i}")
                for hv in range(4):
                    E(lambda hv=hv: sc.activation(
                        out=FA(W3, 128), in_=FA(W1 + 128 * hv, 128), func=AF.Square,
                        bias=stat(8), scale=1.0, accum_out=stat(ST_KS + hv)))
                E(lambda: sc.drain())
                E(lambda: sc.activation(out=stat(ST_SD4, 4), in_=stat(ST_KS, 4),
                                        func=AF.Sqrt, bias=stat(11), scale=1.0 / 128))
                E(lambda: sc.drain(), "act", 1)
                MS(f"ksd_{i}", "act")
                W(sc, f"vc_{i}")
                E(lambda i=i: sc.copy(out=BA(B_VSB + 512 * i, 512), in_=FA(W2, 512)), "act", 1)
                MS(f"vsb_{i}", "act")
                if i >= 1:
                    for hv in range(4):
                        W(sc, f"ktr_{i-1}_{hv}_p")
                        E(lambda i=i, hv=hv: sc.copy(
                            out=BA(B_KFM + 1152 * hv + 128 * (i - 1), 128),
                            in_=PBFS[hv % 2][:, 0:128]), "act", 1)
                        MS(f"ktr_{i-1}_{hv}_e", "act")
                    MS(f"ktr_{i-1}", "act")
            for hv in range(4):
                W(sc, f"ktr_{NTH-1}_{hv}_p")
                E(lambda hv=hv: sc.copy(
                    out=BA(B_KFM + 1152 * hv + 128 * (NTH - 1), 128),
                    in_=PBFS[hv % 2][:, 0:128]), "act", 1)
                MS(f"ktr_{NTH-1}_{hv}_e", "act")
            MS(f"ktr_{NTH-1}", "act")

        @block.sync
        def _(q):
            for hg in range(4):
                for ti, i in enumerate(range(1, NTH)):
                    k = hg * 8 + ti
                    par = k % 2
                    if k >= 2:
                        W(q, f"qrope_{k-2}")
                    else:
                        W(q, f"krope_{NTH-1}")
                    E(lambda i=i, par=par: q.dma_start(
                        out=FA(CS[par], 256), in_=cos_t[128 * i:128 * (i + 1), :]),
                      f"d_cs{par}", 16)
                    E(lambda i=i, par=par: q.dma_start(
                        out=FA(CS[par] + 256, 256), in_=sin_t[128 * i:128 * (i + 1), :]),
                      f"d_cs{par}", 16)
                    MS(f"qcs_{k}", f"d_cs{par}")

        @block.tensor
        def _(pe):
            for hg in range(4):
                W(pe, f"wq_{hg}")
                for ti, i in enumerate(range(1, NTH)):
                    k = hg * 8 + ti
                    if k >= 1:
                        W(pe, f"qc_{k-1}")
                    for c in range(CH):
                        E(lambda i=i, c=c: pe.matmul(
                            PB[3][:, :], BA(B_XNT + 1152 * c + 128 * i, 128),
                            BA(B_WQS + 512 * c, 512), start=(c == 0), stop=(c == CH - 1)),
                          "pe" if c == CH - 1 else None, 1)
                    MS(f"qpe_{k}", "pe")
                    if k >= 1:
                        W(pe, f"qrope_{k-1}")
                        for hh in range(4):
                            pbf_tr(pe, hh % 2, f"qtr_{k-1}_{hh}",
                                   lambda hh=hh, p2=(k - 1) % 2: pe.transpose(
                                       PBFS[hh % 2][:, 0:128],
                                       BA(B_QB[p2] + 128 * hh, 128), V_identb))
                            pbfu[hh % 2] = f"qtr_{k-1}_{hh}_e"
                MS(f"qpe_all_{hg}", "pe")
            W(pe, "qrope_31")
            for hh in range(4):
                pbf_tr(pe, hh % 2, f"qtr_31_{hh}",
                       lambda hh=hh: pe.transpose(
                           PBFS[hh % 2][:, 0:128],
                           BA(B_QB[31 % 2] + 128 * hh, 128), V_identb))
                pbfu[hh % 2] = f"qtr_31_{hh}_e"

        @block.vector
        def _(v):
            for hg in range(4):
                for ti, i in enumerate(range(1, NTH)):
                    k = hg * 8 + ti
                    par = k % 2
                    W(v, f"qpe_{k}")
                    E(lambda hg=hg: v.tensor_add(out=FA(W0, 512), in0=PB[3][:, :],
                                                 in1=V_bbq[:, 512 * hg:512 * (hg + 1)]), "dve", 1)
                    MS(f"qc_{k}", "dve")
                    W(v, f"qsd_{k}")
                    if k >= 2:
                        W(v, f"qtr_{k-2}")
                    E(lambda: v.reciprocal(out=stat(ST_R4, 4), in_=stat(ST_SD4, 4)))
                    E(lambda: v.drain())
                    for hh in range(4):
                        E(lambda hh=hh: v.tensor_scalar_mul(
                            out=FA(W2 + 128 * hh, 128), in0=FA(W0 + 128 * hh, 128),
                            scalar1=stat(ST_R4 + hh)))
                    E(lambda: v.tensor_mul(out=FA(W2, 512), in0=FA(W2, 512), in1=V_qqn))
                    W(v, f"qcs_{k}")
                    rope_dve(v, W2, CS[par], W0, 4)
                    E(lambda: v.tensor_scalar_mul(out=FA(W0, 512), in0=FA(W0, 512),
                                                  scalar1=V_scale))
                    E(lambda par=par: v.tensor_copy(out=BA(B_QB[par], 512), in_=FA(W0, 512)), "dve", 1)
                    MS(f"qrope_{k}", "dve")

        @block.scalar
        def _(sc):
            for hg in range(4):
                for ti in range(8):
                    k = hg * 8 + ti
                    W(sc, f"qc_{k}")
                    for hh in range(4):
                        E(lambda hh=hh: sc.activation(
                            out=FA(W3, 128), in_=FA(W0 + 128 * hh, 128), func=AF.Square,
                            bias=stat(8), scale=1.0, accum_out=stat(ST_KS + hh)))
                    E(lambda: sc.drain())
                    E(lambda: sc.activation(out=stat(ST_SD4, 4), in_=stat(ST_KS, 4),
                                            func=AF.Sqrt, bias=stat(11), scale=1.0 / 128))
                    E(lambda: sc.drain(), "act", 1)
                    MS(f"qsd_{k}", "act")
                    if k >= 1:
                        kk = k - 1
                        hgp, tip = kk // 8, kk % 8
                        for hh in range(4):
                            W(sc, f"qtr_{kk}_{hh}_p")
                            E(lambda hgp=hgp, hh=hh, tip=tip: sc.copy(
                                out=BA(B_QFM + 1024 * (4 * hgp + hh) + 128 * tip, 128),
                                in_=PBFS[hh % 2][:, 0:128]), "act", 1)
                            MS(f"qtr_{kk}_{hh}_e", "act")
                        MS(f"qtr_{kk}", "act")
            kk = 31
            for hh in range(4):
                W(sc, f"qtr_{kk}_{hh}_p")
                E(lambda hh=hh: sc.copy(
                    out=BA(B_QFM + 1024 * (4 * 3 + hh) + 128 * 7, 128),
                    in_=PBFS[hh % 2][:, 0:128]), "act", 1)
                MS(f"qtr_{kk}_{hh}_e", "act")
            MS(f"qtr_{kk}", "act")

        @block.tensor
        def _(pe):
            def emit_tail(j):
                parj = j % 2
                qtj = j % 8
                hvj = (j // 8) // 4
                W(pe, f"pn_{j}")
                if j >= 2:
                    W(pe, f"pt_{j-2}")
                E(lambda parj=parj: pe.transpose(PBFS[parj][:, 0:128],
                                                 BA(B_PN, 128), V_identb))
                E(lambda parj=parj: pe.transpose(PBFS[parj][:, 128:256],
                                                 BA(B_PN + 128, 128), V_identb),
                  "pe", 1)
                MS(f"ptr_{j}", "pe")
                W(pe, f"pt_{j}")
                if j >= 2:
                    W(pe, f"hv_{j-2}")
                E(lambda qtj=qtj, hvj=hvj: pe.matmul(
                    PB[5 + (j % 2)][:, 0:128], BA(B_VSB + 512 * qtj + 128 * hvj, 128),
                    BA(B_PT, 128), start=True, stop=False))
                E(lambda qtj=qtj, hvj=hvj: pe.matmul(
                    PB[5 + (j % 2)][:, 0:128],
                    BA(B_VSB + 512 * (qtj + 1) + 128 * hvj, 128),
                    BA(B_PT + 128, 128), start=False, stop=True), "pe", 1)
                MS(f"av_{j}", "pe")

            first = True
            for h in range(16):
                hv = h // 4
                for qt in range(8):
                    k = h * 8 + qt
                    if first:
                        W(pe, "qtr_31")
                        W(pe, "qtr_30")
                        W(pe, "ktr_8")
                        W(pe, "ktr_7")
                        W(pe, "vsb_8")
                        first = False
                    par = k % 2
                    if k >= 2:
                        W(pe, f"sel_{k-2}")
                    E(lambda h=h, qt=qt, hv=hv, par=par: pe.matmul(
                        PB[3 + par][:, 0:256], BA(B_QFM + 1024 * h + 128 * qt, 128),
                        BA(B_KFM + 1152 * hv + 128 * qt, 256),
                        start=True, stop=True), "pe", 1)
                    MS(f"s_{k}", "pe")
                    if k >= 1:
                        emit_tail(k - 1)
            emit_tail(127)

        @block.vector
        def _(v):
            for h in range(16):
                for qt in range(8):
                    k = h * 8 + qt
                    par = k % 2
                    W(v, f"s_{k}")
                    E(lambda par=par: v.tensor_copy(out=FA(W2, 256), in_=PB[3 + par][:, 0:256]))
                    moff = 0 if qt == 0 else 256
                    E(lambda moff=moff: v.copy_predicated(
                        out=FA(W2, 256), mask=UA(UO["mask_swa"] + moff, 256),
                        data=V_fillswa))
                    E(lambda: v.reduce_max(out=stat(ST_NEGM), in_=FA(W2, 256),
                                           axis=AX.X, negate=True))
                    E(lambda: v.drain(), "dve", 1)
                    MS(f"sel_{k}", "dve")
                    W(v, f"pf_{k}")
                    E(lambda: v.reciprocal(out=stat(ST_RZ), in_=stat(ST_Z)))
                    E(lambda: v.drain())
                    E(lambda: v.tensor_scalar_mul(out=BA(B_PN, 256), in0=FA(W3, 256),
                                                  scalar1=stat(ST_RZ)), "dve", 1)
                    MS(f"pn_{k}", "dve")
                    W(v, f"ptr_{k}")
                    if k >= 1:
                        W(v, f"av_{k-1}")
                    E(lambda par=par: v.tensor_copy(out=BA(B_PT, 256), in_=PBFS[par][:, 0:256]), "dve", 1)
                    MS(f"pt_{k}", "dve")

        @block.scalar
        def _(sc):
            for h in range(16):
                for qt in range(8):
                    k = h * 8 + qt
                    par = k % 2
                    W(sc, f"sel_{k}")
                    E(lambda: sc.activation(out=FA(W3, 256), in_=FA(W2, 256), func=AF.Exp,
                                            bias=stat(ST_NEGM), scale=1.0,
                                            accum_out=stat(ST_Z)))
                    E(lambda: sc.drain(), "act", 1)
                    MS(f"pf_{k}", "act")
                    W(sc, f"av_{k}")
                    E(lambda h=h, qt=qt, par=par: sc.copy(
                        out=BA(B_HEADS + 1024 * h + 128 * qt, 128),
                        in_=PB[5 + par][:, 0:128]), "act", 1)
                    MS(f"hv_{k}", "act")
            MS("attn_all", "act")

        @block.sync
        def _(q):
            for tt in range(8):
                par = tt % 2
                if tt >= 2:
                    W(q, f"yadd_{tt-2}_3")
                E(lambda tt=tt: q.dma_start(
                    out=FA(W4, D), in_=y_part.ap()[128 * tt:128 * (tt + 1), :]),
                  f"d_yb{par}", 16)
                MS(f"ywb_{tt}", f"d_yb{par}")
                for oc in range(4):
                    W(q, f"yadd_{tt}_{oc}")
                E(lambda tt=tt: q.dma_start(
                    out=y_part.ap()[128 * tt:128 * (tt + 1), :], in_=FA(W4, D)), "d_yw", 16)
                if upto == "p2":
                    E(lambda tt=tt: q.dma_start(
                        out=y_out[128 * tt:128 * (tt + 1), :], in_=FA(W4, D)), "d_yw", 16)
                MS(f"yso_{tt}", "d_yw")

        @block.tensor
        def _(pe):
            W(pe, "wo_ready")
            W(pe, "attn_all")
            for tt in range(8):
                for oc in range(4):
                    kk = tt * 4 + oc
                    if kk >= 2:
                        W(pe, f"yadd_{(kk-2)//4}_{(kk-2)%4}")
                    for h in range(16):
                        base = B_WOL if h < 8 else B_WOH
                        E(lambda tt=tt, oc=oc, h=h, base=base, kk=kk: pe.matmul(
                            PB[0 + kk % 2][:, :],
                            BA(B_HEADS + 1024 * h + 128 * tt, 128),
                            BA(base + 2048 * (h % 8) + 512 * oc, 512),
                            start=(h == 0), stop=(h == 15)),
                          "pe" if h == 15 else None, 1)
                    MS(f"wop_{tt}_{oc}", "pe")

        @block.vector
        def _(v):
            for tt in range(8):
                W(v, f"ywb_{tt}")
                for oc in range(4):
                    kk = tt * 4 + oc
                    W(v, f"wop_{tt}_{oc}")
                    E(lambda oc=oc, kk=kk: v.tensor_add(
                        out=FA(W4 + 512 * oc, 512), in0=PB[0 + kk % 2][:, :],
                        in1=FA(W4 + 512 * oc, 512)))
                    E(lambda: v.drain(), "dve", 1)
                    MS(f"yadd_{tt}_{oc}", "dve")

        if upto == "p2" and DBG_P2:
            @block.sync
            def _(q):
                W(q, f"ktr_{NTH-1}")
                E(lambda: q.dma_start(out=dbg_kfm_a.ap(), in_=BA(B_KFM, 4608)), "d_c", 16)
                W(q, "qtr_31")
                E(lambda: q.dma_start(out=dbg_kfm_b.ap(), in_=BA(B_KFM, 4608)), "d_c", 16)

        for name in ENGS:
            dec = dict(pe=block.tensor, dve=block.vector, act=block.scalar,
                       pool=block.gpsimd, sync=block.sync)[name]
            def mk(name):
                return lambda e: env.bar(e, name, "p2")
            dec(mk(name))

    env_dry = Env(True)
    emit(env_dry, FakeBlock())
    env = Env(False, sems=sems, milestones=env_dry.ms)
    with nc.Block() as block:
        emit(env, block)
    assert env.ctr == env_dry.ctr
    ctx.close()
    return nc, d_in


def host_inputs(inp):
    f32 = np.float32
    x = np.asarray(inp["L0_attn_x"]).reshape(8192, D)
    cores = []
    t_glob = np.arange(-128, 8192 + 0, dtype=np.int64)
    inv = (1.0 / 10000.0 ** (np.arange(0, 128, 2, dtype=f32) / 128.0)).astype(f32)

    identf = np.eye(128, dtype=f32)
    identb = np.eye(128, dtype=ml_dtypes.bfloat16)
    onesr = np.ones((1, 128), dtype=f32)
    ln1w = np.ascontiguousarray(np.asarray(inp["L0_ffn_ln1_w"], f32).reshape(CH, 128).T)
    ln2w = np.ascontiguousarray(np.asarray(inp["L0_ffn_ln2_w"], f32).reshape(CH, 128).T)
    scale_bc = np.broadcast_to(np.asarray(inp["L0_attn_scale"], f32).reshape(1, 1), (128, 1)).copy()
    b_bq = np.broadcast_to(np.asarray(inp["L0_attn_bq"], f32)[None, :], (128, D)).copy()
    b_bk = np.broadcast_to(np.asarray(inp["L0_attn_bk"], f32)[None, :], (128, 512)).copy()
    b_bv = np.broadcast_to(np.asarray(inp["L0_attn_bv"], f32)[None, :], (128, 512)).copy()
    b_bo = np.broadcast_to(np.asarray(inp["L0_attn_bo"], f32)[None, :], (128, D)).copy()
    qqn = np.broadcast_to(np.tile(np.asarray(inp["L0_attn_qn_w"], f32), 4)[None, :], (128, 512)).copy()
    kkn = np.broadcast_to(np.tile(np.asarray(inp["L0_attn_kn_w"], f32), 4)[None, :], (128, 512)).copy()

    ii = np.arange(128)[:, None]
    jj = np.arange(256)[None, :]
    m_std = ~((jj >= ii + 64) & (jj <= ii + 128))
    m_first0 = ~((jj >= np.maximum(ii + 64, 128)) & (jj <= ii + 128))
    m_p1 = np.zeros((4, 128, 512), np.uint8)
    m_p2 = np.zeros((4, 128, 512), np.uint8)
    qq = np.arange(128)[:, None]
    kk = np.arange(512)[None, :]
    for pat in range(4):
        m_p1[pat] = (kk > qq + 128 * pat).astype(np.uint8)
    kk2 = np.arange(128)[:, None]
    qq2 = np.arange(512)[None, :]
    for pat in range(4):
        m_p2[pat] = (128 * pat + kk2 > qq2).astype(np.uint8)
    m_p1 = m_p1.reshape(4 * 128, 512)
    m_p2 = m_p2.reshape(4 * 128, 512)

    Wq_a = np.asarray(inp["L0_attn_Wq"], f32)
    Wk_a = np.asarray(inp["L0_attn_Wk"], f32)
    Wv_a = np.asarray(inp["L0_attn_Wv"], f32)
    Wo_a = np.asarray(inp["L0_attn_Wo"], f32)
    wq_sw = np.ascontiguousarray(
        Wq_a.reshape(4, 512, CH, 128).transpose(0, 3, 2, 1))
    wk_sw = np.ascontiguousarray(Wk_a.reshape(512, CH, 128).transpose(2, 1, 0))
    wv_sw = np.ascontiguousarray(Wv_a.reshape(512, CH, 128).transpose(2, 1, 0))
    wo_sw = np.ascontiguousarray(
        Wo_a.reshape(4, 512, 16, 128).transpose(2, 3, 0, 1))

    Wq_f = np.asarray(inp["L0_ffn_Wq"], f32)
    Wk_f = np.asarray(inp["L0_ffn_Wk"], f32)
    Wv_f = np.asarray(inp["L0_ffn_Wv"], f32)
    Wo_f = np.asarray(inp["L0_ffn_Wo"], f32)
    A_m = np.asarray(inp["L0_ffn_A"], f32)
    B_m = np.asarray(inp["L0_ffn_B"], f32)
    W_in = np.asarray(inp["L0_ffn_W_in_base"], f32)
    W_up = np.asarray(inp["L0_ffn_W_up_base"], f32)
    W_dn = np.asarray(inp["L0_ffn_W_down"], f32)

    a_t = np.ascontiguousarray(A_m.reshape(CH, 128, 16).transpose(1, 0, 2))
    b_t = np.ascontiguousarray(B_m)
    w_in_t = np.ascontiguousarray(
        W_in.reshape(NFC, 128, CH, 128).transpose(0, 3, 2, 1))
    w_up_t = np.ascontiguousarray(
        W_up.reshape(NFC, 128, CH, 128).transpose(0, 3, 2, 1))
    w_dn_t = np.ascontiguousarray(
        W_dn.reshape(4, 512, NFC, 128).transpose(0, 3, 2, 1))
    shared = dict(
        m_p1=m_p1, m_p2=m_p2, identf=identf, identb=identb, onesr=onesr,
        ln1w=ln1w, ln2w=ln2w, scale_bc=scale_bc, b_bq=b_bq, b_bk=b_bk,
        b_bv=b_bv, b_bo=b_bo, qqn=qqn, kkn=kkn,
        wq_sw=wq_sw, wk_sw=wk_sw, wv_sw=wv_sw, wo_sw=wo_sw,
        a_t=a_t, b_t=b_t, w_in_t=w_in_t, w_up_t=w_up_t, w_dn_t=w_dn_t,
    )

    for c in range(NCORE):
        m = dict(shared)
        lo = 1024 * c - 128
        if c == 0:
            xs = np.concatenate([np.zeros((128, D), f32), x[0:1024]], axis=0)
        else:
            xs = x[lo:lo + NTH * 128]
        m["x_sl"] = np.ascontiguousarray(xs)
        pos = np.arange(lo, lo + NTH * 128, dtype=f32)
        fr = pos[:, None] * inv[None, :]
        m["cos_t"] = np.ascontiguousarray(np.tile(np.cos(fr), (1, 4)).astype(f32))
        m["sin_t"] = np.ascontiguousarray(np.tile(np.sin(fr), (1, 4)).astype(f32))
        msk0 = m_first0 if c == 0 else m_std
        m["m_swa"] = np.concatenate([msk0, m_std], axis=0).astype(np.uint8)
        sl = slice(256 * c, 256 * (c + 1))
        m["wq_ff"] = np.ascontiguousarray(Wq_f[sl].reshape(256, CH, 128).transpose(2, 1, 0))
        m["wk_ff"] = np.ascontiguousarray(Wk_f[sl].reshape(256, CH, 128).transpose(2, 1, 0))
        m["wv_ff"] = np.ascontiguousarray(Wv_f[sl].reshape(256, CH, 128).transpose(2, 1, 0))
        m["wo_ff"] = np.ascontiguousarray(Wo_f[:, sl].reshape(CH, 128, 256).transpose(1, 0, 2))
        cores.append(m)
    return cores


def kernel(**inp):
    cores = host_inputs(inp)
    nc, _ = build_nc(upto="full")
    res = run_bass_kernel_spmd(nc, cores, core_ids=list(range(NCORE)))
    y = np.concatenate([res.results[c]["y"] for c in range(NCORE)], axis=0)
    return y.reshape(1, 8192, D)


# version 12
# speedup vs baseline: 1.2318x; 1.0012x over previous
import numpy as np
import ml_dtypes
import concourse.bass as bass
import concourse.mybir as mybir
from concourse.bass_utils import run_bass_kernel_spmd
from contextlib import ExitStack

F32 = mybir.dt.float32
BF16 = mybir.dt.bfloat16
U8 = mybir.dt.uint8
AF = mybir.ActivationFunctionType
AX = mybir.AxisListType
ALU = mybir.AluOpType

NCORE = 8
D = 2048
TOK = 1024
NTH = 9
NT = 8
CH = 16
F_ = 8192
NFC = 64
SIG = float(1.0 / np.sqrt(128.0))
LN_EPS = 1e-5
RMS2_EPS = 1e-6
QK_EPS = 1e-8
SWA_FILL = -10000.0
MIN_FILL = -1.0e38

ENGS = ("pe", "dve", "act", "pool", "sync")


class Env:

    def __init__(self, dry, sems=None, milestones=None):
        self.dry = dry
        self.ctr = {}
        self.sems = sems or {}
        self.ms = {} if milestones is None else milestones

    def E(self, f, sem=None, k=1):
        if sem is not None:
            self.ctr[sem] = self.ctr.get(sem, 0) + k
        if self.dry:
            return None
        inst = f()
        if sem is not None:
            inst.then_inc(self.sems[sem], k)
        return inst

    def MS(self, name, sem):
        if self.dry:
            assert name not in self.ms, f"dup milestone {name}"
            self.ms[name] = (sem, self.ctr.get(sem, 0))

    def W(self, eng, name):
        if not self.dry:
            sem, v = self.ms[name]
            if v > 0:
                eng.wait_ge(self.sems[sem], v)

    def bar(self, eng, which, tag):
        if which in ("sync", "pe"):
            self.E(lambda: eng.nop(), f"bs_{which}", 1)
        else:
            self.E(lambda: eng.drain(), f"bs_{which}", 1)
        self.MS(f"bar_{tag}_{which}", f"bs_{which}")
        for o in ENGS:
            if o != which:
                self.W(eng, f"bar_{tag}_{o}")


class FakeBlock:
    def _r(self, f):
        f(None)
    def sync(self, f): self._r(f)
    def vector(self, f): self._r(f)
    def scalar(self, f): self._r(f)
    def tensor(self, f): self._r(f)
    def gpsimd(self, f): self._r(f)


F32_TOT = 19688
BF_TOT = 61500
U8_TOT = 4608

FO = dict(
    ident=0, ln1w=128, ln2w=144, scale_bc=160, onesr=161,
    stats=292, fillswa=356, fillmin=612,
    qqn=1124, kkn=1636, b_bk=2148, b_bv=2660,
    b_bq=3172, b_bo=5220,
    w0=7268, w1=9316, w2=11364, w3=13412, w4=15460,
    cs0=17508, cs1=18020,
)
UO = dict(mask_swa=0, mask_p1=512, mask_p2=2560)
FILL2 = 18532
NEGONES = 19556
BO = dict(identb=0, a_bf=128, woap=384)

SEM_NAMES = [
    "d_c", "d_x0", "d_x1", "d_cs0", "d_cs1",
    "d_tz0", "d_tz1", "d_tx0", "d_tx1", "d_tq0", "d_tq1", "d_tk0", "d_tk1",
    "d_xs0", "d_xs1", "d_yp0", "d_yp1",
    "d_ag", "d_w0", "d_sw", "d_wo",
    "d_z0", "d_z1", "d_zs0", "d_zs1", "d_zt4",
    "d_mp", "d_bb", "d_xr0", "d_xr1",
    "d_wi0", "d_wi1", "d_wu0", "d_wu1", "d_wd0", "d_wd1",
    "d_wi2", "d_wi3", "d_wu2", "d_wu3", "d_wd2", "d_wd3",
    "d_wv0", "d_wv1", "d_wv2", "d_wv3", "d_wx0", "d_wx1", "d_wx2", "d_wx3",
    "d_we0", "d_we1", "d_we2", "d_we3",
    "d_yb0", "d_yb1", "d_yw", "d_out",
    "cc1", "cc2",
    "pe", "dve", "act", "pool",
    "bs_pe", "bs_dve", "bs_act", "bs_pool", "bs_sync",
]

B_ZT = 448
B_ZBF = (16832, 18880)
B_XNBF = (20928, 22976)
B_ZTHS = 25024
B_XNT = 448
B_KFM = 18880
B_VSB = 23488
B_WKS = 28096
B_WVS = 36288
B_XN2 = (44480, 46528)
B_KB = (48576, 49088)
B_WQS = 28096
B_QFM = 44480
B_QB = (36288, 36800)
B_PN = 28096
B_PT = 28352
B_HEADS = 448
B_WOL = 16832
B_WOH = 33216
B_KF3 = 448
B_ZTD = (16832, 25024)
B_WKF = 33216
B_WVF = 37312
B_VA = 41408
B_VT = 43584
B_WQF = 33216
B_QS = (37312, 38336)
B_PTB = (43584, 44608)
B_ZTS = 16832
B_WOFB = 448
B_MLPIN = 448
B_ZTHB = 33216
B_WIN = (41408, 43456)
B_WUP = (45504, 47552)
B_BBF = 49600
B_WDN = (41408, 45504)
B_HAMT = 57792
B_MPB = 58816


def build_nc(upto="full"):
    nc = bass.Bass("TRN2", target_bir_lowering=False)
    d_in = {}

    def IN(name, shape, dt=F32):
        d_in[name] = nc.dram_tensor(name, list(shape), dt, kind="ExternalInput").ap()
        return d_in[name]

    x_sl = IN("x_sl", (NTH * 128, D))
    cos_t = IN("cos_t", (NTH * 128, 256))
    sin_t = IN("sin_t", (NTH * 128, 256))
    m_swa = IN("m_swa", (2 * 128, 256), U8)
    m_p1 = IN("m_p1", (4 * 128, 512), U8)
    m_p2 = IN("m_p2", (4 * 128, 512), U8)
    identf_i = IN("identf", (128, 128))
    identb_i = IN("identb", (128, 128), BF16)
    onesr_i = IN("onesr", (1, 128))
    ln1w_i = IN("ln1w", (128, 16))
    ln2w_i = IN("ln2w", (128, 16))
    scale_i = IN("scale_bc", (128, 1))
    b_bq_i = IN("b_bq", (128, D))
    b_bk_i = IN("b_bk", (128, 512))
    b_bv_i = IN("b_bv", (128, 512))
    b_bo_i = IN("b_bo", (128, D))
    qqn_i = IN("qqn", (128, 512))
    kkn_i = IN("kkn", (128, 512))
    wq_sw = IN("wq_sw", (4, 128, CH, 512))
    wk_sw = IN("wk_sw", (128, CH, 512))
    wv_sw = IN("wv_sw", (128, CH, 512))
    wo_sw = IN("wo_sw", (16, 128, 4, 512))
    wq_ff = IN("wq_ff", (128, CH, 256))
    wk_ff = IN("wk_ff", (128, CH, 256))
    wv_ff = IN("wv_ff", (128, CH, 256))
    wo_ff = IN("wo_ff", (128, CH, 256))
    a_t = IN("a_t", (128, CH, 16))
    b_t = IN("b_t", (16, F_))
    w_in_t = IN("w_in_t", (NFC, 128, CH, 128))
    w_up_t = IN("w_up_t", (NFC, 128, CH, 128))
    w_dn_t = IN("w_dn_t", (4, 128, NFC, 512))

    y_out = nc.dram_tensor("y", [TOK, D], F32, kind="ExternalOutput").ap()

    DBG_P2 = False
    DBG_P4 = False
    if DBG_P4:
        dbg_mlpin = nc.dram_tensor("dbg_mlpin", [128, 32768], BF16, kind="ExternalOutput")
        dbg_hamt = nc.dram_tensor("dbg_hamt", [16, 1024], BF16, kind="ExternalOutput")
        dbg_ha = nc.dram_tensor("dbg_ha", [128, 128], F32, kind="ExternalOutput")
    dbgkind = "ExternalOutput" if upto in ("p1",) else "Internal"
    if upto == "p2" and DBG_P2:
        dbg_heads = nc.dram_tensor("dbg_heads", [128, 16384], BF16, kind="ExternalOutput")
        dbg_qfm = nc.dram_tensor("dbg_qfm", [128, 16384], BF16, kind="ExternalOutput")
        dbg_kfm = nc.dram_tensor("dbg_kfm", [128, 4608], BF16, kind="ExternalOutput")
        dbg_vsb = nc.dram_tensor("dbg_vsb", [128, 4608], BF16, kind="ExternalOutput")
        dbg_kfm_a = nc.dram_tensor("dbg_kfm_a", [128, 4608], BF16, kind="ExternalOutput")
        dbg_kfm_b = nc.dram_tensor("dbg_kfm_b", [128, 4608], BF16, kind="ExternalOutput")
    ag_in = nc.dram_tensor("ag_in", [NT * 128, D], BF16, kind=dbgkind)
    ag_out_t = nc.dram_tensor("ag_out", [NCORE * NT * 128, D], BF16, addr_space="Shared")
    ar_in = nc.dram_tensor("ar_in", [64, 2048], F32,
                           kind="ExternalOutput" if upto == "p3" else "Internal")
    ar_out = nc.dram_tensor("ar_out", [8, 2048], F32)
    y_part = nc.dram_tensor("y_part", [TOK, D], F32,
                            kind="ExternalOutput" if upto in ("p1", "p2") else "Internal")
    xn_dram = nc.dram_tensor("xn_dram", [NTH * 128, D], BF16, kind=dbgkind)
    zth_dram = nc.dram_tensor("zth_dram", [2, 128, CH * 512], BF16, kind=dbgkind)
    ag_out = ag_out_t.ap()

    ctx = ExitStack()
    AR_F = ctx.enter_context(nc.sbuf_tensor("arena_f", [128, F32_TOT], F32))
    AR_B = ctx.enter_context(nc.sbuf_tensor("arena_b", [128, BF_TOT], BF16))
    AR_U = ctx.enter_context(nc.sbuf_tensor("arena_u", [128, U8_TOT], U8))
    PBW01 = ctx.enter_context(nc.psum_tensor("pbw01", [128, 1024], F32))
    PBW56 = ctx.enter_context(nc.psum_tensor("pbw56", [128, 1024], F32))
    PB3T = ctx.enter_context(nc.psum_tensor("pb3", [128, 512], F32))
    PB4T = ctx.enter_context(nc.psum_tensor("pb4", [128, 512], F32))
    PBF = ctx.enter_context(nc.psum_tensor("pbf", [128, 1024], BF16))
    PBF2 = ctx.enter_context(nc.psum_tensor("pbf2", [128, 1024], BF16))
    PBFS = (PBF, PBF2)

    class _Half:
        def __init__(self, t, off):
            self.t, self.off = t, off
        def __getitem__(self, idx):
            ps, cs = idx
            a = cs.start or 0
            b_ = 512 if cs.stop is None else cs.stop
            return self.t[ps, self.off + a:self.off + b_]

    PB = {0: _Half(PBW01, 0), 1: _Half(PBW01, 512),
          3: _Half(PB3T, 0), 4: _Half(PB4T, 0),
          5: _Half(PBW56, 0), 6: _Half(PBW56, 512)}
    sems = {}
    for s_ in SEM_NAMES:
        sems[s_] = ctx.enter_context(nc.semaphore(s_))

    def FA(off, n, p0=0, p=128):
        return AR_F[p0:p0 + p, off:off + n]

    def BA(off, n, p0=0, p=128):
        return AR_B[p0:p0 + p, off:off + n]

    def UA(off, n):
        return AR_U[:, off:off + n]

    V_ident = FA(FO["ident"], 128)
    V_identb = BA(BO["identb"], 128)
    V_ln2w = FA(FO["ln2w"], 16)
    V_ln1w = FA(FO["ln1w"], 16)
    V_scale = FA(FO["scale_bc"], 1)
    V_onesr = FA(FO["onesr"], 128, p0=0, p=1)
    V_fillswa = FA(FO["fillswa"], 256)
    V_fillmin = FA(FO["fillmin"], 512)
    V_qqn = FA(FO["qqn"], 512)
    V_kkn = FA(FO["kkn"], 512)
    V_bbk = FA(FO["b_bk"], 512)
    V_bbv = FA(FO["b_bv"], 512)
    V_bbq = FA(FO["b_bq"], D)
    V_bbo = FA(FO["b_bo"], D)
    V_abf = BA(BO["a_bf"], 256)
    V_woap = BA(BO["woap"], 32)
    W0, W1, W2, W3, W4 = (FO["w0"], FO["w1"], FO["w2"], FO["w3"], FO["w4"])
    CS = (FO["cs0"], FO["cs1"])

    def stat(i, n=1, p=128, p0=0):
        return FA(FO["stats"] + i, n, p0=p0, p=p)

    def emit(env, block):
        E, W, MS = env.E, env.W, env.MS

        def emit_zev(sc, j):
            tt = j - 1
            for c in range(CH):
                W(sc, f"ztrp_{j}_{c}")
                E(lambda c=c, tt=tt: sc.copy(
                    out=BA(B_ZT + 2048 * tt + 128 * c, 128),
                    in_=PBFS[c % 2][:, 0:128]), "act", 1)
                MS(f"zev_{j}_{c}", "act")
            MS(f"ztr_{j}", "act")

        def sync_p1_tail(q, j):
            par = j % 2
            W(q, f"xnb_{j}")
            E(lambda j=j, par=par: q.dma_start(
                out=xn_dram.ap()[128 * j:128 * (j + 1), :],
                in_=BA(B_XNBF[par], D)), f"d_xs{par}", 16)
            MS(f"xnsp_{j}", f"d_xs{par}")
            if j >= 1:
                tt = j - 1
                W(q, f"y_{j}")
                E(lambda tt=tt: q.dma_start(
                    out=y_part.ap()[128 * tt:128 * (tt + 1), :], in_=FA(W4, D)),
                  f"d_yp{j%2}", 16)
                MS(f"yst_{j}", f"d_yp{j%2}")
                if upto == "p1":
                    E(lambda tt=tt: q.dma_start(
                        out=y_out[128 * tt:128 * (tt + 1), :], in_=FA(W4, D)), "d_out", 16)


        @block.sync
        def _(q):
            for f in [
                lambda: q.dma_start(out=V_ident, in_=identf_i),
                lambda: q.dma_start(out=V_identb, in_=identb_i),
                lambda: q.dma_start(out=V_ln1w, in_=ln1w_i),
                lambda: q.dma_start(out=V_ln2w, in_=ln2w_i),
                lambda: q.dma_start(out=V_scale, in_=scale_i),
                lambda: q.dma_start(out=V_onesr, in_=onesr_i),
                lambda: q.dma_start(out=V_qqn, in_=qqn_i),
                lambda: q.dma_start(out=V_kkn, in_=kkn_i),
                lambda: q.dma_start(out=V_bbk, in_=b_bk_i),
                lambda: q.dma_start(out=V_bbv, in_=b_bv_i),
                lambda: q.dma_start(out=V_bbq, in_=b_bq_i),
                lambda: q.dma_start(out=V_bbo, in_=b_bo_i),
                lambda: q.dma_start(out=UA(UO["mask_swa"], 512).rearrange("p (m n) -> p m n", m=2),
                                    in_=m_swa.rearrange("(m p) n -> p m n", p=128)),
                lambda: q.dma_start(out=UA(UO["mask_p1"], 2048).rearrange("p (m n) -> p m n", m=4),
                                    in_=m_p1.rearrange("(m p) n -> p m n", p=128)),
                lambda: q.dma_start(out=UA(UO["mask_p2"], 2048).rearrange("p (m n) -> p m n", m=4),
                                    in_=m_p2.rearrange("(m p) n -> p m n", p=128)),
            ]:
                E(f, "d_c", 16)
            MS("consts", "d_c")
            for i in range(NTH):
                par = i % 2
                if i >= 2:
                    W(q, f"p1x_free_{i-2}")
                E(lambda i=i, par=par: q.dma_start(
                    out=FA((W0, W1)[par], D), in_=x_sl[128 * i:128 * (i + 1), :]),
                  f"d_x{par}", 16)
                MS(f"x_{i}", f"d_x{par}")
                if i >= 1:
                    sync_p1_tail(q, i - 1)
            sync_p1_tail(q, NTH - 1)
            W(q, "ztr_8")
            E(lambda: q.dma_start(
                out=ag_in.ap().rearrange("(t p) n -> p t n", p=128),
                in_=BA(B_ZT, NT * D).rearrange("p (t n) -> p t n", t=NT)), "d_ag", 16)
            MS("ag_in_done", "d_ag")
            for c in range(CH):
                W(q, f"zths_{c}")
                E(lambda c=c: q.dma_start(
                    out=zth_dram.ap()[0, :, 512 * c:512 * (c + 1)],
                    in_=BA(B_ZTHS, 512)), "d_mp", 16)
                E(lambda c=c: q.dma_start(
                    out=zth_dram.ap()[1, :, 512 * c:512 * (c + 1)],
                    in_=BA(B_ZTHS + 512, 512)), "d_mp", 16)
                MS(f"zthsp_{c}", "d_mp")

        @block.vector
        def _(v):
            E(lambda: v.memset(V_fillswa, SWA_FILL))
            E(lambda: v.memset(stat(8), 0.0))
            E(lambda: v.memset(stat(9), LN_EPS))
            E(lambda: v.memset(stat(10), RMS2_EPS))
            E(lambda: v.memset(stat(11), QK_EPS))
            E(lambda: v.memset(FA(FILL2, 1024), MIN_FILL))
            E(lambda: v.memset(FA(NEGONES, 128, p=1), -1.0))
            E(lambda: v.memset(V_fillmin, MIN_FILL), "dve", 1)
            MS("fills", "dve")
            W(v, "consts")
            for i in range(NTH):
                par = i % 2
                own = i >= 1
                W(v, f"x_{i}")
                xb = FA((W0, W1)[par], D)
                E(lambda xb=xb: v.reduce_sum(out=stat(0), in_=xb, axis=AX.X))
                E(lambda: v.drain())
                E(lambda: v.tensor_scalar_mul(out=stat(1), in0=stat(0),
                                              scalar1=-1.0 / D))
                E(lambda: v.drain(), "dve", 1)
                MS(f"nm_{i}", "dve")
                W(v, f"sd_{i}")
                if i >= 1:
                    W(v, f"sd2_{i-1}" if i - 1 >= 1 else "xnb_0")
                E(lambda: v.reciprocal(out=stat(3), in_=stat(2)))
                E(lambda: v.drain())
                E(lambda xb=xb: v.tensor_scalar_add(out=FA(W3, D), in0=xb, scalar1=stat(1)))
                E(lambda: v.tensor_scalar_mul(out=FA(W2, D), in0=FA(W3, D), scalar1=stat(3)))
                if own:
                    if i >= 2:
                        W(v, f"yst_{i-1}")
                    E(lambda xb=xb: v.tensor_add(out=FA(W4, D), in0=xb, in1=FA(W2, D)))
                    E(lambda: v.tensor_add(out=FA(W4, D), in0=FA(W4, D), in1=V_bbo))
                E(lambda: v.drain(), "dve", 1)
                MS(f"xn_{i}", "dve")
                MS(f"y_{i}", "dve")
                MS(f"p1x_free_{i}", "dve")
                if own:
                    W(v, f"sd2_{i}")
                    if i >= 3:
                        W(v, f"ztr_{i-2}")
                    E(lambda: v.reciprocal(out=stat(5), in_=stat(4)))
                    E(lambda: v.drain())
                    E(lambda par=par: v.tensor_scalar_mul(
                        out=BA(B_ZBF[par], D), in0=FA(W2, D), scalar1=stat(5)), "dve", 1)
                    MS(f"z_{i}", "dve")
            W(v, "ztr_8")
            zt3 = BA(B_ZT, NT * D).rearrange("p (t n) -> p t n", t=NT)
            for c in range(CH):
                if c >= 1:
                    W(v, f"zthsp_{c-1}")
                E(lambda c=c: v.tensor_scalar_mul(
                    out=BA(B_ZTHS, 1024).rearrange("p (b n) -> p b n", b=8),
                    in0=zt3[:, :, 128 * c:128 * (c + 1)],
                    scalar1=V_ln2w[:, c:c + 1]), "dve", 1)
                MS(f"zths_{c}", "dve")

        @block.scalar
        def _(sc):
            for i in range(NTH):
                par = i % 2
                own = i >= 1
                xb = FA((W0, W1)[par], D)
                W(sc, f"nm_{i}")
                E(lambda xb=xb: sc.activation(out=FA(W3, D), in_=xb, func=AF.Square,
                                              bias=stat(1), scale=1.0, accum_out=stat(6)))
                E(lambda: sc.drain())
                E(lambda: sc.activation(out=stat(2), in_=stat(6), func=AF.Sqrt,
                                        bias=stat(9), scale=1.0 / D))
                E(lambda: sc.drain(), "act", 1)
                MS(f"sd_{i}", "act")
                W(sc, f"xn_{i}")
                if i >= 2:
                    W(sc, f"xnsp_{i-2}")
                E(lambda par=par: sc.copy(out=BA(B_XNBF[par], D), in_=FA(W2, D)))
                if own:
                    E(lambda: sc.activation(out=FA(W3, D), in_=FA(W2, D), func=AF.Square,
                                            bias=stat(8), scale=1.0, accum_out=stat(7)))
                    E(lambda: sc.drain())
                    E(lambda: sc.activation(out=stat(4), in_=stat(7), func=AF.Sqrt,
                                            bias=stat(10), scale=1.0 / D))
                E(lambda: sc.drain(), "act", 1)
                MS(f"xnb_{i}", "act")
                if own:
                    MS(f"sd2_{i}", "act")
                if i >= 2:
                    emit_zev(sc, i - 1)
            emit_zev(sc, NTH - 1)

        @block.tensor
        def _(pe):
            for j in range(1, NTH):
                par = j % 2
                W(pe, f"z_{j}")
                for c in range(CH):
                    if (j, c) != (1, 0) and not (j == 1 and c == 1):
                        pc_, cc_ = (j, c - 2) if c >= 2 else (j - 1, CH - 2 + c)
                        W(pe, f"zev_{pc_}_{cc_}")
                    E(lambda c=c, par=par: pe.transpose(
                        PBFS[c % 2][:, 0:128], BA(B_ZBF[par] + 128 * c, 128),
                        V_identb), "pe", 1)
                    MS(f"ztrp_{j}_{c}", "pe")

        for name in ENGS:
            dec = dict(pe=block.tensor, dve=block.vector, act=block.scalar,
                       pool=block.gpsimd, sync=block.sync)[name]
            def mk(name):
                return lambda e: env.bar(e, name, "p1")
            dec(mk(name))

        if upto == "p1":
            @block.gpsimd
            def _(g):
                n = env.ctr.get("d_out", 0)
                if not env.dry and n:
                    g.wait_ge(sems["d_out"], n)
            return

        emit_p2(env, block)
        if upto == "p2":
            @block.gpsimd
            def _(g):
                n = env.ctr.get("d_yw", 0)
                if not env.dry and n:
                    g.wait_ge(sems["d_yw"], n)
            return
        emit_p3(env, block)
        if upto in ("p3", "p3r"):
            @block.gpsimd
            def _(g):
                if upto == "p3r":
                    W(g, "rs_trig")
                W(g, "ar_in_done")
                n = env.ctr.get("d_yw", 0)
                if not env.dry and n:
                    g.wait_ge(sems["d_yw"], n)
            return
        emit_p4(env, block, stop=(upto if upto in ("p4a", "p4z", "p4y") else None))
        @block.gpsimd
        def _(g):
            if upto == "p4a":
                W(g, "ham_done")
            elif upto == "p4z":
                W(g, "mod_in")
            elif upto == "p4y":
                W(g, "hAev_7")
            else:
                W(g, "y_done")

    ST_MC = 33
    ST_CS3 = 35
    ST_NEGB = 37
    ST_BIAS = 39
    ST_RZ3 = 41

    def emit_p3(env, block):
        E, W, MS = env.E, env.W, env.MS

        def zt_view(buf_off):
            return BA(buf_off, 8192).rearrange("p (b c l) -> p b c l", b=4, c=CH)

        def nkt(qt):
            return 4 * qt + 4

        def p1_iters(qt):
            out = []
            for sub in range(4):
                nf = (128 * (4 * qt + sub)) // 512
                for kc in range(nf + 1):
                    out.append((sub, kc, kc == nf))
            return out

        @block.gpsimd
        def _(g):
            E(lambda: g.dma_start(out=BA(448, 4096).rearrange("p (c j) -> p c j", c=CH),
                                  in_=wo_ff), "d_w0", 16)
            E(lambda: g.dma_start(out=V_abf.rearrange("p (c r) -> p c r", c=CH),
                                  in_=a_t), "d_w0", 16)
            E(lambda: g.dma_start(out=BA(B_WKF, 4096).rearrange("p (c o) -> p c o", c=CH),
                                  in_=wk_ff), "d_w0", 16)
            E(lambda: g.dma_start(out=BA(B_WVF, 4096).rearrange("p (c o) -> p c o", c=CH),
                                  in_=wv_ff), "d_w0", 16)
            MS("p3w", "d_w0")
            E(lambda: g.memset(BA(B_VA, 2176), 1.0))
            E(lambda: g.drain(), "pool", 1)
            MS("va_ones", "pool")
            W(g, "p3kv_pe_done")
            E(lambda: g.dma_start(out=BA(B_WQF, 4096).rearrange("p (c o) -> p c o", c=CH),
                                  in_=wq_ff), "d_w0", 16)
            MS("p3wq", "d_w0")
            if upto not in ("p3",):
                W(g, "ar_in_done")
                E(lambda: g.collective_compute(
                    "ReduceScatter", ALU.add, replica_groups=[list(range(NCORE))],
                    ins=[ar_in.ap().opt()], outs=[ar_out.ap().opt()]), "cc2", 1)
                MS("rs_trig", "cc2")

        @block.tensor
        def _(pe):
            W(pe, "p3w")
            for hh in range(2):
                if hh == 1:
                    W(pe, "woa_ev_0")
                for c in range(CH):
                    E(lambda hh=hh, c=c: pe.matmul(
                        PB[0][:, 0:16], BA(448 + 256 * c + 128 * hh, 128),
                        V_abf[:, 16 * c:16 * (c + 1)],
                        start=(c == 0), stop=(c == CH - 1)),
                      "pe" if c == CH - 1 else None, 1)
                MS(f"woa_{hh}", "pe")
            MS("woa_pe_done", "pe")

        @block.scalar
        def _(sc):
            for hh in range(2):
                W(sc, f"woa_{hh}")
                E(lambda hh=hh: sc.copy(out=V_woap[:, 16 * hh:16 * (hh + 1)],
                                        in_=PB[0][:, 0:16]))
                E(lambda: sc.drain(), "act", 1)
                MS(f"woa_ev_{hh}", "act")

        @block.vector
        def _(v):
            W(v, "p3w")
            for c in range(CH):
                E(lambda c=c: v.tensor_scalar_mul(
                    out=BA(B_WKF + 256 * c, 256), in0=BA(B_WKF + 256 * c, 256),
                    scalar1=V_ln1w[:, c:c + 1]))
                E(lambda c=c: v.tensor_scalar_mul(
                    out=BA(B_WVF + 256 * c, 256), in0=BA(B_WVF + 256 * c, 256),
                    scalar1=V_ln1w[:, c:c + 1]))
            E(lambda: v.drain(), "dve", 1)
            MS("kvf_folded", "dve")

        @block.sync
        def _(q):
            for cc in range(16):
                par = cc % 2
                if cc >= 2:
                    W(q, f"p3kv_used_{cc-2}")
                if cc == 0:
                    W(q, "ag_trig")
                E(lambda cc=cc, par=par: q.dma_start(
                    out=zt_view(B_ZTD[par]),
                    in_=ag_out[512 * cc:512 * (cc + 1), :]
                    .rearrange("(b p) n -> p b n", p=128)), f"d_z{par}", 16)
                MS(f"zt_{cc}", f"d_z{par}")

        @block.tensor
        def _(pe):
            W(pe, "kvf_folded")
            W(pe, "woa_ev_1")
            for cc in range(16):
                par = cc % 2
                W(pe, f"zt_{cc}")
                for h in range(2):
                    if not (cc == 0 and h == 0):
                        pcc, ph = (cc, 0) if h == 1 else (cc - 1, 1)
                        W(pe, f"kev_{pcc}_{ph}")
                        W(pe, f"vev_{pcc}_{ph}")
                        W(pe, f"vaev_{pcc}_{ph}")
                    for c in range(CH):
                        E(lambda h=h, c=c, par=par: pe.matmul(
                            PB[0][:, :], BA(B_WKF + 256 * c + 128 * h, 128),
                            zt_view(B_ZTD[par])[:, :, c, :],
                            start=(c == 0), stop=(c == CH - 1)),
                          "pe" if c == CH - 1 else None, 1)
                    MS(f"kpe3_{cc}_{h}", "pe")
                    for c in range(CH):
                        E(lambda h=h, c=c, par=par: pe.matmul(
                            PB[1][:, :], BA(B_WVF + 256 * c + 128 * h, 128),
                            zt_view(B_ZTD[par])[:, :, c, :],
                            start=(c == 0), stop=(c == CH - 1)),
                          "pe" if c == CH - 1 else None, 1)
                    MS(f"vpe3_{cc}_{h}", "pe")
                    W(pe, f"vev_{cc}_{h}")
                    for j in range(4):
                        E(lambda h=h, j=j: pe.matmul(
                            PB[3][:, 16 * j:16 * (j + 1)], BA(B_VT + 128 * j, 128),
                            V_woap[:, 16 * h:16 * (h + 1)], start=True, stop=True),
                          "pe" if j == 3 else None, 1)
                    MS(f"vape_{cc}_{h}", "pe")
                MS(f"p3kv_used_{cc}", "pe")
            MS("p3kv_pe_done", "pe")

        @block.scalar
        def _(sc):
            W(sc, "va_ones")
            for cc in range(16):
                for h in range(2):
                    W(sc, f"kpe3_{cc}_{h}")
                    E(lambda cc=cc, h=h: sc.copy(
                        out=BA(B_KF3 + 8192 * h + 512 * cc, 512), in_=PB[0][:, :]), "act", 1)
                    MS(f"kev_{cc}_{h}", "act")
                    W(sc, f"vpe3_{cc}_{h}")
                    E(lambda h=h: sc.copy(out=BA(B_VT, 512), in_=PB[1][:, :]), "act", 1)
                    MS(f"vev_{cc}_{h}", "act")
                    W(sc, f"vape_{cc}_{h}")
                    for j in range(4):
                        E(lambda cc=cc, h=h, j=j: sc.copy(
                            out=BA(B_VA + 1088 * h + 17 * (4 * cc + j), 16),
                            in_=PB[3][:, 16 * j:16 * (j + 1)]),
                          "act" if j == 3 else None, 1)
                    MS(f"vaev_{cc}_{h}", "act")

        PW = (PBW01, PBW56)
        s_last = {0: "vev_15_1", 1: None}
        C0 = FO["cs0"]
        KCH = C0
        KPF = C0 + 32
        QM2 = C0 + 96
        QK2 = C0 + 100
        LTS = C0 + 104
        CVS = C0 + 108
        K2B = 45632
        Q2B = 47680

        ZTSB = (B_ZTS, B_ZTD[1])

        @block.sync
        def _(q):
            for qt in range(16):
                if qt >= 2:
                    W(q, f"qproj_done_{qt-2}")
                else:
                    W(q, "p3kv_pe_done")
                E(lambda qt=qt: q.dma_start(
                    out=zt_view(ZTSB[qt % 2]),
                    in_=ag_out[512 * qt:512 * (qt + 1), :]
                    .rearrange("(b p) n -> p b n", p=128)), f"d_zs{qt%2}", 16)
                MS(f"zts_{qt}", f"d_zs{qt%2}")
            W(q, "modpreT_ready")
            E(lambda: q.dma_start(
                out=ar_in.ap().rearrange("s (r p) -> r s p", r=16),
                in_=FA(W0, 8192, p=16).rearrange("r (s p) -> r s p", s=64)),
              "d_mp", 16)
            MS("ar_in_done", "d_mp")

        @block.tensor
        def _(pe):
            W(pe, "qf_folded")
            for i in range(32):
                if i == 0:
                    W(pe, "vaev_15_1")
                if i >= 2:
                    W(pe, f"km_{i-2}")
                W(pe, f"k2k_{i}")
                E(lambda i=i: pe.matmul(
                    PB[(3, 4)[i % 2]][0:1, 0:512], BA(B_VA + 16, 1),
                    BA(K2B + 512 * (i % 4), 512),
                    start=True, stop=True), "pe", 1)
                MS(f"csk_{i}", "pe")
            W(pe, "km_30")
            W(pe, "km_31")
            for qt in range(16):
                W(pe, f"zts_{qt}")
                for h in range(2):
                    if qt >= 1 or h >= 1:
                        pqt, ph = (qt, 0) if h == 1 else (qt - 1, 1)
                        W(pe, f"qev_{pqt}_{ph}")
                    for c in range(CH):
                        E(lambda h=h, c=c, qt=qt: pe.matmul(
                            PB[3][:, :], BA(B_WQF + 256 * c + 128 * h, 128),
                            zt_view(ZTSB[qt % 2])[:, :, c, :],
                            start=(c == 0), stop=(c == CH - 1)),
                          "pe" if c == CH - 1 else None, 1)
                    MS(f"qpe3_{qt}_{h}", "pe")
                for h in range(2):
                    W(pe, f"q2_{qt}_{h}")
                    E(lambda h=h: pe.matmul(
                        PB[(4, 3)[h]][0:1, 0:512], BA(B_VA + 16, 1),
                        BA(Q2B + 512 * h, 512),
                        start=True, stop=True), "pe", 1)
                    MS(f"csq_{qt}_{h}", "pe")
                for h in range(2):
                    np2 = nkt(qt) // 2
                    order = list(range(2 * qt, np2)) + list(range(2 * qt))

                    def emit_av(oj, order=order, np2=np2, qt=qt, h=h):
                        pj = order[oj]
                        W(pe, f"exppt_{qt}_{h}_{pj}")
                        for half in range(2):
                            kt = 2 * pj + half
                            E(lambda kt=kt, pj=pj, half=half, qt=qt, h=h,
                              st=(oj == 0 and half == 0),
                              sp=(oj == np2 - 1 and half == 1): pe.matmul(
                                PB[4][0:17, :], BA(B_VA + 1088 * h + 17 * kt, 17),
                                BA(B_PTB[pj % 2] + 512 * half, 512),
                                start=st, stop=sp),
                              "pe", 1)
                        MS(f"av_{qt}_{h}_{pj}", "pe")

                    for oi, pi in enumerate(order):
                        wb = pi % 2
                        if s_last[wb]:
                            W(pe, s_last[wb])
                        for half in range(2):
                            kt = 2 * pi + half
                            E(lambda qt=qt, h=h, kt=kt, wb=wb, half=half: pe.matmul(
                                PW[wb][:, 512 * half:512 * (half + 1)],
                                BA(B_KF3 + 8192 * h + 128 * kt, 128),
                                BA(B_QS[qt % 2] + 512 * h, 512),
                                start=True, stop=True), "pe", 1)
                        MS(f"s2_{qt}_{h}_{pi}", "pe")
                        s_last[wb] = f"exp_{qt}_{h}_{pi}"
                        if oi == 1:
                            if h == 0:
                                W(pe, f"qev_{qt}_1")
                                W(pe, f"qk2_{qt}_1")
                            W(pe, f"cq_{qt}_{h}")
                            E(lambda h=h: pe.matmul(
                                PB[3][:, 256 + 32 * h:257 + 32 * h],
                                FA(NEGONES, 128, p=1),
                                FA(CVS + h, 1, p=1), start=True, stop=True),
                              "pe", 1)
                            MS(f"biasmm_{qt}_{h}", "pe")
                        if oi >= 1:
                            emit_av(oi - 1)
                    emit_av(np2 - 1)
                    MS(f"avdone_{qt}_{h}", "pe")
                    W(pe, f"oev_{qt}_{h}")
                    for jj in range(4):
                        E(lambda jj=jj: pe.transpose(
                            PB[3][:, 32 * jj:32 * jj + 17],
                            FA(W2 + 128 * jj, 128, p=17), V_ident[0:17, 0:17]),
                          "pe" if jj == 3 else None, 1)
                    MS(f"otr_{qt}_{h}", "pe")
                    W(pe, f"modp_{qt}_{h}")
            if s_last[0]:
                W(pe, s_last[0])
            for s in range(64):
                if s >= 2:
                    W(pe, f"mptev_{s-2}")
                E(lambda s=s: pe.transpose(PB[s % 2][0:16, 0:128],
                                           FA(W4 + 16 * s, 16), V_ident),
                  "pe", 1)
                MS(f"mptr_{s}", "pe")

        @block.vector
        def _(v):
            W(v, "p3wq")
            for c in range(CH):
                E(lambda c=c: v.tensor_scalar_mul(
                    out=BA(B_WQF + 256 * c, 256), in0=BA(B_WQF + 256 * c, 256),
                    scalar1=V_ln1w[:, c:c + 1]))
            E(lambda: v.drain(), "dve", 1)
            MS("qf_folded", "dve")

            def emit_km(i):
                W(v, f"csk_{i}")
                E(lambda i=i: v.reduce_max(
                    out=FA(KCH + 16 * (i % 2) + i // 2, 1, p=1),
                    in_=PB[(3, 4)[i % 2]][0:1, 0:512], axis=AX.X))
                E(lambda: v.drain(), "dve", 1)
                MS(f"km_{i}", "dve")

            for i in range(32):
                cc, h = i // 2, i % 2
                W(v, f"kev_{cc}_{h}")
                if i >= 4:
                    W(v, f"csk_{i-4}")
                E(lambda cc=cc, h=h, i=i: v.tensor_mul(
                    out=BA(K2B + 512 * (i % 4), 512),
                    in0=BA(B_KF3 + 8192 * h + 512 * cc, 512),
                    in1=BA(B_KF3 + 8192 * h + 512 * cc, 512)), "dve", 1)
                MS(f"k2k_{i}", "dve")
                if i >= 2:
                    emit_km(i - 2)
            for i in (30, 31):
                emit_km(i)
            E(lambda: v.drain())
            for h in range(2):
                for qt in range(16):
                    E(lambda h=h, qt=qt: v.reduce_max(
                        out=FA(KPF + 16 * h + qt, 1, p=1),
                        in_=FA(KCH + 16 * h, qt + 1, p=1), axis=AX.X))
                E(lambda: v.drain())
                E(lambda h=h: v.tensor_scalar_mul(
                    out=FA(KPF + 16 * h, 16, p=1),
                    in0=FA(KPF + 16 * h, 16, p=1), scalar1=SIG * SIG))
            E(lambda: v.drain(), "dve", 1)
            MS("kpf_done", "dve")
            for qt in range(16):
                for h in range(2):
                    W(v, f"qev_{qt}_{h}")
                    if qt >= 1:
                        W(v, f"csq_{qt-1}_{h}")
                    E(lambda h=h, qt=qt: v.tensor_mul(
                        out=BA(Q2B + 512 * h, 512),
                        in0=BA(B_QS[qt % 2] + 512 * h, 512),
                        in1=BA(B_QS[qt % 2] + 512 * h, 512)), "dve", 1)
                    MS(f"q2_{qt}_{h}", "dve")
                for h in range(2):
                    W(v, f"csq_{qt}_{h}")
                    E(lambda h=h: v.reduce_max(
                        out=FA(QM2 + h, 1, p=1),
                        in_=PB[(4, 3)[h]][0:1, 0:512], axis=AX.X))
                    E(lambda: v.drain())
                    E(lambda h=h, qt=qt: v.tensor_mul(
                        out=FA(QK2 + h, 1, p=1), in0=FA(QM2 + h, 1, p=1),
                        in1=FA(KPF + 16 * h + qt, 1, p=1)))
                    E(lambda: v.drain(), "dve", 1)
                    MS(f"qk2_{qt}_{h}", "dve")
                for h in range(2):
                    np2 = nkt(qt) // 2
                    for pi in range(2 * qt, np2):
                        wb = pi % 2
                        W(v, f"s2_{qt}_{h}_{pi}")
                        if pi - 1 >= 2 * qt:
                            W(v, f"exp_{qt}_{h}_{pi-1}")
                        E(lambda wb=wb: v.tensor_copy(out=FA(W1, 1024),
                                                      in_=PW[wb][:, 0:1024]))
                        E(lambda qt=qt, pi=pi: v.copy_predicated(
                            out=FA(W1, 1024),
                            mask=UA(UO["mask_p2"] + 1024 * (pi - 2 * qt), 1024),
                            data=FA(FILL2, 1024)), "dve", 1)
                        MS(f"sel2_{qt}_{h}_{pi}", "dve")
                    W(v, f"otr_{qt}_{h}")
                    for jj in range(4):
                        s = 4 * qt + jj
                        E(lambda jj=jj: v.reciprocal(
                            out=stat(ST_RZ3),
                            in_=PB[3][:, 32 * jj + 16:32 * jj + 17]))
                        E(lambda: v.drain())
                        if h == 0:
                            E(lambda jj=jj, s=s: v.tensor_scalar_mul(
                                out=FA(W4 + 16 * s, 16),
                                in0=PB[3][:, 32 * jj:32 * jj + 16],
                                scalar1=stat(ST_RZ3)))
                        else:
                            E(lambda jj=jj, s=s: v.scalar_tensor_tensor(
                                out=FA(W4 + 16 * s, 16),
                                in0=PB[3][:, 32 * jj:32 * jj + 16],
                                scalar=stat(ST_RZ3), in1=FA(W4 + 16 * s, 16),
                                op0=ALU.mult, op1=ALU.add))
                    E(lambda: v.drain(), "dve", 1)
                    MS(f"modp_{qt}_{h}", "dve")
            for s in range(64):
                W(v, f"mptr_{s}")
                E(lambda s=s: v.tensor_copy(out=FA(W0 + 128 * s, 128, p=16),
                                            in_=PB[s % 2][0:16, 0:128]), "dve", 1)
                MS(f"mptev_{s}", "dve")
            MS("modpreT_ready", "dve")

        @block.scalar
        def _(sc):
            for qt in range(16):
                for h in range(2):
                    W(sc, f"qpe3_{qt}_{h}")
                    if qt >= 2 and h == 0:
                        W(sc, f"avdone_{qt-2}_1")
                    E(lambda qt=qt, h=h: sc.copy(
                        out=BA(B_QS[qt % 2] + 512 * h, 512), in_=PB[3][:, :]), "act", 1)
                    MS(f"qev_{qt}_{h}", "act")
                MS(f"qproj_done_{qt}", "act")
                for h in range(2):
                    W(sc, f"qk2_{qt}_{h}")
                    E(lambda h=h: sc.activation(
                        out=FA(LTS + h, 1, p=1), in_=FA(QK2 + h, 1, p=1),
                        func=AF.Ln, bias=stat(8, 1, p=1), scale=1.0))
                    E(lambda: sc.drain())
                    E(lambda h=h: sc.activation(
                        out=FA(CVS + h, 1, p=1), in_=FA(LTS + h, 1, p=1),
                        func=AF.Exp, bias=stat(8, 1, p=1), scale=0.5))
                    E(lambda: sc.drain(), "act", 1)
                    MS(f"cq_{qt}_{h}", "act")
                for h in range(2):
                    W(sc, f"biasmm_{qt}_{h}")
                    E(lambda h=h: sc.copy(out=stat(ST_BIAS + h),
                                          in_=PB[3][:, 256 + 32 * h:257 + 32 * h]))
                    E(lambda: sc.drain(), "act", 1)
                    MS(f"bias_{qt}_{h}", "act")
                    np2 = nkt(qt) // 2
                    order = list(range(2 * qt, np2)) + list(range(2 * qt))
                    for oi, pi in enumerate(order):
                        diag = pi >= 2 * qt
                        if oi >= 2:
                            W(sc, f"av_{qt}_{h}_{order[oi-2]}")
                        if diag:
                            W(sc, f"sel2_{qt}_{h}_{pi}")
                            E(lambda pi=pi, h=h: sc.activation(
                                out=BA(B_PTB[pi % 2], 1024), in_=FA(W1, 1024),
                                func=AF.Exp, bias=stat(ST_BIAS + h), scale=SIG), "act", 1)
                        else:
                            W(sc, f"s2_{qt}_{h}_{pi}")
                            E(lambda pi=pi, h=h, wb=pi % 2: sc.activation(
                                out=BA(B_PTB[pi % 2], 1024),
                                in_=PW[wb][:, 0:1024],
                                func=AF.Exp, bias=stat(ST_BIAS + h), scale=SIG), "act", 1)
                        MS(f"exp_{qt}_{h}_{pi}", "act")
                        MS(f"exppt_{qt}_{h}_{pi}", "act")
                    W(sc, f"avdone_{qt}_{h}")
                    E(lambda: sc.copy(out=FA(W2, 512, p=17), in_=PB[4][0:17, :]), "act", 1)
                    MS(f"oev_{qt}_{h}", "act")

        for name in ENGS:
            dec = dict(pe=block.tensor, dve=block.vector, act=block.scalar,
                       pool=block.gpsimd, sync=block.sync)[name]
            def mk(name):
                return lambda e: env.bar(e, name, "p3")
            dec(mk(name))

    HA_F = FO["w4"] + 1536
    MPRE_F = HA_F + 128
    GS_F = FO["w4"]
    GL_F = FO["w4"] + 512
    MS_F = FO["w4"] + 1024
    OWNX_F = FO["b_bq"]

    def emit_p4(env, block, stop=None):
        E, W, MS = env.E, env.W, env.MS

        @block.sync
        def _(q):
            for b in range(2):
                if stop == "p4z":
                    break
                if b == 1:
                    W(q, "hA_pe_0")
                E(lambda b=b: q.dma_start(out=BA(B_ZTHB, 8192),
                                          in_=zth_dram.ap()[b]), "d_zt4", 16)
                MS(f"zthA_{b}", "d_zt4")
            W(q, "rs_trig")
            E(lambda: q.dma_start(out=FA(OWNX_F, 2048, p=8), in_=ar_out.ap()),
              "d_mp", 16)
            MS("mod_in", "d_mp")
            if stop in ("p4a", "p4z", "p4y"):
                return
            for b in range(2):
                if b == 1 and DBG_P4:
                    W(q, "mlpin_0_63")
                    E(lambda: q.dma_start(out=dbg_mlpin.ap(), in_=BA(B_MLPIN, 32768)), "d_bb", 16)
                    E(lambda: q.dma_start(out=dbg_hamt.ap(), in_=BA(B_HAMT, 1024, p=16)), "d_bb", 16)
                    E(lambda: q.dma_start(out=dbg_ha.ap(), in_=FA(HA_F, 128)), "d_bb", 16)
                    MS("dbg_p4_done", "d_bb")
                W(q, "hA_pe_1" if b == 0 else f"gate_pe_{b-1}")
                E(lambda b=b: q.dma_start(out=BA(B_ZTHB, 8192),
                                          in_=zth_dram.ap()[b]), "d_zt4", 16)
                MS(f"zthB_{b}", "d_zt4")
                for ts in range(4):
                    W(q, "ham_done" if b == 0 else f"yfin_st_{b-1}_{ts}")
                    E(lambda b=b, ts=ts: q.dma_start(
                        out=FA((W0, W1, W2, W3)[ts], D),
                        in_=y_part.ap()[512 * b + 128 * ts:512 * b + 128 * (ts + 1), :]),
                      f"d_yb{ts%2}", 16)
                    MS(f"ybk4_{b}_{ts}", f"d_yb{ts%2}")
                for ts in range(4):
                    for dc in range(4):
                        W(q, f"yadd4_{b}_{dc}_{ts}")
                    E(lambda b=b, ts=ts: q.dma_start(
                        out=y_out[512 * b + 128 * ts:512 * b + 128 * (ts + 1), :],
                        in_=FA((W0, W1, W2, W3)[ts], D)), "d_out", 16)
                    MS(f"yfin_st_{b}_{ts}", "d_out")
            MS("y_done", "d_out")

        @block.gpsimd
        def _(g):
            E(lambda: g.dma_start(out=BA(B_BBF, F_, p=16), in_=b_t), "d_bb", 16)
            MS("bbf", "d_bb")
            if stop in ("p4a", "p4z", "p4y"):
                return

            for b in range(2):
                for fc in range(NFC):
                    par = fc % 2
                    if (b, fc) != (0, 0) and not (b == 1 and fc == 0):
                        if fc >= 2:
                            W(g, f"mlpin_{b}_{fc-2}")
                    if b == 1 and fc < 2:
                        W(g, f"mlpin_{0}_{NFC-2+fc}")
                    if b == 1 and fc == 0:
                        W(g, "wdd_0_3")
                    E(lambda b=b, fc=fc, par=par: g.dma_start(
                        out=BA(B_WIN[par], 2048).rearrange("p (c f) -> p c f", c=CH),
                        in_=w_in_t[fc]), f"d_wi{par}", 16)
                    E(lambda b=b, fc=fc, par=par: g.dma_start(
                        out=BA(B_WUP[par], 2048).rearrange("p (c f) -> p c f", c=CH),
                        in_=w_up_t[fc]), f"d_wu{par}", 16)
                    MS(f"wio_{b}_{fc}", f"d_wi{par}")
                    MS(f"wup_{b}_{fc}", f"d_wu{par}")
                for dc in range(4):
                    for pc in range(8):
                        par = pc % 2
                        k = dc * 8 + pc
                        if k == 0:
                            W(g, f"gate_pe_{b}")
                        if k >= 2:
                            W(g, f"wdn_used_{b}_{k-2}")
                        E(lambda b=b, dc=dc, pc=pc, par=par: g.dma_start(
                            out=BA(B_WDN[par], 4096).rearrange("p (f n) -> p f n", f=8),
                            in_=w_dn_t[dc, :, 8 * pc:8 * (pc + 1), :]),
                          f"d_wd{par}", 16)
                        MS(f"wdn_{b}_{k}", f"d_wd{par}")

        @block.tensor
        def _(pe):
            if stop == "p4z":
                return
            for b in range(2):
                W(pe, f"zthA_{b}")
                for tsl in range(4):
                    tt = 4 * b + tsl
                    if tt >= 1:
                        W(pe, f"hAev_{tt-1}")
                    for c in range(CH):
                        E(lambda c=c, tsl=tsl: pe.matmul(
                            PB[6][:, 0:16], BA(B_ZTHB + 512 * c + 128 * tsl, 128),
                            V_abf[:, 16 * c:16 * (c + 1)],
                            start=(c == 0), stop=(c == CH - 1)),
                          "pe" if c == CH - 1 else None, 1)
                    MS(f"hA_{tt}", "pe")
                MS(f"hA_pe_{b}", "pe")
            if stop == "p4y":
                return
            W(pe, "mod_in")
            for r in range(16):
                if r >= 2:
                    W(pe, f"mprev_{r-2}")
                E(lambda r=r: pe.transpose(PB[r % 2][:, 0:8],
                                           FA(OWNX_F + 128 * r, 128, p=8),
                                           V_ident[0:8, 0:8]), "pe", 1)
                MS(f"mptr4_{r}", "pe")
            W(pe, "hambf")
            for tt in range(8):
                if tt >= 2:
                    W(pe, f"hamtev_{tt-2}")
                E(lambda tt=tt: pe.transpose(
                    PBFS[tt % 2][0:16, 0:128],
                    BA(B_MPB + 16 * tt, 16), V_identb), "pe", 1)
                MS(f"hamtr_{tt}", "pe")
            if stop == "p4a":
                return
            for b in range(2):
                W(pe, f"zthB_{b}")
                W(pe, "ham_done")
                if b == 1:
                    for ts in range(4):
                        W(pe, f"yadd4_0_3_{ts}")
                    if DBG_P4:
                        W(pe, "dbg_p4_done")
                for fc in range(NFC):
                    par = fc % 2
                    W(pe, f"wio_{b}_{fc}")
                    W(pe, f"wup_{b}_{fc}")
                    if fc >= 2:
                        W(pe, f"gadd_{b}_{fc-2}")
                        W(pe, f"mlpin_{b}_{fc-2}")
                        W(pe, f"mev_{b}_{fc-2}")
                    for c in range(CH):
                        E(lambda c=c, par=par: pe.matmul(
                            PB[0 + par][:, :], BA(B_WIN[par] + 128 * c, 128),
                            BA(B_ZTHB + 512 * c, 512),
                            start=(c == 0), stop=(c == CH - 1)),
                          "pe" if c == CH - 1 else None, 1)
                    E(lambda b=b, fc=fc, par=par: pe.matmul(
                        PB[5 + par][:, :], BA(B_BBF + 128 * fc, 128, p=16),
                        BA(B_HAMT + 512 * b, 512, p=16), start=True, stop=True),
                      "pe", 1)
                    for c in range(CH):
                        E(lambda c=c, par=par: pe.matmul(
                            PB[3 + par][:, :], BA(B_WUP[par] + 128 * c, 128),
                            BA(B_ZTHB + 512 * c, 512),
                            start=(c == 0), stop=(c == CH - 1)),
                          "pe" if c == CH - 1 else None, 1)
                    MS(f"gup_{b}_{fc}", "pe")
                MS(f"gate_pe_{b}", "pe")
                for dc in range(4):
                    for pc in range(8):
                        k = dc * 8 + pc
                        par = pc % 2
                        W(pe, f"wdn_{b}_{k}")
                        if k == 0:
                            W(pe, f"mlpin_{b}_{NFC-1}")
                            W(pe, f"gadd_{b}_{NFC-1}")
                            W(pe, f"gadd_{b}_{NFC-2}")
                            W(pe, f"mev_{b}_{NFC-1}")
                            W(pe, f"mev_{b}_{NFC-2}")
                        if pc == 0 and dc >= 1:
                            for ts in range(4):
                                W(pe, f"yadd4_{b}_{dc-1}_{ts}")
                        for ts in range(4):
                            for fcl in range(8):
                                fc = 8 * pc + fcl
                                E(lambda ts=ts, fcl=fcl, fc=fc, par=par: pe.matmul(
                                    PB[(0, 1, 3, 4)[ts]][:, :],
                                    BA(B_MLPIN + 512 * fc + 128 * ts, 128),
                                    BA(B_WDN[par] + 512 * fcl, 512),
                                    start=(fc == 0), stop=(fc == NFC - 1)),
                                  "pe" if fcl == 7 else None, 1)
                        MS(f"wdnp_{b}_{k}", "pe")
                        MS(f"wdn_used_{b}_{k}", "pe")
                    MS(f"wdd_{b}_{dc}", "pe")

        @block.vector
        def _(v):
            if stop == "p4z":
                return
            for b in range(2):
                for tsl in range(4):
                    tt = 4 * b + tsl
                    W(v, f"hA_{tt}")
                    E(lambda tt=tt: v.tensor_copy(out=FA(HA_F + 16 * tt, 16),
                                                  in_=PB[6][:, 0:16]), "dve", 1)
                    MS(f"hAev_{tt}", "dve")
            if stop == "p4y":
                return
            for r in range(16):
                W(v, f"mptr4_{r}")
                E(lambda r=r: v.tensor_copy(
                    out=AR_F[:, MPRE_F + r:MPRE_F + 128:16],
                    in_=PB[r % 2][:, 0:8]), "dve", 1)
                MS(f"mprev_{r}", "dve")
            W(v, "tanh_done")
            E(lambda: v.scalar_tensor_tensor(out=FA(MPRE_F, 128), in0=FA(FO["w3"], 128),
                                             scalar=0.1, in1=FA(HA_F, 128),
                                             op0=ALU.mult, op1=ALU.mult))
            E(lambda: v.tensor_copy(out=BA(B_MPB, 128), in_=FA(MPRE_F, 128)))
            E(lambda: v.drain(), "dve", 1)
            MS("hambf", "dve")
            for tt in range(8):
                W(v, f"hamtr_{tt}")
                E(lambda tt=tt: v.tensor_copy(
                    out=BA(B_HAMT + 128 * tt, 128, p=16),
                    in_=PBFS[tt % 2][0:16, 0:128]), "dve", 1)
                MS(f"hamtev_{tt}", "dve")
            MS("ham_done", "dve")
            if stop == "p4a":
                return
            for b in range(2):
                for fc in range(NFC):
                    par = fc % 2
                    W(v, f"gup_{b}_{fc}")
                    W(v, f"mev_{b}_{fc}")
                    if fc >= 1:
                        W(v, f"silu_{b}_{fc-1}")
                    elif b == 1:
                        W(v, f"silu_0_{NFC-1}")
                    E(lambda par=par: v.tensor_add(out=FA(GS_F, 512),
                                                   in0=PB[0 + par][:, :],
                                                   in1=FA(MS_F, 512)), "dve", 1)
                    MS(f"gadd_{b}_{fc}", "dve")
                    W(v, f"silu_{b}_{fc}")
                    E(lambda fc=fc, par=par: v.tensor_mul(
                        out=BA(B_MLPIN + 512 * fc, 512), in0=PB[3 + par][:, :],
                        in1=FA(GL_F, 512)), "dve", 1)
                    MS(f"mlpin_{b}_{fc}", "dve")
                for dc in range(4):
                    for ts in range(4):
                        W(v, f"wdd_{b}_{dc}") if ts == 0 else None
                        W(v, f"ybk4_{b}_{ts}")
                        E(lambda b=b, dc=dc, ts=ts: v.tensor_add(
                            out=FA((W0, W1, W2, W3)[ts] + 512 * dc, 512),
                            in0=PB[(0, 1, 3, 4)[ts]][:, :],
                            in1=FA((W0, W1, W2, W3)[ts] + 512 * dc, 512)), "dve", 1)
                        MS(f"yadd4_{b}_{dc}_{ts}", "dve")

        @block.scalar
        def _(sc):
            if stop in ("p4z", "p4y"):
                return
            W(sc, "mprev_15")
            E(lambda: sc.activation(out=FA(FO["w3"], 128), in_=FA(MPRE_F, 128),
                                    func=AF.Tanh, bias=stat(8), scale=1.0))
            E(lambda: sc.drain(), "act", 1)
            MS("tanh_done", "act")
            if stop == "p4a":
                return
            for b in range(2):
                for fc in range(NFC):
                    par = fc % 2
                    W(sc, f"gup_{b}_{fc}")
                    E(lambda par=par: sc.copy(out=FA(MS_F, 512), in_=PB[5 + par][:, :]), "act", 1)
                    MS(f"mev_{b}_{fc}", "act")
                    W(sc, f"gadd_{b}_{fc}")
                    E(lambda: sc.activation(out=FA(GL_F, 512), in_=FA(GS_F, 512),
                                            func=AF.Silu, bias=stat(8), scale=1.0), "act", 1)
                    MS(f"silu_{b}_{fc}", "act")

    def emit_p2(env, block):
        E, W, MS = env.E, env.W, env.MS
        ST_KS = 16
        ST_SD4 = 20
        ST_R4 = 24
        ST_NEGM = 28
        ST_Z = 29
        ST_RZ = 30

        def rope_dve(v, src_off, cs_off, dst_off, n4):
            se = AR_F[:, src_off:src_off + 512:2]
            so = AR_F[:, src_off + 1:src_off + 512:2]
            de = AR_F[:, dst_off:dst_off + 512:2]
            do = AR_F[:, dst_off + 1:dst_off + 512:2]
            cc_ = FA(cs_off, 256)
            ss_ = FA(cs_off + 256, 256)
            t1 = FA(W3, 256)
            t2 = FA(W3 + 256, 256)
            E(lambda: v.tensor_mul(out=t1, in0=se, in1=cc_))
            E(lambda: v.tensor_mul(out=t2, in0=so, in1=ss_))
            E(lambda: v.tensor_sub(out=de, in0=t1, in1=t2))
            E(lambda: v.tensor_mul(out=t1, in0=se, in1=ss_))
            E(lambda: v.tensor_mul(out=t2, in0=so, in1=cc_))
            E(lambda: v.tensor_add(out=do, in0=t1, in1=t2))

        @block.gpsimd
        def _(g):
            W(g, "ag_in_done")
            E(lambda: g.collective_compute(
                "AllGather", ALU.bypass, replica_groups=[list(range(NCORE))],
                ins=[ag_in.ap().opt()], outs=[ag_out.opt()]), "cc1", 1)
            MS("ag_trig", "cc1")
            E(lambda: g.dma_start(out=BA(B_WKS, 8192).rearrange("p (c o) -> p c o", c=CH),
                                  in_=wk_sw), "d_sw", 16)
            E(lambda: g.dma_start(out=BA(B_WVS, 8192).rearrange("p (c o) -> p c o", c=CH),
                                  in_=wv_sw), "d_sw", 16)
            MS("kvw", "d_sw")
            for hg in range(4):
                if hg >= 1:
                    W(g, f"qpe_all_{hg-1}")
                else:
                    W(g, "p2kv_pe_done")
                E(lambda hg=hg: g.dma_start(
                    out=BA(B_WQS, 8192).rearrange("p (c o) -> p c o", c=CH),
                    in_=wq_sw[hg]), "d_sw", 16)
                MS(f"wq_{hg}", "d_sw")
            W(g, "attn_all")
            if upto == "p2" and DBG_P2:
                E(lambda: g.dma_start(out=dbg_heads.ap(), in_=BA(B_HEADS, 16384)), "d_out", 16)
                E(lambda: g.dma_start(out=dbg_qfm.ap(), in_=BA(B_QFM, 16384)), "d_out", 16)
                E(lambda: g.dma_start(out=dbg_kfm.ap(), in_=BA(B_KFM, 4608)), "d_out", 16)
                E(lambda: g.dma_start(out=dbg_vsb.ap(), in_=BA(B_VSB, 4608)), "d_out", 16)
                MS("dbgp2_done", "d_out")
                W(g, "dbgp2_done")
            E(lambda: g.dma_start(
                out=BA(B_WOL, 16384).rearrange("p (h n) -> p h n", h=8),
                in_=wo_sw[0:8].rearrange("h p a o -> p h (a o)")), "d_wo", 16)
            E(lambda: g.dma_start(
                out=BA(B_WOH, 16384).rearrange("p (h n) -> p h n", h=8),
                in_=wo_sw[8:16].rearrange("h p a o -> p h (a o)")), "d_wo", 16)
            MS("wo_ready", "d_wo")

        @block.sync
        def _(q):
            for i in range(NTH):
                par = i % 2
                if i >= 2:
                    W(q, f"xnT_{i-2}")
                    W(q, f"krope_{i-2}")
                E(lambda i=i, par=par: q.dma_start(
                    out=BA(B_XN2[par], D), in_=xn_dram.ap()[128 * i:128 * (i + 1), :]),
                  f"d_xr{par}", 16)
                MS(f"xr_{i}", f"d_xr{par}")
                E(lambda i=i, par=par: q.dma_start(
                    out=FA(CS[par], 256), in_=cos_t[128 * i:128 * (i + 1), :]),
                  f"d_cs{par}", 16)
                E(lambda i=i, par=par: q.dma_start(
                    out=FA(CS[par] + 256, 256), in_=sin_t[128 * i:128 * (i + 1), :]),
                  f"d_cs{par}", 16)
                MS(f"cs_{i}", f"d_cs{par}")


        pbfu = {0: None, 1: None}

        def pbf_tr(pe, idx, msname, f):
            if pbfu[idx]:
                W(pe, pbfu[idx])
            E(f, "pe", 1)
            MS(msname + "_p", "pe")

        @block.tensor
        def _(pe):
            for i in range(NTH):
                par = i % 2
                W(pe, f"xr_{i}")
                for c in range(CH):
                    pbf_tr(pe, c % 2, f"xtr_{i}_{c}",
                           lambda c=c, par=par: pe.transpose(
                               PBFS[c % 2][:, 0:128],
                               BA(B_XN2[par] + 128 * c, 128), V_identb))
                    pbfu[c % 2] = f"xtr_{i}_{c}_e"
                W(pe, "kvw")
                W(pe, f"xnT_{i}")
                if i >= 1:
                    W(pe, f"kc_{i-1}")
                    W(pe, f"vc_{i-1}")
                for c in range(CH):
                    E(lambda i=i, c=c: pe.matmul(
                        PB[0][:, :], BA(B_XNT + 1152 * c + 128 * i, 128),
                        BA(B_WKS + 512 * c, 512), start=(c == 0), stop=(c == CH - 1)),
                      "pe" if c == CH - 1 else None, 1)
                MS(f"kpe_{i}", "pe")
                for c in range(CH):
                    E(lambda i=i, c=c: pe.matmul(
                        PB[1][:, :], BA(B_XNT + 1152 * c + 128 * i, 128),
                        BA(B_WVS + 512 * c, 512), start=(c == 0), stop=(c == CH - 1)),
                      "pe" if c == CH - 1 else None, 1)
                MS(f"vpe_{i}", "pe")
                if i >= 1:
                    W(pe, f"krope_{i-1}")
                    for hv in range(4):
                        pbf_tr(pe, hv % 2, f"ktr_{i-1}_{hv}",
                               lambda hv=hv, p2=(i - 1) % 2: pe.transpose(
                                   PBFS[hv % 2][:, 0:128],
                                   BA(B_KB[p2] + 128 * hv, 128), V_identb))
                        pbfu[hv % 2] = f"ktr_{i-1}_{hv}_e"
            W(pe, f"krope_{NTH-1}")
            for hv in range(4):
                pbf_tr(pe, hv % 2, f"ktr_{NTH-1}_{hv}",
                       lambda hv=hv, p2=(NTH - 1) % 2: pe.transpose(
                           PBFS[hv % 2][:, 0:128],
                           BA(B_KB[p2] + 128 * hv, 128), V_identb))
                pbfu[hv % 2] = f"ktr_{NTH-1}_{hv}_e"
            MS("p2kv_pe_done", "pe")

        @block.vector
        def _(v):
            for i in range(NTH):
                par = i % 2
                W(v, f"kpe_{i}")
                E(lambda: v.tensor_add(out=FA(W1, 512), in0=PB[0][:, :], in1=V_bbk), "dve", 1)
                MS(f"kc_{i}", "dve")
                W(v, f"ksd_{i}")
                if i >= 2:
                    W(v, f"ktr_{i-2}")
                if i >= 1:
                    W(v, f"vsb_{i-1}")
                E(lambda: v.reciprocal(out=stat(ST_R4, 4), in_=stat(ST_SD4, 4)))
                E(lambda: v.drain())
                for hv in range(4):
                    E(lambda hv=hv: v.tensor_scalar_mul(
                        out=FA(W2 + 128 * hv, 128), in0=FA(W1 + 128 * hv, 128),
                        scalar1=stat(ST_R4 + hv)))
                E(lambda: v.tensor_mul(out=FA(W2, 512), in0=FA(W2, 512), in1=V_kkn))
                W(v, f"cs_{i}")
                rope_dve(v, W2, CS[par], W1, 4)
                E(lambda par=par: v.tensor_copy(out=BA(B_KB[par], 512), in_=FA(W1, 512)), "dve", 1)
                MS(f"krope_{i}", "dve")
                W(v, f"vpe_{i}")
                E(lambda: v.tensor_add(out=FA(W2, 512), in0=PB[1][:, :], in1=V_bbv), "dve", 1)
                MS(f"vc_{i}", "dve")

        @block.scalar
        def _(sc):
            for i in range(NTH):
                for c in range(CH):
                    W(sc, f"xtr_{i}_{c}_p")
                    E(lambda i=i, c=c: sc.copy(
                        out=BA(B_XNT + 1152 * c + 128 * i, 128),
                        in_=PBFS[c % 2][:, 0:128]), "act", 1)
                    MS(f"xtr_{i}_{c}_e", "act")
                MS(f"xnT_{i}", "act")
                W(sc, f"kc_{i}")
                for hv in range(4):
                    E(lambda hv=hv: sc.activation(
                        out=FA(W3, 128), in_=FA(W1 + 128 * hv, 128), func=AF.Square,
                        bias=stat(8), scale=1.0, accum_out=stat(ST_KS + hv)))
                E(lambda: sc.drain())
                E(lambda: sc.activation(out=stat(ST_SD4, 4), in_=stat(ST_KS, 4),
                                        func=AF.Sqrt, bias=stat(11), scale=1.0 / 128))
                E(lambda: sc.drain(), "act", 1)
                MS(f"ksd_{i}", "act")
                W(sc, f"vc_{i}")
                E(lambda i=i: sc.copy(out=BA(B_VSB + 512 * i, 512), in_=FA(W2, 512)), "act", 1)
                MS(f"vsb_{i}", "act")
                if i >= 1:
                    for hv in range(4):
                        W(sc, f"ktr_{i-1}_{hv}_p")
                        E(lambda i=i, hv=hv: sc.copy(
                            out=BA(B_KFM + 1152 * hv + 128 * (i - 1), 128),
                            in_=PBFS[hv % 2][:, 0:128]), "act", 1)
                        MS(f"ktr_{i-1}_{hv}_e", "act")
                    MS(f"ktr_{i-1}", "act")
            for hv in range(4):
                W(sc, f"ktr_{NTH-1}_{hv}_p")
                E(lambda hv=hv: sc.copy(
                    out=BA(B_KFM + 1152 * hv + 128 * (NTH - 1), 128),
                    in_=PBFS[hv % 2][:, 0:128]), "act", 1)
                MS(f"ktr_{NTH-1}_{hv}_e", "act")
            MS(f"ktr_{NTH-1}", "act")

        @block.sync
        def _(q):
            for hg in range(4):
                for ti, i in enumerate(range(1, NTH)):
                    k = hg * 8 + ti
                    par = k % 2
                    if k >= 2:
                        W(q, f"qrope_{k-2}")
                    else:
                        W(q, f"krope_{NTH-1}")
                    E(lambda i=i, par=par: q.dma_start(
                        out=FA(CS[par], 256), in_=cos_t[128 * i:128 * (i + 1), :]),
                      f"d_cs{par}", 16)
                    E(lambda i=i, par=par: q.dma_start(
                        out=FA(CS[par] + 256, 256), in_=sin_t[128 * i:128 * (i + 1), :]),
                      f"d_cs{par}", 16)
                    MS(f"qcs_{k}", f"d_cs{par}")

        @block.tensor
        def _(pe):
            for hg in range(4):
                W(pe, f"wq_{hg}")
                for ti, i in enumerate(range(1, NTH)):
                    k = hg * 8 + ti
                    if k >= 1:
                        W(pe, f"qc_{k-1}")
                    for c in range(CH):
                        E(lambda i=i, c=c: pe.matmul(
                            PB[3][:, :], BA(B_XNT + 1152 * c + 128 * i, 128),
                            BA(B_WQS + 512 * c, 512), start=(c == 0), stop=(c == CH - 1)),
                          "pe" if c == CH - 1 else None, 1)
                    MS(f"qpe_{k}", "pe")
                    if k >= 1:
                        W(pe, f"qrope_{k-1}")
                        for hh in range(4):
                            pbf_tr(pe, hh % 2, f"qtr_{k-1}_{hh}",
                                   lambda hh=hh, p2=(k - 1) % 2: pe.transpose(
                                       PBFS[hh % 2][:, 0:128],
                                       BA(B_QB[p2] + 128 * hh, 128), V_identb))
                            pbfu[hh % 2] = f"qtr_{k-1}_{hh}_e"
                MS(f"qpe_all_{hg}", "pe")
            W(pe, "qrope_31")
            for hh in range(4):
                pbf_tr(pe, hh % 2, f"qtr_31_{hh}",
                       lambda hh=hh: pe.transpose(
                           PBFS[hh % 2][:, 0:128],
                           BA(B_QB[31 % 2] + 128 * hh, 128), V_identb))
                pbfu[hh % 2] = f"qtr_31_{hh}_e"

        @block.vector
        def _(v):
            for hg in range(4):
                for ti, i in enumerate(range(1, NTH)):
                    k = hg * 8 + ti
                    par = k % 2
                    W(v, f"qpe_{k}")
                    E(lambda hg=hg: v.tensor_add(out=FA(W0, 512), in0=PB[3][:, :],
                                                 in1=V_bbq[:, 512 * hg:512 * (hg + 1)]), "dve", 1)
                    MS(f"qc_{k}", "dve")
                    W(v, f"qsd_{k}")
                    if k >= 2:
                        W(v, f"qtr_{k-2}")
                    E(lambda: v.reciprocal(out=stat(ST_R4, 4), in_=stat(ST_SD4, 4)))
                    E(lambda: v.drain())
                    for hh in range(4):
                        E(lambda hh=hh: v.tensor_scalar_mul(
                            out=FA(W2 + 128 * hh, 128), in0=FA(W0 + 128 * hh, 128),
                            scalar1=stat(ST_R4 + hh)))
                    E(lambda: v.tensor_mul(out=FA(W2, 512), in0=FA(W2, 512), in1=V_qqn))
                    W(v, f"qcs_{k}")
                    rope_dve(v, W2, CS[par], W0, 4)
                    E(lambda: v.tensor_scalar_mul(out=FA(W0, 512), in0=FA(W0, 512),
                                                  scalar1=V_scale))
                    E(lambda par=par: v.tensor_copy(out=BA(B_QB[par], 512), in_=FA(W0, 512)), "dve", 1)
                    MS(f"qrope_{k}", "dve")

        @block.scalar
        def _(sc):
            for hg in range(4):
                for ti in range(8):
                    k = hg * 8 + ti
                    W(sc, f"qc_{k}")
                    for hh in range(4):
                        E(lambda hh=hh: sc.activation(
                            out=FA(W3, 128), in_=FA(W0 + 128 * hh, 128), func=AF.Square,
                            bias=stat(8), scale=1.0, accum_out=stat(ST_KS + hh)))
                    E(lambda: sc.drain())
                    E(lambda: sc.activation(out=stat(ST_SD4, 4), in_=stat(ST_KS, 4),
                                            func=AF.Sqrt, bias=stat(11), scale=1.0 / 128))
                    E(lambda: sc.drain(), "act", 1)
                    MS(f"qsd_{k}", "act")
                    if k >= 1:
                        kk = k - 1
                        hgp, tip = kk // 8, kk % 8
                        for hh in range(4):
                            W(sc, f"qtr_{kk}_{hh}_p")
                            E(lambda hgp=hgp, hh=hh, tip=tip: sc.copy(
                                out=BA(B_QFM + 1024 * (4 * hgp + hh) + 128 * tip, 128),
                                in_=PBFS[hh % 2][:, 0:128]), "act", 1)
                            MS(f"qtr_{kk}_{hh}_e", "act")
                        MS(f"qtr_{kk}", "act")
            kk = 31
            for hh in range(4):
                W(sc, f"qtr_{kk}_{hh}_p")
                E(lambda hh=hh: sc.copy(
                    out=BA(B_QFM + 1024 * (4 * 3 + hh) + 128 * 7, 128),
                    in_=PBFS[hh % 2][:, 0:128]), "act", 1)
                MS(f"qtr_{kk}_{hh}_e", "act")
            MS(f"qtr_{kk}", "act")

        @block.tensor
        def _(pe):
            def emit_tail(j):
                parj = j % 2
                qtj = j % 8
                hvj = (j // 8) // 4
                W(pe, f"pn_{j}")
                if j >= 2:
                    W(pe, f"pt_{j-2}")
                E(lambda parj=parj: pe.transpose(PBFS[parj][:, 0:128],
                                                 BA(B_PN, 128), V_identb))
                E(lambda parj=parj: pe.transpose(PBFS[parj][:, 128:256],
                                                 BA(B_PN + 128, 128), V_identb),
                  "pe", 1)
                MS(f"ptr_{j}", "pe")
                W(pe, f"pt_{j}")
                if j >= 2:
                    W(pe, f"hv_{j-2}")
                E(lambda qtj=qtj, hvj=hvj: pe.matmul(
                    PB[5 + (j % 2)][:, 0:128], BA(B_VSB + 512 * qtj + 128 * hvj, 128),
                    BA(B_PT, 128), start=True, stop=False))
                E(lambda qtj=qtj, hvj=hvj: pe.matmul(
                    PB[5 + (j % 2)][:, 0:128],
                    BA(B_VSB + 512 * (qtj + 1) + 128 * hvj, 128),
                    BA(B_PT + 128, 128), start=False, stop=True), "pe", 1)
                MS(f"av_{j}", "pe")

            first = True
            for h in range(16):
                hv = h // 4
                for qt in range(8):
                    k = h * 8 + qt
                    if first:
                        W(pe, "qtr_31")
                        W(pe, "qtr_30")
                        W(pe, "ktr_8")
                        W(pe, "ktr_7")
                        W(pe, "vsb_8")
                        first = False
                    par = k % 2
                    if k >= 2:
                        W(pe, f"sel_{k-2}")
                    E(lambda h=h, qt=qt, hv=hv, par=par: pe.matmul(
                        PB[3 + par][:, 0:256], BA(B_QFM + 1024 * h + 128 * qt, 128),
                        BA(B_KFM + 1152 * hv + 128 * qt, 256),
                        start=True, stop=True), "pe", 1)
                    MS(f"s_{k}", "pe")
                    if k >= 1:
                        emit_tail(k - 1)
            emit_tail(127)

        @block.vector
        def _(v):
            for h in range(16):
                for qt in range(8):
                    k = h * 8 + qt
                    par = k % 2
                    W(v, f"s_{k}")
                    E(lambda par=par: v.tensor_copy(out=FA(W2, 256), in_=PB[3 + par][:, 0:256]))
                    moff = 0 if qt == 0 else 256
                    E(lambda moff=moff: v.copy_predicated(
                        out=FA(W2, 256), mask=UA(UO["mask_swa"] + moff, 256),
                        data=V_fillswa))
                    E(lambda: v.reduce_max(out=stat(ST_NEGM), in_=FA(W2, 256),
                                           axis=AX.X, negate=True))
                    E(lambda: v.drain(), "dve", 1)
                    MS(f"sel_{k}", "dve")
                    W(v, f"pf_{k}")
                    E(lambda: v.reciprocal(out=stat(ST_RZ), in_=stat(ST_Z)))
                    E(lambda: v.drain())
                    E(lambda: v.tensor_scalar_mul(out=BA(B_PN, 256), in0=FA(W3, 256),
                                                  scalar1=stat(ST_RZ)), "dve", 1)
                    MS(f"pn_{k}", "dve")
                    W(v, f"ptr_{k}")
                    if k >= 1:
                        W(v, f"av_{k-1}")
                    E(lambda par=par: v.tensor_copy(out=BA(B_PT, 256), in_=PBFS[par][:, 0:256]), "dve", 1)
                    MS(f"pt_{k}", "dve")

        @block.scalar
        def _(sc):
            for h in range(16):
                for qt in range(8):
                    k = h * 8 + qt
                    par = k % 2
                    W(sc, f"sel_{k}")
                    E(lambda: sc.activation(out=FA(W3, 256), in_=FA(W2, 256), func=AF.Exp,
                                            bias=stat(ST_NEGM), scale=1.0,
                                            accum_out=stat(ST_Z)))
                    E(lambda: sc.drain(), "act", 1)
                    MS(f"pf_{k}", "act")
                    W(sc, f"av_{k}")
                    E(lambda h=h, qt=qt, par=par: sc.copy(
                        out=BA(B_HEADS + 1024 * h + 128 * qt, 128),
                        in_=PB[5 + par][:, 0:128]), "act", 1)
                    MS(f"hv_{k}", "act")
            MS("attn_all", "act")

        @block.sync
        def _(q):
            for tt in range(8):
                par = tt % 2
                if tt >= 2:
                    W(q, f"yadd_{tt-2}_3")
                E(lambda tt=tt: q.dma_start(
                    out=FA(W4, D), in_=y_part.ap()[128 * tt:128 * (tt + 1), :]),
                  f"d_yb{par}", 16)
                MS(f"ywb_{tt}", f"d_yb{par}")
                for oc in range(4):
                    W(q, f"yadd_{tt}_{oc}")
                E(lambda tt=tt: q.dma_start(
                    out=y_part.ap()[128 * tt:128 * (tt + 1), :], in_=FA(W4, D)), "d_yw", 16)
                if upto == "p2":
                    E(lambda tt=tt: q.dma_start(
                        out=y_out[128 * tt:128 * (tt + 1), :], in_=FA(W4, D)), "d_yw", 16)
                MS(f"yso_{tt}", "d_yw")

        @block.tensor
        def _(pe):
            W(pe, "wo_ready")
            W(pe, "attn_all")
            for tt in range(8):
                for oc in range(4):
                    kk = tt * 4 + oc
                    if kk >= 2:
                        W(pe, f"yadd_{(kk-2)//4}_{(kk-2)%4}")
                    for h in range(16):
                        base = B_WOL if h < 8 else B_WOH
                        E(lambda tt=tt, oc=oc, h=h, base=base, kk=kk: pe.matmul(
                            PB[0 + kk % 2][:, :],
                            BA(B_HEADS + 1024 * h + 128 * tt, 128),
                            BA(base + 2048 * (h % 8) + 512 * oc, 512),
                            start=(h == 0), stop=(h == 15)),
                          "pe" if h == 15 else None, 1)
                    MS(f"wop_{tt}_{oc}", "pe")

        @block.vector
        def _(v):
            for tt in range(8):
                W(v, f"ywb_{tt}")
                for oc in range(4):
                    kk = tt * 4 + oc
                    W(v, f"wop_{tt}_{oc}")
                    E(lambda oc=oc, kk=kk: v.tensor_add(
                        out=FA(W4 + 512 * oc, 512), in0=PB[0 + kk % 2][:, :],
                        in1=FA(W4 + 512 * oc, 512)))
                    E(lambda: v.drain(), "dve", 1)
                    MS(f"yadd_{tt}_{oc}", "dve")

        if upto == "p2" and DBG_P2:
            @block.sync
            def _(q):
                W(q, f"ktr_{NTH-1}")
                E(lambda: q.dma_start(out=dbg_kfm_a.ap(), in_=BA(B_KFM, 4608)), "d_c", 16)
                W(q, "qtr_31")
                E(lambda: q.dma_start(out=dbg_kfm_b.ap(), in_=BA(B_KFM, 4608)), "d_c", 16)

        for name in ENGS:
            dec = dict(pe=block.tensor, dve=block.vector, act=block.scalar,
                       pool=block.gpsimd, sync=block.sync)[name]
            def mk(name):
                return lambda e: env.bar(e, name, "p2")
            dec(mk(name))

    env_dry = Env(True)
    emit(env_dry, FakeBlock())
    env = Env(False, sems=sems, milestones=env_dry.ms)
    with nc.Block() as block:
        emit(env, block)
    assert env.ctr == env_dry.ctr
    ctx.close()
    return nc, d_in


def host_inputs(inp):
    f32 = np.float32
    x = np.asarray(inp["L0_attn_x"]).reshape(8192, D)
    cores = []
    t_glob = np.arange(-128, 8192 + 0, dtype=np.int64)
    inv = (1.0 / 10000.0 ** (np.arange(0, 128, 2, dtype=f32) / 128.0)).astype(f32)

    identf = np.eye(128, dtype=f32)
    identb = np.eye(128, dtype=ml_dtypes.bfloat16)
    onesr = np.ones((1, 128), dtype=f32)
    ln1w = np.ascontiguousarray(np.asarray(inp["L0_ffn_ln1_w"], f32).reshape(CH, 128).T)
    ln2w = np.ascontiguousarray(np.asarray(inp["L0_ffn_ln2_w"], f32).reshape(CH, 128).T)
    scale_bc = np.broadcast_to(np.asarray(inp["L0_attn_scale"], f32).reshape(1, 1), (128, 1)).copy()
    b_bq = np.broadcast_to(np.asarray(inp["L0_attn_bq"], f32)[None, :], (128, D)).copy()
    b_bk = np.broadcast_to(np.asarray(inp["L0_attn_bk"], f32)[None, :], (128, 512)).copy()
    b_bv = np.broadcast_to(np.asarray(inp["L0_attn_bv"], f32)[None, :], (128, 512)).copy()
    b_bo = np.broadcast_to(np.asarray(inp["L0_attn_bo"], f32)[None, :], (128, D)).copy()
    qqn = np.broadcast_to(np.tile(np.asarray(inp["L0_attn_qn_w"], f32), 4)[None, :], (128, 512)).copy()
    kkn = np.broadcast_to(np.tile(np.asarray(inp["L0_attn_kn_w"], f32), 4)[None, :], (128, 512)).copy()

    ii = np.arange(128)[:, None]
    jj = np.arange(256)[None, :]
    m_std = ~((jj >= ii + 64) & (jj <= ii + 128))
    m_first0 = ~((jj >= np.maximum(ii + 64, 128)) & (jj <= ii + 128))
    m_p1 = np.zeros((4, 128, 512), np.uint8)
    m_p2 = np.zeros((4, 128, 512), np.uint8)
    qq = np.arange(128)[:, None]
    kk = np.arange(512)[None, :]
    for pat in range(4):
        m_p1[pat] = (kk > qq + 128 * pat).astype(np.uint8)
    kk2 = np.arange(128)[:, None]
    qq2 = np.arange(512)[None, :]
    for pat in range(4):
        m_p2[pat] = (128 * pat + kk2 > qq2).astype(np.uint8)
    m_p1 = m_p1.reshape(4 * 128, 512)
    m_p2 = m_p2.reshape(4 * 128, 512)

    Wq_a = np.asarray(inp["L0_attn_Wq"], f32)
    Wk_a = np.asarray(inp["L0_attn_Wk"], f32)
    Wv_a = np.asarray(inp["L0_attn_Wv"], f32)
    Wo_a = np.asarray(inp["L0_attn_Wo"], f32)
    wq_sw = np.ascontiguousarray(
        Wq_a.reshape(4, 512, CH, 128).transpose(0, 3, 2, 1))
    wk_sw = np.ascontiguousarray(Wk_a.reshape(512, CH, 128).transpose(2, 1, 0))
    wv_sw = np.ascontiguousarray(Wv_a.reshape(512, CH, 128).transpose(2, 1, 0))
    wo_sw = np.ascontiguousarray(
        Wo_a.reshape(4, 512, 16, 128).transpose(2, 3, 0, 1))

    Wq_f = np.asarray(inp["L0_ffn_Wq"], f32)
    Wk_f = np.asarray(inp["L0_ffn_Wk"], f32)
    Wv_f = np.asarray(inp["L0_ffn_Wv"], f32)
    Wo_f = np.asarray(inp["L0_ffn_Wo"], f32)
    A_m = np.asarray(inp["L0_ffn_A"], f32)
    B_m = np.asarray(inp["L0_ffn_B"], f32)
    W_in = np.asarray(inp["L0_ffn_W_in_base"], f32)
    W_up = np.asarray(inp["L0_ffn_W_up_base"], f32)
    W_dn = np.asarray(inp["L0_ffn_W_down"], f32)

    a_t = np.ascontiguousarray(A_m.reshape(CH, 128, 16).transpose(1, 0, 2))
    b_t = np.ascontiguousarray(B_m)
    w_in_t = np.ascontiguousarray(
        W_in.reshape(NFC, 128, CH, 128).transpose(0, 3, 2, 1))
    w_up_t = np.ascontiguousarray(
        W_up.reshape(NFC, 128, CH, 128).transpose(0, 3, 2, 1))
    w_dn_t = np.ascontiguousarray(
        W_dn.reshape(4, 512, NFC, 128).transpose(0, 3, 2, 1))
    shared = dict(
        m_p1=m_p1, m_p2=m_p2, identf=identf, identb=identb, onesr=onesr,
        ln1w=ln1w, ln2w=ln2w, scale_bc=scale_bc, b_bq=b_bq, b_bk=b_bk,
        b_bv=b_bv, b_bo=b_bo, qqn=qqn, kkn=kkn,
        wq_sw=wq_sw, wk_sw=wk_sw, wv_sw=wv_sw, wo_sw=wo_sw,
        a_t=a_t, b_t=b_t, w_in_t=w_in_t, w_up_t=w_up_t, w_dn_t=w_dn_t,
    )

    for c in range(NCORE):
        m = dict(shared)
        lo = 1024 * c - 128
        if c == 0:
            xs = np.concatenate([np.zeros((128, D), f32), x[0:1024]], axis=0)
        else:
            xs = x[lo:lo + NTH * 128]
        m["x_sl"] = np.ascontiguousarray(xs)
        pos = np.arange(lo, lo + NTH * 128, dtype=f32)
        fr = pos[:, None] * inv[None, :]
        m["cos_t"] = np.ascontiguousarray(np.tile(np.cos(fr), (1, 4)).astype(f32))
        m["sin_t"] = np.ascontiguousarray(np.tile(np.sin(fr), (1, 4)).astype(f32))
        msk0 = m_first0 if c == 0 else m_std
        m["m_swa"] = np.concatenate([msk0, m_std], axis=0).astype(np.uint8)
        sl = slice(256 * c, 256 * (c + 1))
        m["wq_ff"] = np.ascontiguousarray(Wq_f[sl].reshape(256, CH, 128).transpose(2, 1, 0))
        m["wk_ff"] = np.ascontiguousarray(Wk_f[sl].reshape(256, CH, 128).transpose(2, 1, 0))
        m["wv_ff"] = np.ascontiguousarray(Wv_f[sl].reshape(256, CH, 128).transpose(2, 1, 0))
        m["wo_ff"] = np.ascontiguousarray(Wo_f[:, sl].reshape(CH, 128, 256).transpose(1, 0, 2))
        cores.append(m)
    return cores


def kernel(**inp):
    cores = host_inputs(inp)
    nc, _ = build_nc(upto="full")
    res = run_bass_kernel_spmd(nc, cores, core_ids=list(range(NCORE)))
    y = np.concatenate([res.results[c]["y"] for c in range(NCORE)], axis=0)
    return y.reshape(1, 8192, D)
